# version 04
# baseline (speedup 1.0000x reference)
import math
import numpy as np
import ml_dtypes
import concourse.bass as bass
import concourse.mybir as mybir
from concourse.bass_utils import run_bass_kernel_spmd

F32 = mybir.dt.float32
BF16 = mybir.dt.bfloat16
AF = mybir.ActivationFunctionType
ALU = mybir.AluOpType
AX = mybir.AxisListType
bf = ml_dtypes.bfloat16

D = 1024
NH = 16
HDIM = 64
HID = 2816
NE = 9
LATD = 256
NLH = 4
NLAT = 64
NEG = -1.0e9
EPS_LN = 1e-5


class _Slot:
    __slots__ = ("h", "c")

    def __init__(self, h):
        self.h = h
        self.c = 0


class Sems:

    def __init__(self, prog, tag, names, sw=()):
        self.slots = {}
        self.base = {}
        gi = 0
        si = 0
        for n in names:
            if n in sw:
                slot = prog.sw_slot(si)
                si += 1
            else:
                slot = prog.gen_slot(gi)
                gi += 1
            self.slots[n] = slot
            self.base[n] = slot.c

    @property
    def h(self):
        return {n: s.h for n, s in self.slots.items()}

    @property
    def c(self):
        return {n: sl.c - self.base[n] for n, sl in self.slots.items()}

    def inc(self, instr, name, n=1):
        slot = self.slots[name]
        instr.then_inc(slot.h, n)
        slot.c += n
        return slot.c - self.base[name]

    def wait(self, eng, name, v):
        if v > 0:
            eng.wait_ge(self.slots[name].h, self.base[name] + v)

    def finals(self):
        return {sl.h: sl.c for n, sl in self.slots.items() if sl.c > self.base[n]}

    def edge(self, eng, name, instr):
        v = self.inc(instr, name, 1)
        self.wait(eng, name, v)
        return v


def wait_finals(eng, finals):
    for h, v in finals.items():
        eng.wait_ge(h, v)


class Ring:

    def __init__(self, prog, tag, n, base_idx=8):
        self.ss = [prog.gen_slot(base_idx + j) for j in range(n)]
        self.n = n
        self.target = {}

    def inc(self, instr, item, amt=16):
        s = self.ss[item % self.n]
        instr.then_inc(s.h, amt)
        s.c += amt
        self.target[item] = s.c

    def wait_done(self, eng, item):
        if item < 0 or item not in self.target:
            return
        eng.wait_ge(self.ss[item % self.n].h, self.target[item])

    def finals(self):
        return {s.h: s.c for s in self.ss if s.c > 0}


def rr8(ap, c=8):
    return ap.rearrange("(c p) n -> p c n", p=128)


class Prog:
    def __init__(self, T, debug_outs=()):
        assert T % 512 == 0
        self.T = T
        self.nt = T // 128
        self.ng = T // 512
        self.CH = T // NLAT
        self.nc = bass.Bass("TRN2", target_bir_lowering=False, debug=False)
        self.prev = {}
        self._gen_slots = {}
        self._sw_slots = {}
        self.debug_outs = set(debug_outs)
        self.dbg = {}

    def gen_slot(self, i):
        if i not in self._gen_slots:
            self._gen_slots[i] = _Slot(self.nc.alloc_semaphore(f"g{i}"))
        return self._gen_slots[i]

    def sw_slot(self, i):
        if i not in self._sw_slots:
            self._sw_slots[i] = _Slot(self.nc.alloc_semaphore(f"sw{i}"))
        return self._sw_slots[i]

    @property
    def wslots(self):
        if not hasattr(self, "_wslots"):
            self._wslots = [_Slot(self.nc.alloc_semaphore("wsl0")),
                            _Slot(self.nc.alloc_semaphore("wsl1"))]
            self.wtarget = {}
        return self._wslots

    def make_dram(self):
        nc, T = self.nc, self.T
        din = lambda n, s: nc.dram_tensor(n, s, F32, kind="ExternalInput").ap()
        dbi = lambda n, s: nc.dram_tensor(n, s, BF16, kind="ExternalInput").ap()
        scr = lambda n, s, dt=F32: nc.dram_tensor(n, s, dt, kind="Internal").ap()
        d = {}
        d["x"] = din("x", [T, D])
        d["wqkv_a"] = din("wqkv_a", [D, 3 * D])
        d["wout_a"] = din("wout_a", [D, D])
        d["head_scales"] = din("head_scales", [NH, 1])
        d["hdblk"] = din("hdblk", [D, NH])
        d["wqkv_b"] = din("wqkv_b", [D, 3 * D])
        d["bqkv_b"] = din("bqkv_b", [1, 3 * D])
        d["wrq"] = din("wrq", [D, LATD])
        d["brq"] = din("brq", [1, LATD])
        d["wrkv"] = din("wrkv", [D, 2 * LATD])
        d["brkv"] = din("brkv", [1, 2 * LATD])
        d["wroutT"] = din("wroutT", [D, LATD])
        d["wrout"] = din("wrout", [LATD, D])
        d["brout"] = din("brout", [1, D])
        d["wout_b"] = din("wout_b", [D, D])
        d["bout_b"] = din("bout_b", [1, D])
        d["glblk"] = din("glblk", [D, NH])
        d["grblk"] = din("grblk", [D, NH])
        d["w1T"] = din("w1T", [NE, D, HID])
        d["w3T"] = din("w3T", [NE, D, HID])
        d["w2T"] = din("w2T", [NE, HID, D])
        d["gateT"] = din("gateT", [D, NE])
        d["gbias"] = din("gbias", [1, NE])
        d["id_bf"] = dbi("id_bf", [128, 128])
        d["id_f32"] = din("id_f32", [128, 128])
        d["ones_bf"] = dbi("ones_bf", [1, 512])
        d["onesT"] = dbi("onesT", [1, self.T])
        d["ones_f32"] = din("ones_f32", [1, 64])
        d["maskA"] = dbi("maskA", [4 * 128, 512])
        d["maskBT"] = dbi("maskBT", [128, 128])
        d["maskL"] = dbi("maskL", [128, 256])
        d["maskLT1"] = dbi("maskLT1", [128, 128])
        d["maskLT2"] = dbi("maskLT2", [128, 128])
        d["latC"] = din("latC", [128, NLAT])
        d["latCT"] = din("latCT", [NLAT, 128])
        d["ind"] = dbi("ind", [self.nt * 128, NLAT])
        d["Ra"] = scr("Ra", [T, D])
        d["Rb"] = scr("Rb", [T, D])
        d["Qh"] = scr("Qh", [D, T], BF16)
        d["Kh"] = scr("Kh", [D, T], BF16)
        d["Vh"] = scr("Vh", [T, D], BF16)
        d["qp"] = scr("qp", [NH, T], BF16)
        d["kp"] = scr("kp", [NH, T], BF16)
        d["stats"] = scr("stats", [self.nt, 128])
        d["xfm"] = scr("xfm", [D, T], BF16)
        d["w1b"] = scr("w1b", [NE, D, HID], BF16)
        d["w3b"] = scr("w3b", [NE, D, HID], BF16)
        d["w2b"] = scr("w2b", [NE, HID, D], BF16)
        d["g1"] = scr("g1", [NH, T])
        d["out"] = nc.dram_tensor("out", [T, D], F32, kind="ExternalOutput").ap()
        self.d = d
        for name in self.debug_outs:
            shp, dt = {
                "d_xlnT": ([D, T], BF16), "d_Q": ([D, T], BF16), "d_K": ([D, T], BF16),
                "d_V": ([T, D], BF16), "d_qp": ([NH, T], BF16), "d_kp": ([NH, T], BF16),
                "d_attnout": ([D, T], BF16), "d_m": ([NH * self.nt, 128], F32),
                "d_cw": ([128, self.nt * NE], F32),
                "d_latout": ([LATD, T], BF16), "d_local": ([D, T], BF16),
                "d_pool": ([NLAT, D], BF16), "d_g1": ([NH, T], F32),
                "d_Ra": ([T, D], F32), "d_Rb": ([T, D], F32),
            }[name]
            self.dbg[name] = nc.dram_tensor(name, shp, dt, kind="ExternalOutput").ap()

    def make_persist(self):
        nc, T, nt = self.nc, self.T, self.nt
        import contextlib
        self.top = contextlib.ExitStack()
        sb = lambda n, s, dt: self.top.enter_context(nc.sbuf_tensor(n, s, dt))
        self.s_id_bf = sb("c_id_bf", [128, 128], BF16)
        self.s_id_f32 = sb("c_id_f32", [128, 128], F32)
        self.s_ones_bf = sb("c_ones_bf", [1, 512], BF16)
        self.s_ones_f32 = sb("c_ones_f32", [1, 64], F32)
        self.s_maskA = sb("c_maskA", [128, 4 * 512], BF16)
        self.s_maskBT = sb("c_maskBT", [128, 128], BF16)
        self.s_maskL = sb("c_maskL", [128, 256], BF16)
        self.s_maskLT1 = sb("c_maskLT1", [128, 128], BF16)
        self.s_maskLT2 = sb("c_maskLT2", [128, 128], BF16)
        self.s_latC = sb("c_latC", [128, NLAT], F32)
        self.s_latCT = sb("c_latCT", [NLAT, 128], F32)
        self.s_ind = sb("c_ind", [128, nt * NLAT], BF16)
        self.s_hd = sb("c_hd", [128, 8 * NH], BF16)
        self.s_gl = sb("c_gl", [128, 8 * NH], BF16)
        self.s_gr = sb("c_gr", [128, 8 * NH], BF16)
        self.s_hs8 = sb("c_hs8", [NH, 1], F32)
        self.s_cw = sb("c_cw", [128, nt * NE], F32)
        self.s_stat = sb("c_stat", [128, 4 * nt], F32)
        self.s_m = sb("c_m", [128, nt], F32)
        self.s_mneg = sb("c_mneg", [128, nt], F32)
        self.s_acc = sb("c_acc", [128, 4 * D], F32)
        self.s_row = sb("c_row", [64, 128], F32)
        self.s_msk = sb("c_msk", [128, 512], F32)
        self.s_tmp1 = sb("c_tmp1", [128, 2], F32)
        self.s_linv = sb("c_linv", [1, 512], F32)
        self.s_eps = sb("c_eps", [128, 1], F32)
        self.s_lexp = sb("c_lexp", [64, 512], F32)
        self.pb = [self.top.enter_context(nc.psum_tensor(f"pb{i}", [128, 512], F32))
                   for i in range(8)]

    def setup_phase(self):
        nc, d = self.nc, self.d
        S = Sems(self, "set", ["qp", "qs"], sw=("qp",))
        wslot = self.sw_slot(1)
        with nc.Block("setup") as blk:
            @blk.gpsimd
            def _(g):
                sbr = lambda t, c: t.ap().rearrange("p (c n) -> p c n", c=c)
                S.inc(g.dma_start(out=sbr(self.s_hd, 8), in_=rr8(d["hdblk"])), "qp", 16)
                S.inc(g.dma_start(out=sbr(self.s_gl, 8), in_=rr8(d["glblk"])), "qp", 16)
                S.inc(g.dma_start(out=sbr(self.s_gr, 8), in_=rr8(d["grblk"])), "qp", 16)
                S.inc(g.dma_start(out=self.s_hs8[:, :], in_=d["head_scales"]), "qp", 16)
                for e in range(NE):
                    for nm in ("w1", "w3", "w2"):
                        ins = g.dma_start(out=d[nm + "b"][e], in_=d[nm + "T"][e])
                        ins.then_inc(wslot.h, 16)
                        wslot.c += 16
                self.wcast = (wslot.h, wslot.c)

            @blk.sync
            def _(s):
                S.inc(s.dma_start(out=self.s_id_bf[:, :], in_=d["id_bf"]), "qs", 16)
                S.inc(s.dma_start(out=self.s_id_f32[:, :], in_=d["id_f32"]), "qs", 16)
                S.inc(s.dma_start(out=self.s_ones_bf[:, :], in_=d["ones_bf"]), "qs", 16)
                S.inc(s.dma_start(out=self.s_ones_f32[:, :], in_=d["ones_f32"]), "qs", 16)
                S.inc(s.dma_start(out=self.s_maskA.ap().rearrange("p (c n) -> p c n", c=4),
                                  in_=rr8(d["maskA"], 4)), "qs", 16)
                S.inc(s.dma_start(out=self.s_maskBT[:, :], in_=d["maskBT"]), "qs", 16)
                S.inc(s.dma_start(out=self.s_maskL[:, :], in_=d["maskL"]), "qs", 16)
                S.inc(s.dma_start(out=self.s_maskLT1[:, :], in_=d["maskLT1"]), "qs", 16)
                S.inc(s.dma_start(out=self.s_maskLT2[:, :], in_=d["maskLT2"]), "qs", 16)
                S.inc(s.dma_start(out=self.s_latC[:, :], in_=d["latC"]), "qs", 16)
                S.inc(s.dma_start(out=self.s_latCT[:, :], in_=d["latCT"]), "qs", 16)
                S.inc(s.dma_start(out=self.s_ind.ap().rearrange("p (c n) -> p c n", c=self.nt),
                                  in_=rr8(d["ind"], self.nt)), "qs", 16)

            @blk.vector
            def _(v):
                v.memset(self.s_eps[:, :], EPS_LN)
                S.wait(v, "qp", S.c["qp"])
                S.inc(v.tensor_scalar(self.s_hs8[:, :], self.s_hs8[:, :], 8.0, None,
                                      ALU.mult), "qs", 1)
        self.prev = S.finals()

    def ln_pass(self, tag, Rcur, s_xlnT, pooled_out=None, ctx=None):
        nc, nt, T = self.nc, self.nt, self.T
        S = Sems(self, tag, ["sum", "xln", "ev", "sq", "chn", "tr", "pm", "pl"])
        RL = Ring(self, f"{tag}ld", 3)
        prev = self.prev
        import contextlib as _ctl
        lctx = _ctl.ExitStack()
        s_xlnb = lctx.enter_context(nc.sbuf_tensor(f"{tag}_xlnb", [128, 2 * D], BF16))
        s_xin = lctx.enter_context(nc.sbuf_tensor(f"{tag}_xin", [128, 3 * D], F32))
        s_sq = lctx.enter_context(nc.sbuf_tensor(f"{tag}_sq", [128, D], F32))
        s_pacc = None
        if pooled_out is not None:
            s_pacc = lctx.enter_context(nc.sbuf_tensor(f"{tag}_pacc", [64, D], F32))
        with nc.Block(f"b_{tag}") as blk:
            @blk.sync
            def _(s):
                wait_finals(s, prev)
                for i in range(nt):
                    if i >= 3:
                        S.wait(s, "xln", i - 2)
                    RL.inc(s.dma_start(out=s_xin[:, (i % 3) * D:(i % 3 + 1) * D],
                                       in_=Rcur[i * 128:(i + 1) * 128, :]), i)

            @blk.scalar
            def _(sc):
                wait_finals(sc, prev)
                st = self.s_stat
                for i in range(nt):
                    RL.wait_done(sc, i)
                    S.wait(sc, "chn", i)
                    xin = s_xin[:, (i % 3) * D:(i % 3 + 1) * D]
                    S.edge(sc, "sq", sc.activation(s_sq[:, :], xin, AF.Square,
                                                   accum_out=st[:, 2 * nt + i:2 * nt + i + 1]))
                    S.wait(sc, "sum", i + 1)
                    S.edge(sc, "sq", sc.activation(st[:, nt + i:nt + i + 1], st[:, i:i + 1],
                                                   AF.Identity, scale=1.0 / D))
                    S.edge(sc, "sq", sc.activation(s_sq[:, 0:1],
                                                   st[:, nt + i:nt + i + 1], AF.Square))
                    S.edge(sc, "sq", sc.activation(s_sq[:, 1:2], s_sq[:, 0:1],
                                                   AF.Identity, scale=-1.0))
                    S.edge(sc, "sq", sc.activation(s_sq[:, 2:3],
                                                   st[:, 2 * nt + i:2 * nt + i + 1],
                                                   AF.Identity, scale=1.0 / D,
                                                   bias=s_sq[:, 1:2]))
                    S.edge(sc, "sq", sc.activation(s_sq[:, 3:4], s_sq[:, 2:3],
                                                   AF.Ln, bias=self.s_eps[:, :]))
                    S.inc(sc.activation(st[:, 3 * nt + i:3 * nt + i + 1],
                                        s_sq[:, 3:4], AF.Exp, scale=-0.5), "chn", 1)

            @blk.vector
            def _(v):
                wait_finals(v, prev)

                def emit_xln(i):
                    S.wait(v, "chn", i + 1)
                    if i >= 2:
                        S.wait(v, "tr", i - 1)
                    xin = s_xin[:, (i % 3) * D:(i % 3 + 1) * D]
                    mean = self.s_stat[:, nt + i:nt + i + 1]
                    rstd = self.s_stat[:, 3 * nt + i:3 * nt + i + 1]
                    dst = s_xlnb[:, (i % 2) * D:(i % 2 + 1) * D]
                    S.inc(v.tensor_scalar(dst, xin, mean, rstd, ALU.subtract, ALU.mult),
                          "xln", 1)

                def emit_evac(i):
                    S.wait(v, "tr", i + 1)
                    src = self.pb[2 + (i % 2)].ap().bitcast(BF16)
                    dstv = s_xlnT.ap().rearrange("p (c t) -> p c t", c=8)[:, :, i * 128:(i + 1) * 128]
                    S.inc(v.tensor_copy(dstv, src.rearrange("p (c t) -> p c t", c=8)),
                          "ev", 1)
                    if pooled_out is not None:
                        S.wait(v, "pm", i + 1)
                        if i == 0:
                            v.tensor_copy(s_pacc[:, 0:512], self.pb[6].ap()[0:NLAT, :])
                            iv = v.tensor_copy(s_pacc[:, 512:1024], self.pb[7].ap()[0:NLAT, :])
                        else:
                            v.tensor_tensor(s_pacc[:, 0:512], s_pacc[:, 0:512],
                                            self.pb[6].ap()[0:NLAT, :], ALU.add)
                            iv = v.tensor_tensor(s_pacc[:, 512:1024], s_pacc[:, 512:1024],
                                                 self.pb[7].ap()[0:NLAT, :], ALU.add)
                        S.inc(iv, "pl", 1)

                for i in range(nt):
                    RL.wait_done(v, i)
                    xin = s_xin[:, (i % 3) * D:(i % 3 + 1) * D]
                    S.inc(v.tensor_reduce(self.s_stat[:, i:i + 1], xin, AX.X, ALU.add),
                          "sum", 1)
                    if i >= 3:
                        emit_evac(i - 3)
                    if i >= 2:
                        emit_xln(i - 2)
                emit_evac(nt - 3)
                emit_xln(nt - 2)
                emit_evac(nt - 2)
                emit_xln(nt - 1)
                emit_evac(nt - 1)
                if pooled_out is not None:
                    S.wait(v, "pl", S.c["pl"])
                    S.inc(v.tensor_copy(pooled_out[:, :], s_pacc[:, :]), "pl", 1)

            @blk.tensor
            def _(t):
                wait_finals(t, prev)
                for i in range(nt):
                    S.wait(t, "xln", i + 1)
                    src = s_xlnb[:, (i % 2) * D:(i % 2 + 1) * D]
                    if pooled_out is not None:
                        if i >= 1:
                            S.wait(t, "pl", i)
                        t.matmul(self.pb[6].ap()[0:NLAT, :],
                                 self.s_ind[:, i * NLAT:(i + 1) * NLAT],
                                 src[:, 0:512], start=True, stop=True)
                        S.inc(t.matmul(self.pb[7].ap()[0:NLAT, :],
                                       self.s_ind[:, i * NLAT:(i + 1) * NLAT],
                                       src[:, 512:1024], start=True, stop=True), "pm", 1)
                    if i >= 2:
                        S.wait(t, "ev", i - 1)
                    dst = self.pb[2 + (i % 2)].ap().bitcast(BF16)
                    for c in range(8):
                        iv = t.transpose(dst[:, c * 128:(c + 1) * 128],
                                         src[:, c * 128:(c + 1) * 128], self.s_id_bf[:, :])
                    S.inc(iv, "tr", 1)
        lctx.close()
        f = S.finals()
        f.update(RL.finals())
        self.prev = f

    def proj_spill(self, tag, s_xlnT, W_in=None, COLS=0, bias_in=None, fcmodes=(),
                   Qdst=None, Kdst=None, Vdst=None, do_pp=False, do_route=False,
                   rq=None, xfm_spill=None, ctx=None, dbg_xlnT=False):
        nc, nt, T, ng = self.nc, self.nt, self.T, self.ng
        d = self.d
        S = Sems(self, tag, ["wld", "mm", "ev", "tr", "fmv", "sp", "pp", "ppv", "rt", "ra", "ppd", "e2"], sw=("wld",))
        RV = Ring(self, f"{tag}rv", 2, base_idx=16)
        RF = Ring(self, f"{tag}rf", 2, base_idx=18)
        prev = self.prev
        nfc = len(fcmodes)
        W = 512
        s_w = s_bias = s_fm = s_ev = s_vev = None
        if W_in is not None:
            s_w = ctx.enter_context(nc.sbuf_tensor(f"{tag}_w", [128, 2 * 8 * 512], BF16))
            if bias_in is not None:
                s_bias = ctx.enter_context(nc.sbuf_tensor(f"{tag}_bias", [1, COLS], BF16))
            s_fm = ctx.enter_context(nc.sbuf_tensor(f"{tag}_fm", [128, 4 * T], BF16))
            s_vev = ctx.enter_context(nc.sbuf_tensor(f"{tag}_vev", [128, 2 * 512], BF16))
        if W_in is not None or rq is not None:
            s_ev = ctx.enter_context(nc.sbuf_tensor(f"{tag}_ev", [128, 2 * 512], BF16))
        s_pp = s_ppb = None
        if do_pp:
            s_pp = ctx.enter_context(nc.sbuf_tensor(f"{tag}_pp", [NH, T], F32))
            s_ppb = ctx.enter_context(nc.sbuf_tensor(f"{tag}_ppb", [NH, T], BF16))
        s_gt = s_gb = s_rt = None
        if do_route:
            s_gt = ctx.enter_context(nc.sbuf_tensor(f"{tag}_gt", [128, 8 * NE], BF16))
            s_gb = ctx.enter_context(nc.sbuf_tensor(f"{tag}_gb", [1, NE], BF16))
            s_rt = ctx.enter_context(nc.sbuf_tensor(f"{tag}_rt", [128, 4 * NE], F32))
        s_rqw = s_rqb = s_rqT = None
        if rq is not None:
            wrq_in, brq_in, s_rqT = rq
            s_rqw = ctx.enter_context(nc.sbuf_tensor(f"{tag}_rqw", [128, 8 * LATD], BF16))
            s_rqb = ctx.enter_context(nc.sbuf_tensor(f"{tag}_rqb", [1, LATD], BF16))

        xlnT_ch = s_xlnT.ap().rearrange("p (c t) -> p c t", c=8)
        nts = T // 512
        nqk = sum(1 for m in fcmodes if m in ("q", "k"))
        units = nfc * nt + (nt if rq is not None else 0)

        def fcw(fc):
            return 256 if fc == nfc else 512

        def wsrc(fc, c):
            if fc == nfc:
                return s_rqw[:, c * LATD:c * LATD + 256]
            base = (fc % 2) * 8 * 512 + c * 512
            return s_w[:, base:base + 512]

        def bsrc(fc):
            if fc == nfc:
                return s_rqb[:, :]
            return s_bias[:, fc * 512:(fc + 1) * 512] if s_bias is not None else None

        allmodes = list(fcmodes) + (["rq"] if rq is not None else [])

        with nc.Block(f"b_{tag}") as blk:
            @blk.gpsimd
            def _(g):
                wait_finals(g, prev)
                if W_in is not None and bias_in is not None:
                    S.inc(g.dma_start(out=s_bias[:, :], in_=bias_in), "wld", 16)
                if do_route:
                    S.inc(g.dma_start(out=s_gt.ap().rearrange("p (c n) -> p c n", c=8),
                                      in_=rr8(d["gateT"])), "wld", 16)
                    S.inc(g.dma_start(out=s_gb[:, :], in_=d["gbias"]), "wld", 16)
                if rq is not None:
                    S.inc(g.dma_start(out=s_rqw.ap().rearrange("p (c n) -> p c n", c=8),
                                      in_=rr8(wrq_in)), "wld", 16)
                    S.inc(g.dma_start(out=s_rqb[:, :], in_=brq_in), "wld", 16)
                if W_in is not None:
                    for fc in range(nfc):
                        if fc >= 2:
                            S.wait(g, "mm", (fc - 1) * nt)
                        sl = self.wslots[fc % 2]
                        ins = g.dma_start(
                            out=s_w.ap().rearrange("p (b c n) -> p b c n", b=2, c=8)[:, fc % 2],
                            in_=W_in[:, fc * 512:(fc + 1) * 512].rearrange("(c p) n -> p c n", p=128))
                        ins.then_inc(sl.h, 16)
                        sl.c += 16
                        self.wtarget[(tag, fc)] = sl.c

            @blk.sync
            def _(s):
                wait_finals(s, prev)
                u = 0
                vv = 0
                for fc, mode in enumerate(allmodes):
                    if mode in ("q", "k"):
                        qki = sum(1 for f in allmodes[:fc] if f in ("q", "k"))
                        S.wait(s, "fmv", (qki + 1) * nt)
                        dstD = Qdst if mode == "q" else Kdst
                        for c in range(4):
                            gch = (fc % 2) * 4 + c
                            RF.inc(s.dma_start(out=dstD[gch * 128:(gch + 1) * 128, :],
                                               in_=s_fm[:, c * T:(c + 1) * T]), qki)
                        if do_pp and fc == 1:
                            S.wait(s, "ppv", 2 * nts + 1)
                            S.inc(s.dma_start(out=d["qp"], in_=s_ppb[:, :]), "ppd", 16)
                        if do_pp and fc == 3:
                            S.wait(s, "ppv", 4 * nts + 2)
                            S.inc(s.dma_start(out=d["kp"], in_=s_ppb[:, :]), "sp", 16)
                        u += nt
                    elif mode == "rq":
                        u += nt
                    else:
                        vfc = sum(1 for f in allmodes[:fc] if f == "v")
                        for i in range(nt):
                            S.wait(s, "ev", u + 1)
                            RV.inc(s.dma_start(out=Vdst[i * 128:(i + 1) * 128,
                                                        vfc * 512:(vfc + 1) * 512],
                                               in_=s_vev[:, (vv % 2) * 512:(vv % 2 + 1) * 512]),
                                   vv)
                            u += 1
                            vv += 1

                if xfm_spill is not None:
                    for c in range(8):
                        S.inc(s.dma_start(out=xfm_spill[c * 128:(c + 1) * 128, :],
                                          in_=s_xlnT[:, c * T:(c + 1) * T]), "sp", 16)
                if dbg_xlnT and "d_xlnT" in self.dbg:
                    for c in range(8):
                        S.inc(s.dma_start(out=self.dbg["d_xlnT"][c * 128:(c + 1) * 128, :],
                                          in_=s_xlnT[:, c * T:(c + 1) * T]), "sp", 16)
            @blk.tensor
            def _(t):
                wait_finals(t, prev)
                S.wait(t, "wld", S.c["wld"])
                u = 0
                uq = 0
                for fc, mode in enumerate(allmodes):
                    w = fcw(fc)
                    if W_in is not None and fc < nfc:
                        t.wait_ge(self.wslots[fc % 2].h, self.wtarget[(tag, fc)])
                    for i in range(nt):
                        if u >= 2:
                            S.wait(t, "ev", u - 1)
                        bank = self.pb[u % 2].ap()[:, 0:w]
                        hasb = bsrc(fc) is not None
                        last = None
                        for c in range(8):
                            last = t.matmul(bank, xlnT_ch[:, c, i * 128:(i + 1) * 128],
                                            wsrc(fc, c), start=(c == 0),
                                            stop=(c == 7 and not hasb))
                        if hasb:
                            last = t.matmul(bank, self.s_ones_bf[:, 0:128], bsrc(fc),
                                            start=False, stop=True)
                        S.inc(last, "mm", 1)
                        if mode in ("q", "k", "rq"):
                            S.wait(t, "ev", u + 1)
                            if uq >= 2:
                                S.wait(t, "fmv", uq - 1)
                            dstp = self.pb[2 + (uq % 2)].ap().bitcast(BF16)
                            for c in range(w // 128):
                                iv2 = t.transpose(dstp[:, c * 128:(c + 1) * 128],
                                                  s_ev[:, (uq % 2) * 512 + c * 128:(uq % 2) * 512 + (c + 1) * 128],
                                                  self.s_id_bf[:, :])
                            S.inc(iv2, "tr", 1)
                            uq += 1
                        u += 1
                    if do_pp and mode in ("q", "k"):
                        qki = sum(1 for f in allmodes[:fc] if f in ("q", "k"))
                        S.wait(t, "fmv", (qki + 1) * nt)
                        for ts in range(nts):
                            pidx = fc * nts + ts
                            if pidx >= 2:
                                S.wait(t, "ppv", pidx - 1)
                            bank = self.pb[4 + (pidx % 2)].ap()[0:NH, :]
                            for c in range(4):
                                gch = (fc % 2) * 4 + c
                                iv = t.matmul(bank,
                                              self.s_hd[:, gch * NH:(gch + 1) * NH],
                                              s_fm[:, c * T + ts * 512:c * T + ts * 512 + 512],
                                              start=(c == 0), stop=(c == 3))
                            S.inc(iv, "pp", 1)
                if do_route:
                    for i in range(nt):
                        if i >= 2:
                            S.wait(t, "ra", i - 1)
                        bank = self.pb[4 + (i % 2)].ap()[:, 0:NE]
                        for c in range(8):
                            t.matmul(bank, xlnT_ch[:, c, i * 128:(i + 1) * 128],
                                     s_gt[:, c * NE:(c + 1) * NE],
                                     start=(c == 0), stop=False)
                        iv = t.matmul(bank, self.s_ones_bf[:, 0:128], s_gb[:, :],
                                      start=False, stop=True)
                        S.inc(iv, "mm", 1)

            @blk.vector
            def _(v):
                wait_finals(v, prev)
                u = 0
                uq = 0
                vv = 0
                for fc, mode in enumerate(allmodes):
                    w = fcw(fc)
                    for i in range(nt):
                        S.wait(v, "mm", u + 1)
                        bank = self.pb[u % 2].ap()[:, 0:w]
                        if mode in ("q", "rq"):
                            if uq >= 2:
                                S.wait(v, "tr", uq - 1)
                            S.inc(v.tensor_scalar(s_ev[:, (uq % 2) * 512:(uq % 2) * 512 + w],
                                                  bank, 0.125, None, ALU.mult), "ev", 1)
                        elif mode == "k":
                            if uq >= 2:
                                S.wait(v, "tr", uq - 1)
                            S.inc(v.tensor_copy(s_ev[:, (uq % 2) * 512:(uq % 2) * 512 + w],
                                                bank), "ev", 1)
                        else:
                            if vv >= 2:
                                RV.wait_done(v, vv - 2)
                            S.inc(v.tensor_copy(s_vev[:, (vv % 2) * 512:(vv % 2 + 1) * 512],
                                                bank), "ev", 1)
                            vv += 1
                        if mode in ("q", "k", "rq"):
                            if mode in ("q", "k") and i == 0:
                                qki = sum(1 for f in allmodes[:fc] if f in ("q", "k"))
                                RF.wait_done(v, qki - 1)
                            S.wait(v, "tr", uq + 1)
                            src = self.pb[2 + (uq % 2)].ap().bitcast(BF16)
                            nblk = w // 128
                            if mode == "rq":
                                dstv = s_rqT.ap().rearrange("p (c t) -> p c t", c=2)[:, :, i * 128:(i + 1) * 128]
                            else:
                                dstv = s_fm.ap().rearrange("p (c t) -> p c t", c=4)[:, :, i * 128:(i + 1) * 128]
                            S.inc(v.tensor_copy(dstv, src.rearrange("p (c t) -> p c t", c=8)[:, 0:nblk, :]),
                                  "fmv", 1)
                            uq += 1
                        u += 1
                    if do_pp and mode in ("q", "k"):
                        for ts in range(nts):
                            pidx = fc * nts + ts
                            S.wait(v, "pp", pidx + 1)
                            bank = self.pb[4 + (pidx % 2)].ap()[0:NH, :]
                            dst = s_pp[:, ts * 512:(ts + 1) * 512]
                            if fc % 2 == 0:
                                iv = v.tensor_copy(dst, bank)
                            else:
                                iv = v.tensor_tensor(dst, dst, bank, ALU.add)
                            S.inc(iv, "ppv", 1)
                        if fc % 2 == 1:
                            S.wait(v, "ppv", S.c["ppv"])
                            if mode == "q":
                                iv = v.tensor_scalar(s_ppb[:, :], s_pp[:, :], self.s_hs8[:, :],
                                                     None, ALU.mult)
                            else:
                                S.wait(v, "ppd", 16)
                                iv = v.tensor_copy(s_ppb[:, :], s_pp[:, :])
                            S.inc(iv, "ppv", 1)
                if do_route:
                    mmbase = units
                    for i in range(nt):
                        rt = s_rt[:, (i % 2) * 2 * NE:(i % 2) * 2 * NE + NE]
                        rt2 = s_rt[:, (i % 2) * 2 * NE + NE:(i % 2) * 2 * NE + 2 * NE]
                        S.wait(v, "ra", i + 1)
                        S.wait(v, "rt", S.c["rt"])
                        m1 = self.s_tmp1[:, 0:1]
                        S.edge(v, "e2", v.tensor_reduce(m1, rt, AX.X, ALU.max))
                        S.edge(v, "e2", v.tensor_scalar(rt2, rt, m1, None, ALU.is_ge))
                        S.edge(v, "e2", v.scalar_tensor_tensor(rt2, rt2, -1e9, rt, ALU.mult, ALU.add))
                        S.edge(v, "e2", v.tensor_reduce(m1, rt2, AX.X, ALU.max))
                        S.edge(v, "e2", v.tensor_scalar(rt2, rt, m1, None, ALU.is_ge))
                        cw = self.s_cw[:, i * NE:(i + 1) * NE]
                        S.edge(v, "e2", v.tensor_tensor(cw, rt, rt2, ALU.mult))
                        S.edge(v, "e2", v.tensor_reduce(m1, cw, AX.X, ALU.add))
                        S.edge(v, "e2", v.tensor_scalar(m1, m1, 1e-20, None, ALU.add))
                        S.edge(v, "e2", v.reciprocal(m1, m1))
                        S.inc(v.tensor_scalar(cw, cw, m1, None, ALU.mult), "rt", 1)

            if do_route:
                @blk.scalar
                def _(sc):
                    wait_finals(sc, prev)
                    mmbase = units
                    for i in range(nt):
                        S.wait(sc, "mm", mmbase + i + 1)
                        bank = self.pb[4 + (i % 2)].ap()[:, 0:NE]
                        rt = s_rt[:, (i % 2) * 2 * NE:(i % 2) * 2 * NE + NE]
                        if i >= 2:
                            S.wait(sc, "rt", i - 1)
                        S.inc(sc.activation(rt, bank, AF.Sigmoid), "ra", 1)

        f = S.finals()
        f.update(RV.finals())
        f.update(RF.finals())
        self.prev = f

    def attn_a_heads(self, s_ao, ctx=None):
        nc, nt, T, ng = self.nc, self.nt, self.T, self.ng
        d = self.d
        S = Sems(self, "a2", ["pa", "rm", "sx", "sro", "pbm", "mk", "ex", "pv",
                              "nz", "lx", "dbg", "ms", "e1", "sri"], sw=("sri",))
        RK = Ring(self, "a2kld", 2, base_idx=16)
        prev = self.prev
        s_K = [ctx.enter_context(nc.sbuf_tensor(f"a2_K{p}", [66, T], BF16)) for p in range(2)]
        s_Q = [ctx.enter_context(nc.sbuf_tensor(f"a2_Q{p}", [66, T], BF16)) for p in range(2)]
        s_V = [ctx.enter_context(nc.sbuf_tensor(f"a2_V{p}", [128, nt * 65], BF16)) for p in range(2)]
        s_pt = ctx.enter_context(nc.sbuf_tensor("a2_pt", [128, 2 * 512], BF16))

        npa_h = nt * (nt + 4) // 8
        npa_h = sum(i // 4 + 1 for i in range(nt))
        nbu_h = sum(4 * qg + 4 for qg in range(ng))
        pbm_end = [nbu_h * (h + 1) for h in range(NH)]
        pv_end = [nbu_h * (h + 1) for h in range(NH)]

        with nc.Block("b_a2") as blk:
            @blk.sync
            def _(s):
                wait_finals(s, prev)
                for p in range(2):
                    S.inc(s.dma_start(out=s_K[p].ap()[65:66, :], in_=d["onesT"]), "ms", 16)
                for h in range(NH):
                    par = h % 2
                    if h >= 2:
                        S.wait(s, "pbm", pbm_end[h - 2])
                        S.wait(s, "pv", pv_end[h - 2])
                    RK.inc(s.dma_start(out=s_K[par].ap()[0:64, :],
                                       in_=d["Kh"][h * 64:(h + 1) * 64, :]), h)
                    RK.inc(s.dma_start(out=s_K[par].ap()[64:65, :],
                                       in_=d["kp"][h:h + 1, :]), h)
                    RK.inc(s.dma_start(out=s_Q[par].ap()[0:64, :],
                                       in_=d["Qh"][h * 64:(h + 1) * 64, :]), h)
                    RK.inc(s.dma_start(out=s_Q[par].ap()[64:65, :],
                                       in_=d["qp"][h:h + 1, :]), h)
                    RK.inc(s.dma_start(out=s_V[par].ap().rearrange("p (t c) -> p t c", c=65)[:, :, 0:64],
                                       in_=d["Vh"][:, h * 64:(h + 1) * 64].rearrange("(t p) d -> p t d", p=128)),
                           h)
                    S.wait(s, "sx", 3 * h + 3)
                    S.wait(s, "sri", 16 * h)
                    S.inc(s.dma_start(out=d["stats"], in_=self.s_row[0:nt, :]), "sro", 16)
                    if "d_m" in self.dbg:
                        S.inc(s.dma_start(out=self.dbg["d_m"][h * nt:(h + 1) * nt, :],
                                          in_=self.s_row[0:nt, :]), "dbg", 16)
                if "d_attnout" in self.dbg:
                    S.wait(s, "nz", 3 * ng * NH)
                    for c in range(8):
                        S.inc(s.dma_start(out=self.dbg["d_attnout"][c * 128:(c + 1) * 128, :],
                                          in_=s_ao[:, c * T:(c + 1) * T]), "dbg", 16)

            @blk.gpsimd
            def _(g):
                wait_finals(g, prev)
                for h in range(NH):
                    par = h % 2
                    S.wait(g, "sro", 16 * (h + 1))
                    if h >= 2:
                        S.wait(g, "pbm", pbm_end[h - 2])
                    S.inc(g.dma_start(out=s_Q[par].ap()[65:66, :],
                                      in_=d["stats"].rearrange("a b -> (a b)").unsqueeze(0)),
                          "sri", 16)

            @blk.tensor
            def _(t):
                wait_finals(t, prev)
                au = 0
                bu = 0
                qgi = 0
                S.wait(t, "ms", 33)

                def pe_passA(h):
                    nonlocal au
                    par = h % 2
                    RK.wait_done(t, h)
                    for i in range(nt):
                        for kg in range(i // 4 + 1):
                            if au >= 2:
                                S.wait(t, "rm", au - 1)
                            bank = self.pb[au % 2].ap()
                            S.inc(t.matmul(bank, s_Q[par].ap()[0:65, i * 128:(i + 1) * 128],
                                           s_K[par].ap()[0:65, kg * 512:(kg + 1) * 512],
                                           start=True, stop=True), "pa", 1)
                            au += 1
                    S.wait(t, "sx", 3 * h + 1)
                    S.inc(t.transpose(self.pb[6].ap()[0:nt, 0:128], self.s_mneg[:, 0:nt],
                                      self.s_id_f32[:, :]), "sx", 1)

                def pe_passB(h):
                    nonlocal bu, qgi
                    par = h % 2
                    S.wait(t, "sri", 16 * (h + 1))
                    for qg in range(ng):
                        nkt = 4 * qg + 4

                        def emit_pv(kt, bu_kt):
                            qoff = max(0, (kt - 4 * qg) * 128)
                            N = 512 - qoff
                            S.wait(t, "ex", bu_kt + 1)
                            if kt == 0:
                                S.wait(t, "nz", 3 * qgi)
                            S.inc(t.matmul(self.pb[4].ap()[0:65, qoff:512],
                                           s_V[par].ap()[:, kt * 65:kt * 65 + 65],
                                           s_pt[:, (bu_kt % 2) * 512:(bu_kt % 2) * 512 + N],
                                           start=(kt == 0), stop=(kt == nkt - 1)), "pv", 1)

                        for kt in range(nkt):
                            qoff = max(0, (kt - 4 * qg) * 128)
                            N = 512 - qoff
                            if bu >= 2:
                                S.wait(t, "ex", bu - 1)
                            bank = self.pb[2 + bu % 2].ap()
                            S.inc(t.matmul(bank[:, 0:N],
                                           s_K[par].ap()[0:66, kt * 128:(kt + 1) * 128],
                                           s_Q[par].ap()[0:66, qg * 512 + qoff:(qg + 1) * 512],
                                           start=True, stop=True), "pbm", 1)
                            if kt > 0:
                                emit_pv(kt - 1, bu - 1)
                            bu += 1
                        emit_pv(nkt - 1, bu - 1)
                        S.wait(t, "nz", 3 * qgi + 1)
                        S.inc(t.matmul(self.pb[5].ap()[0:64, :], self.s_ones_f32[0:1, 0:64],
                                       self.s_linv[0:1, 0:512], start=True, stop=True),
                              "lx", 1)
                        qgi += 1

                for h in range(NH):
                    pe_passA(h)
                    if h >= 1:
                        pe_passB(h - 1)
                pe_passB(NH - 1)

            @blk.vector
            def _(v):
                wait_finals(v, prev)
                for p in range(2):
                    iv = v.memset(s_V[p].ap().rearrange("p (t c) -> p t c", c=65)[:, :, 64:65], 1.0)
                S.inc(iv, "ms", 1)
                au = 0
                qgi = 0

                def dve_passA(h):
                    nonlocal au
                    for i in range(nt):
                        ndiag = i // 4
                        for kg in range(ndiag + 1):
                            S.wait(v, "pa", au + 1)
                            bank = self.pb[au % 2].ap()
                            if kg == ndiag:
                                S.edge(v, "e1", v.tensor_tensor(
                                    self.s_msk[:, :], bank,
                                    self.s_maskA[:, (i % 4) * 512:(i % 4 + 1) * 512],
                                    ALU.add))
                                srcr = self.s_msk[:, :]
                            else:
                                srcr = bank
                            if kg == 0:
                                iv = v.tensor_reduce(self.s_m[:, i:i + 1], srcr, AX.X, ALU.max)
                            else:
                                S.edge(v, "e1", v.tensor_reduce(self.s_tmp1[:, 1:2], srcr,
                                                                AX.X, ALU.max))
                                iv = v.tensor_tensor(self.s_m[:, i:i + 1], self.s_m[:, i:i + 1],
                                                     self.s_tmp1[:, 1:2], ALU.max)
                            S.inc(iv, "rm", 1)
                            S.wait(v, "rm", S.c["rm"])
                            au += 1
                    if h >= 1:
                        S.wait(v, "sx", 3 * (h - 1) + 2)
                    S.inc(v.tensor_scalar(self.s_mneg[:, 0:nt], self.s_m[:, 0:nt],
                                          -1e8, -1.0, ALU.max, ALU.mult), "sx", 1)
                    S.wait(v, "sx", 3 * h + 2)
                    S.wait(v, "sro", 16 * h)
                    if "d_m" in self.dbg:
                        S.wait(v, "dbg", 16 * h)
                    S.inc(v.tensor_copy(self.s_row[0:nt, :], self.pb[6].ap()[0:nt, 0:128]),
                          "sx", 1)

                def dve_passB(h):
                    nonlocal qgi
                    for qg in range(ng):
                        for kt in range(4 * qg, 4 * qg + 4):
                            buu = h * nbu_h + sum(4 * g + 4 for g in range(qg)) + kt
                            S.wait(v, "pbm", buu + 1)
                            bank = self.pb[2 + buu % 2].ap()
                            S.inc(v.tensor_tensor(bank[0:128, 0:128], bank[0:128, 0:128],
                                                  self.s_maskBT[:, :], ALU.add), "mk", 1)
                        S.wait(v, "pv", h * nbu_h + sum(4 * g + 4 for g in range(qg + 1)))
                        if qgi >= 1:
                            S.wait(v, "lx", qgi)
                        S.edge(v, "e1", v.tensor_scalar(self.s_linv[:, :],
                                                        self.pb[4].ap()[64:65, 0:512],
                                                        1e-30, None, ALU.max))
                        S.inc(v.reciprocal(self.s_linv[:, :], self.s_linv[:, :]), "nz", 1)
                        S.wait(v, "lx", qgi + 1)
                        iv = v.tensor_copy(self.s_lexp[:, :], self.pb[5].ap()[0:64, :])
                        S.inc(iv, "nz", 1)
                        S.wait(v, "nz", S.c["nz"])
                        dst = s_ao.ap()[(h % 2) * 64:(h % 2) * 64 + 64,
                                        (h // 2) * T + qg * 512:(h // 2) * T + (qg + 1) * 512]
                        S.inc(v.tensor_tensor(dst, self.pb[4].ap()[0:64, 0:512],
                                              self.s_lexp[:, :], ALU.mult), "nz", 1)
                        qgi += 1

                for h in range(NH):
                    dve_passA(h)
                    if h >= 1:
                        dve_passB(h - 1)
                dve_passB(NH - 1)

            @blk.scalar
            def _(sc):
                wait_finals(sc, prev)
                bu = 0
                for h in range(NH):
                    for qg in range(ng):
                        for kt in range(4 * qg + 4):
                            qoff = max(0, (kt - 4 * qg) * 128)
                            N = 512 - qoff
                            S.wait(sc, "pbm", bu + 1)
                            bank = self.pb[2 + bu % 2].ap()
                            if kt >= 4 * qg:
                                S.wait(sc, "mk", self._mkcount(h, qg, kt, ng))
                            if bu >= 2:
                                S.wait(sc, "pv", bu - 1)
                            S.inc(sc.activation(s_pt[:, (bu % 2) * 512:(bu % 2) * 512 + N],
                                                bank[:, 0:N], AF.Exp), "ex", 1)
                            bu += 1

        f = S.finals()
        f.update(RK.finals())
        self.prev = f

    def _mkcount(self, h, qg, kt, ng):
        return h * 4 * ng + qg * 4 + (kt - 4 * qg) + 1

    def proj_residual(self, tag, s_src, W_in, bias_in, Rcur, Rnext, ctx=None,
                      extra_dump=None):
        nc, nt, T = self.nc, self.nt, self.T
        S = Sems(self, tag, ["wld", "mm", "ve", "st"], sw=("wld",))
        RR = Ring(self, f"{tag}rr", 4, base_idx=16)
        RS = Ring(self, f"{tag}rs", 2, base_idx=20)
        prev = self.prev
        s_w = ctx.enter_context(nc.sbuf_tensor(f"{tag}_w", [128, 8 * D], BF16))
        s_res = ctx.enter_context(nc.sbuf_tensor(f"{tag}_res", [128, 2 * D], F32))
        s_bias = None
        if bias_in is not None:
            s_bias = ctx.enter_context(nc.sbuf_tensor(f"{tag}_bias", [1, D], BF16))
        src_ch = s_src.ap().rearrange("p (c t) -> p c t", c=8)

        with nc.Block(f"b_{tag}") as blk:
            @blk.gpsimd
            def _(g):
                wait_finals(g, prev)
                S.inc(g.dma_start(out=s_w.ap().rearrange("p (c n) -> p c n", c=8),
                                  in_=rr8(W_in)), "wld", 16)
                if bias_in is not None:
                    S.inc(g.dma_start(out=s_bias[:, :], in_=bias_in), "wld", 16)

            @blk.tensor
            def _(t):
                wait_finals(t, prev)
                S.wait(t, "wld", S.c["wld"])
                for i in range(nt):
                    for half in range(2):
                        if i >= 1:
                            S.wait(t, "ve", 2 * i - 1 + half)
                        bank = self.pb[half].ap()
                        last = None
                        for c in range(8):
                            last = t.matmul(bank, src_ch[:, c, i * 128:(i + 1) * 128],
                                            s_w[:, c * D + half * 512:c * D + half * 512 + 512],
                                            start=(c == 0), stop=(c == 7 and bias_in is None))
                        if bias_in is not None:
                            last = t.matmul(bank, self.s_ones_bf[:, 0:128],
                                            s_bias[:, half * 512:(half + 1) * 512],
                                            start=False, stop=True)
                        S.inc(last, "mm", 1)

            @blk.sync
            def _(s):
                wait_finals(s, prev)
                def emit_store(i):
                    S.wait(s, "ve", 2 * i + 2)
                    RS.inc(s.dma_start(out=Rnext[i * 128:(i + 1) * 128, :],
                                       in_=s_res[:, (i % 2) * D:(i % 2 + 1) * D]), i)

                for i in range(nt):
                    if i >= 4:
                        emit_store(i - 4)
                        S.wait(s, "ve", 2 * (i - 3))
                    RR.inc(s.dma_start(out=self.s_acc[:, (i % 4) * D:(i % 4 + 1) * D],
                                       in_=Rcur[i * 128:(i + 1) * 128, :]), i)
                for i in range(max(0, nt - 4), nt):
                    emit_store(i)
                if extra_dump is not None:
                    for i in range(nt):
                        RS.wait_done(s, i)
                        S.inc(s.dma_start(out=extra_dump[i * 128:(i + 1) * 128, :],
                                          in_=Rnext[i * 128:(i + 1) * 128, :]), "st", 16)
                for i in range(nt - 2, nt):
                    RS.wait_done(s, i)
                S.wait(s, "st", S.c["st"])

            @blk.vector
            def _(v):
                wait_finals(v, prev)
                for i in range(nt):
                    RR.wait_done(v, i)
                    if i >= 2:
                        RS.wait_done(v, i - 2)
                    for half in range(2):
                        S.wait(v, "mm", 2 * i + half + 1)
                        S.inc(v.tensor_tensor(s_res[:, (i % 2) * D + half * 512:(i % 2) * D + half * 512 + 512],
                                              self.pb[half].ap(),
                                              self.s_acc[:, (i % 4) * D + half * 512:(i % 4) * D + half * 512 + 512],
                                              ALU.add), "ve", 1)
        f = S.finals()
        f.update(RR.finals())
        f.update(RS.finals())
        self.prev = f



def host_consts(T):
    nt = T // 128
    CH = T // NLAT
    c = {}
    c["id_bf"] = np.eye(128, dtype=bf)
    c["id_f32"] = np.eye(128, dtype=np.float32)
    c["ones_bf"] = np.ones((1, 512), dtype=bf)
    c["onesT"] = np.ones((1, T), dtype=bf)
    c["ones_f32"] = np.ones((1, 64), dtype=np.float32)
    mA = np.zeros((4, 128, 512), dtype=np.float32)
    q = np.arange(128)[:, None]
    k = np.arange(128)[None, :]
    tri = np.where(k > q, NEG, 0.0).astype(np.float32)
    for v in range(4):
        for jj in range(4):
            if jj < v:
                pass
            elif jj == v:
                mA[v][:, jj * 128:(jj + 1) * 128] = tri
            else:
                mA[v][:, jj * 128:(jj + 1) * 128] = NEG
    c["maskA"] = mA.reshape(4 * 128, 512).astype(bf)
    c["maskBT"] = np.ascontiguousarray(np.where(k > q, NEG, 0.0).T).astype(bf)
    mL = np.zeros((128, 256), dtype=np.float32)
    mL[:, 0:128] = np.where(k >= q, 0.0, NEG)
    mL[:, 128:256] = np.where(k < q, 0.0, NEG)
    c["maskL"] = mL.astype(bf)
    c["maskLT1"] = np.ascontiguousarray(np.where(k >= q, 0.0, NEG).T).astype(bf)
    c["maskLT2"] = np.ascontiguousarray(np.where(k < q, 0.0, NEG).T).astype(bf)
    lpt = max(1, 128 // CH)
    l = np.arange(NLAT)[None, :]
    p = np.arange(128)[:, None]
    c["latC"] = (l - p // CH).astype(np.float32)
    c["latCT"] = np.ascontiguousarray(c["latC"].T)
    tok = np.arange(nt * 128)[:, None]
    c["ind"] = np.where(np.arange(NLAT)[None, :] == tok // CH, 1.0 / CH, 0.0).astype(bf)
    return c


def host_inputs(inputs, core, T):
    g = lambda n: np.asarray(inputs[n])
    m = {}
    m["x"] = np.ascontiguousarray(g("L0_attn_a_x")[core])
    m["wqkv_a"] = g("L0_attn_a_w_qkv")
    m["wout_a"] = g("L0_attn_a_w_out")
    m["head_scales"] = g("L0_attn_a_head_scales").reshape(NH, 1)
    hd = np.zeros((D, NH), dtype=np.float32)
    hdirs = g("L0_attn_a_head_dirs")
    for h in range(NH):
        hd[h * 64:h * 64 + 8, h] = hdirs[h]
    m["hdblk"] = hd
    m["wqkv_b"] = g("L0_attn_b_qkv_w")
    m["bqkv_b"] = g("L0_attn_b_qkv_b").reshape(1, 3 * D)
    m["wrq"] = g("L0_attn_b_rq_w")
    m["brq"] = g("L0_attn_b_rq_b").reshape(1, LATD)
    m["wrkv"] = g("L0_attn_b_rkv_w")
    m["brkv"] = g("L0_attn_b_rkv_b").reshape(1, 2 * LATD)
    m["wrout"] = g("L0_attn_b_rout_w")
    m["wroutT"] = np.ascontiguousarray(g("L0_attn_b_rout_w").T)
    m["brout"] = g("L0_attn_b_rout_b").reshape(1, D)
    m["wout_b"] = g("L0_attn_b_out_w")
    m["bout_b"] = g("L0_attn_b_out_b").reshape(1, D)
    glb = np.zeros((D, NH), dtype=np.float32)
    grb = np.zeros((D, NH), dtype=np.float32)
    for h in range(NH):
        glb[h * 64:(h + 1) * 64, h] = g("L0_attn_b_gl_w")[:, 0]
        grb[h * 64:(h + 1) * 64, h] = g("L0_attn_b_gr_w")[:, 0]
    m["glblk"] = glb
    m["grblk"] = grb
    w1 = np.concatenate([g("L0_ffn_w1"), g("L0_ffn_new_w1")], axis=0)
    w2 = np.concatenate([g("L0_ffn_w2"), g("L0_ffn_new_w2")], axis=0)
    w3 = np.concatenate([g("L0_ffn_w3"), g("L0_ffn_new_w3")], axis=0)
    m["w1T"] = np.ascontiguousarray(w1.transpose(0, 2, 1))
    m["w3T"] = np.ascontiguousarray(w3.transpose(0, 2, 1))
    m["w2T"] = np.ascontiguousarray(w2.transpose(0, 2, 1))
    gate = np.concatenate([g("L0_ffn_gate_weight"), g("L0_ffn_new_gate_weight")], axis=0)
    m["gateT"] = np.ascontiguousarray(gate.T)
    gb = np.zeros((1, NE), dtype=np.float32)
    gb[0, 8] = g("L0_ffn_gate_bias")[0]
    m["gbias"] = gb
    m.update(host_consts(T))
    return m


def build_program(T, phases=("a", "f1", "b", "f2"), debug_outs=()):
    import contextlib
    P = Prog(T, debug_outs=debug_outs)
    P.make_dram()
    P.make_persist()
    P.setup_phase()
    d = P.d

    def ffn_block(tagp, Rcur, Rnext):
        with contextlib.ExitStack() as c3:
            s_xlnT = c3.enter_context(P.nc.sbuf_tensor(f"{tagp}_xlnT", [128, 8 * T], BF16))
            P.ln_pass(f"{tagp}l", Rcur, s_xlnT, ctx=c3)
            P.proj_spill(f"{tagp}p", s_xlnT, do_route=True, xfm_spill=d["xfm"], ctx=c3)
        with contextlib.ExitStack() as c4:
            P.ffn_experts(f"{tagp}e", Rcur, Rnext, ctx=c4)

    def attnb_block(Rcur, Rnext):
        with contextlib.ExitStack() as cb:
            s_latout = cb.enter_context(P.nc.sbuf_tensor("b_latout", [128, 2 * T], BF16))
            with contextlib.ExitStack() as cmid:
                s_rqT = cmid.enter_context(P.nc.sbuf_tensor("b_rqT", [128, 2 * T], BF16))
                s_pool = cmid.enter_context(P.nc.sbuf_tensor("b_pool", [64, D], BF16))
                with contextlib.ExitStack() as c5:
                    s_xlnT = c5.enter_context(P.nc.sbuf_tensor("b_xlnT", [128, 8 * T], BF16))
                    P.ln_pass("b1l", Rcur, s_xlnT, pooled_out=s_pool, ctx=c5)
                    P.proj_spill("b1p", s_xlnT, W_in=d["wqkv_b"], COLS=3 * D,
                                 bias_in=d["bqkv_b"], fcmodes=("q", "q", "k", "k", "v", "v"),
                                 Qdst=d["Qh"], Kdst=d["Kh"], Vdst=d["Vh"], ctx=c5)
                    P.proj_spill("b1r", s_xlnT, rq=(d["wrq"], d["brq"], s_rqT), ctx=c5)
                with contextlib.ExitStack() as c6:
                    P.latent_attn("bl", s_rqT, s_pool, s_latout, ctx=c6)
            with contextlib.ExitStack() as cloc:
                s_local = cloc.enter_context(P.nc.sbuf_tensor("b_local", [128, 8 * T], BF16))
                s_g1 = cloc.enter_context(P.nc.sbuf_tensor("b_g1", [NH, T], F32))
                with contextlib.ExitStack() as c7:
                    P.local_attn("bo", s_local, s_g1, ctx=c7)
                if P.dbg:
                    dump = []
                    if "d_latout" in P.dbg:
                        dump.append((P.dbg["d_latout"], s_latout, 2))
                    if "d_local" in P.dbg:
                        dump.append((P.dbg["d_local"], s_local, 8))
                    if dump or "d_g1" in P.dbg or "d_pool" in P.dbg:
                        Sd = Sems(P, "bdbg", ["x"])
                        pv = P.prev
                        with P.nc.Block("b_bdbg") as blk:
                            @blk.sync
                            def _(s):
                                wait_finals(s, pv)
                                for dst, srct, ncch in dump:
                                    for c in range(ncch):
                                        Sd.inc(s.dma_start(out=dst[c * 128:(c + 1) * 128, :],
                                                           in_=srct.ap()[:, c * T:(c + 1) * T]),
                                               "x", 16)
                                if "d_g1" in P.dbg:
                                    Sd.inc(s.dma_start(out=P.dbg["d_g1"], in_=s_g1.ap()),
                                           "x", 16)
                                Sd.wait(s, "x", Sd.c["x"])
                        pvf = P.prev.copy()
                        pvf.update(Sd.finals())
                        P.prev = pvf
                with contextlib.ExitStack() as c8:
                    P.mix_proj("b3", s_latout, s_local, s_g1, Rcur, Rnext, ctx=c8)

    with contextlib.ExitStack() as ctx:
        if "a" in phases or "a1" in phases:
            with contextlib.ExitStack() as c1:
                s_xlnT = c1.enter_context(P.nc.sbuf_tensor("a_xlnT", [128, 8 * T], BF16))
                P.ln_pass("a1l", d["x"], s_xlnT, ctx=c1)
                P.proj_spill("a1p", s_xlnT, W_in=d["wqkv_a"], COLS=3 * D, bias_in=None,
                             fcmodes=("q", "q", "k", "k", "v", "v"),
                             Qdst=d["Qh"], Kdst=d["Kh"], Vdst=d["Vh"], do_pp=True,
                             ctx=c1, dbg_xlnT=("d_xlnT" in P.dbg))
        if "a3x" in phases:
            with contextlib.ExitStack() as c2b:
                s_ao = c2b.enter_context(P.nc.sbuf_tensor("a_ao", [128, 8 * T], BF16))
                Si = Sems(P, "ini", ["x"])
                pv = P.prev
                with P.nc.Block("b_ini") as blk:
                    @blk.vector
                    def _(v):
                        wait_finals(v, pv)
                        Si.inc(v.memset(s_ao.ap(), 0.25), "x", 1)
                pp = P.prev.copy()
                pp.update(Si.finals())
                P.prev = pp
                P.proj_residual("a3", s_ao, d["wout_a"], None, d["x"], d["Ra"], ctx=c2b)
        if "a" in phases or "a2" in phases:
            with contextlib.ExitStack() as c2:
                s_ao = c2.enter_context(P.nc.sbuf_tensor("a_ao", [128, 8 * T], BF16))
                P.attn_a_heads(s_ao, ctx=c2)
                if "a" in phases or "a3" in phases:
                    P.proj_residual("a3", s_ao, d["wout_a"], None, d["x"], d["Ra"], ctx=c2,
                                    extra_dump=P.dbg.get("d_Ra"))
        if "f1r" in phases:
            with contextlib.ExitStack() as c3:
                s_xlnT = c3.enter_context(P.nc.sbuf_tensor("f1_xlnT", [128, 8 * T], BF16))
                P.ln_pass("f1l", d["Ra"], s_xlnT, ctx=c3)
                P.proj_spill("f1p", s_xlnT, do_route=True, xfm_spill=d["xfm"], ctx=c3)
        if "f2o" in phases:
            with contextlib.ExitStack() as c4:
                P.ffn_experts("f2x", d["Ra"], d["Rb"], ctx=c4)
        if "f1" in phases:
            ffn_block("f1", d["Ra"], d["Rb"])
            if "d_cw" in P.dbg:
                S = Sems(P, "fdbg", ["x"])
                pv = P.prev
                with P.nc.Block("b_fdbg") as blk:
                    @blk.sync
                    def _(s):
                        wait_finals(s, pv)
                        S.inc(s.dma_start(out=P.dbg["d_cw"], in_=P.s_cw[:, 0:P.nt * NE]), "x", 16)
                        S.wait(s, "x", 16)
                pvf = P.prev.copy()
                pvf.update(S.finals())
                P.prev = pvf
        if "b" in phases:
            Rcur = d["Rb"] if "f1" in phases else d["Ra"]
            attnb_block(Rcur, d["Ra"] if "f1" in phases else d["Rb"])
        if "f2" in phases:
            ffn_block("f2", d["Ra"], d["out"])
    P.top.close()
    return P


def kernel(**inputs):
    T = 4096
    P = build_program(T, phases=("a", "f1", "b", "f2"))
    in_maps = [host_inputs(inputs, c, T) for c in range(8)]
    res = run_bass_kernel_spmd(P.nc, in_maps, core_ids=list(range(8)))
    return np.stack([np.asarray(r["out"]) for r in res.results], axis=0)


def _ffn_experts(self, tag, Rcur, Rnext, ctx=None):
    nc, nt, T, ng = self.nc, self.nt, self.T, self.ng
    d = self.d
    S = Sems(self, tag, ["wl", "wl2", "hm", "sl", "ml", "ym", "yv", "e1"],
             sw=("wl", "wl2"))
    RX = Ring(self, f"{tag}rx", 2, base_idx=16)
    RA = Ring(self, f"{tag}ra", 4, base_idx=18)
    RST = Ring(self, f"{tag}rs", 4, base_idx=22)
    prev = self.prev
    s_w1 = ctx.enter_context(nc.sbuf_tensor(f"{tag}_w1", [128, 8 * HID], BF16))
    s_w3 = ctx.enter_context(nc.sbuf_tensor(f"{tag}_w3", [128, 8 * HID], BF16))
    s_w2 = ctx.enter_context(nc.sbuf_tensor(f"{tag}_w2", [128, 22 * D], BF16))
    s_x = ctx.enter_context(nc.sbuf_tensor(f"{tag}_x", [128, 2 * 8 * 512], BF16))
    s_h = ctx.enter_context(nc.sbuf_tensor(f"{tag}_h", [128, 11 * 512], BF16))
    s_sl = ctx.enter_context(nc.sbuf_tensor(f"{tag}_sl", [128, 2 * 512], F32))

    def hm_end(e):
        return (e + 1) * ng * 44

    def ym_end(e):
        return (e + 1) * ng * 8

    with nc.Block(f"b_{tag}") as blk:
        @blk.gpsimd
        def _(g):
            wait_finals(g, prev)
            g.wait_ge(*self.wcast)
            for e in range(NE):
                if e >= 1:
                    S.wait(g, "hm", hm_end(e - 1))
                S.inc(g.dma_start(out=s_w1.ap().rearrange("p (c n) -> p c n", c=8),
                                  in_=rr8(d["w1b"][e])), "wl", 16)
                S.inc(g.dma_start(out=s_w3.ap().rearrange("p (c n) -> p c n", c=8),
                                  in_=rr8(d["w3b"][e])), "wl", 16)
                if e >= 1:
                    S.wait(g, "ym", ym_end(e - 1))
                S.inc(g.dma_start(out=s_w2.ap().rearrange("p (c n) -> p c n", c=22),
                                  in_=rr8(d["w2b"][e], 22)), "wl2", 16)

        @blk.sync
        def _(s):
            wait_finals(s, prev)
            def stv(e, i):
                return e * 16 * ng + (i // 4) * 16 + 8 + 2 * ((i % 4) + 1)

            def emit_store(s, e, i):
                it = e * nt + i
                S.wait(s, "yv", stv(e, i))
                RST.inc(s.dma_start(out=Rnext[i * 128:(i + 1) * 128, :],
                                    in_=self.s_acc[:, (i % 4) * D:(i % 4 + 1) * D]), it)

            for e in range(NE):
                rsrc = Rcur if e == 0 else Rnext
                for gi in range(ng):
                    it = e * ng + gi
                    if it >= 2:
                        S.wait(s, "hm", 44 * (it - 1))
                    RX.inc(s.dma_start(
                        out=s_x.ap().rearrange("p (b c n) -> p b c n", b=2, c=8)[:, it % 2],
                        in_=d["xfm"][:, gi * 512:(gi + 1) * 512].rearrange("(c p) n -> p c n", p=128)), it)
                    for ii in range(4):
                        i = 4 * gi + ii
                        if i >= 4:
                            emit_store(s, e, i - 4)
                            RST.wait_done(s, e * nt + i - 4)
                        elif e > 0:
                            emit_store(s, e - 1, nt - 4 + ii)
                            RST.wait_done(s, (e - 1) * nt + nt - 4 + ii)
                        if e > 0:
                            RST.wait_done(s, (e - 1) * nt + i)
                        RA.inc(s.dma_start(out=self.s_acc[:, (i % 4) * D:(i % 4 + 1) * D],
                                           in_=rsrc[i * 128:(i + 1) * 128, :]), e * nt + i)
            for i in range(nt - 4, nt):
                emit_store(s, NE - 1, i)
            for it in range(NE * nt - 4, NE * nt):
                RST.wait_done(s, it)

        @blk.tensor
        def _(t):
            wait_finals(t, prev)
            jj_u = 0
            y_u = 0
            for e in range(NE):
                S.wait(t, "wl", 32 * (e + 1))
                w2_waited = False
                for gi in range(ng):
                    it = e * ng + gi
                    RX.wait_done(t, it)
                    xs = s_x.ap().rearrange("p (b c n) -> p b c n", b=2, c=8)[:, it % 2]
                    for jh in range(2):
                        for jj in range(11):
                            j = jh * 11 + jj
                            bA = self.pb[2 * (jj_u % 2)].ap()
                            bB = self.pb[2 * (jj_u % 2) + 1].ap()
                            if jj_u >= 2:
                                S.wait(t, "ml", jj_u - 1)
                            last = None
                            for c in range(8):
                                last = t.matmul(bA, s_w1[:, c * HID + j * 128:c * HID + (j + 1) * 128],
                                                xs[:, c, :], start=(c == 0), stop=(c == 7))
                            S.inc(last, "hm", 1)
                            for c in range(8):
                                last = t.matmul(bB, s_w3[:, c * HID + j * 128:c * HID + (j + 1) * 128],
                                                xs[:, c, :], start=(c == 0), stop=(c == 7))
                            S.inc(last, "hm", 1)
                            jj_u += 1
                        if not w2_waited:
                            S.wait(t, "wl2", 16 * (e + 1))
                            w2_waited = True
                        S.wait(t, "ml", jj_u)
                        for ii in range(4):
                            bY0 = self.pb[4 + 2 * (ii % 2)].ap()
                            bY1 = self.pb[5 + 2 * (ii % 2)].ap()
                            if y_u >= 2:
                                S.wait(t, "yv", 2 * (y_u - 1))
                            for jj in range(11):
                                j = jh * 11 + jj
                                lh = s_h[:, jj * 512 + ii * 128:jj * 512 + (ii + 1) * 128]
                                t.matmul(bY0, lh, s_w2[:, j * D:j * D + 512],
                                         start=(jj == 0), stop=(jj == 10))
                                last = t.matmul(bY1, lh, s_w2[:, j * D + 512:(j + 1) * D],
                                                start=(jj == 0), stop=(jj == 10))
                            S.inc(last, "ym", 1)
                            y_u += 1

        @blk.scalar
        def _(sc):
            wait_finals(sc, prev)
            jj_u = 0
            for e in range(NE):
                for gi in range(ng):
                    for jh in range(2):
                        for jj in range(11):
                            S.wait(sc, "hm", 2 * jj_u + 1)
                            if jj_u >= 2:
                                S.wait(sc, "ml", jj_u - 1)
                            bA = self.pb[2 * (jj_u % 2)].ap()
                            S.inc(sc.activation(s_sl[:, (jj_u % 2) * 512:(jj_u % 2 + 1) * 512],
                                                bA, AF.Silu), "sl", 1)
                            jj_u += 1

        @blk.vector
        def _(v):
            wait_finals(v, prev)
            jj_u = 0
            y_u = 0
            for e in range(NE):
                for gi in range(ng):
                    for jh in range(2):
                        for jj in range(11):
                            if jj == 0:
                                S.wait(v, "ym", y_u)
                            S.wait(v, "sl", jj_u + 1)
                            S.wait(v, "hm", 2 * jj_u + 2)
                            bB = self.pb[2 * (jj_u % 2) + 1].ap()
                            S.inc(v.tensor_tensor(s_h[:, jj * 512:(jj + 1) * 512],
                                                  s_sl[:, (jj_u % 2) * 512:(jj_u % 2 + 1) * 512],
                                                  bB, ALU.mult), "ml", 1)
                            jj_u += 1
                        for ii in range(4):
                            i = gi * 4 + ii
                            S.wait(v, "ym", y_u + 1)
                            RA.wait_done(v, e * nt + i)
                            bY0 = self.pb[4 + 2 * (ii % 2)].ap()
                            bY1 = self.pb[5 + 2 * (ii % 2)].ap()
                            cwc = self.s_cw[:, i * NE + e:i * NE + e + 1]
                            accl = self.s_acc[:, (i % 4) * D:(i % 4) * D + 512]
                            acch = self.s_acc[:, (i % 4) * D + 512:(i % 4 + 1) * D]
                            iv = v.scalar_tensor_tensor(accl, bY0, cwc, accl, ALU.mult, ALU.add)
                            S.inc(iv, "yv", 1)
                            S.wait(v, "yv", S.c["yv"])
                            iv = v.scalar_tensor_tensor(acch, bY1, cwc, acch, ALU.mult, ALU.add)
                            S.inc(iv, "yv", 1)
                            S.wait(v, "yv", S.c["yv"])
                            y_u += 1
    f = S.finals()
    for r in (RX, RA, RST):
        f.update(r.finals())
    self.prev = f


Prog.ffn_experts = _ffn_experts


def _latent_attn(self, tag, s_rqT, s_pool, s_latout, ctx=None):
    nc, nt, T = self.nc, self.nt, self.T
    d = self.d
    LPT = 128 // self.CH if self.CH <= 128 else 0
    S = Sems(self, tag, ["wl", "st", "se", "rk", "pa", "rm", "sx", "sro", "sri",
                         "pbm", "mk", "ex", "pv", "nz", "lx", "e1", "ms", "qa"],
             sw=("wl", "sri"))
    prev = self.prev
    s_rkvw = ctx.enter_context(nc.sbuf_tensor(f"{tag}_rkvw", [128, 8 * 512], BF16))
    s_rkvb = ctx.enter_context(nc.sbuf_tensor(f"{tag}_rkvb", [1, 512], BF16))
    s_poolT = ctx.enter_context(nc.sbuf_tensor(f"{tag}_poolT", [128, 512], BF16))
    s_rkv = ctx.enter_context(nc.sbuf_tensor(f"{tag}_rkv", [64, 512], BF16))
    s_rkA = ctx.enter_context(nc.sbuf_tensor(f"{tag}_rkA", [65, 4 * 64], BF16))
    s_rkP = ctx.enter_context(nc.sbuf_tensor(f"{tag}_rkP", [128, 4 * 64], BF16))
    s_rva = ctx.enter_context(nc.sbuf_tensor(f"{tag}_rva", [64, 4 * 65], BF16))
    s_rqa = [ctx.enter_context(nc.sbuf_tensor(f"{tag}_rqa{p}", [65, T], BF16)) for p in range(2)]
    s_cmpT = ctx.enter_context(nc.sbuf_tensor(f"{tag}_cmpT", [64, 128], F32))
    s_lm = ctx.enter_context(nc.sbuf_tensor(f"{tag}_lm", [128, 64], F32))
    s_cmp = ctx.enter_context(nc.sbuf_tensor(f"{tag}_cmp", [128, 64], F32))
    s_pt = ctx.enter_context(nc.sbuf_tensor(f"{tag}_pt", [64, 2 * 128], BF16))

    with nc.Block(f"b_{tag}") as blk:
        @blk.gpsimd
        def _(g):
            wait_finals(g, prev)
            S.inc(g.dma_start(out=s_rkvw.ap().rearrange("p (c n) -> p c n", c=8),
                              in_=rr8(d["wrkv"])), "wl", 16)
            S.inc(g.dma_start(out=s_rkvb[:, :], in_=d["brkv"]), "wl", 16)
            for lh in range(NLH):
                S.wait(g, "sro", 16 * (lh + 1))
                if lh >= 2:
                    S.wait(g, "pbm", (lh - 1) * nt)
                S.inc(g.dma_start(out=s_rqa[lh % 2].ap()[64:65, :],
                                  in_=d["stats"].rearrange("a b -> (a b)").unsqueeze(0)),
                      "sri", 16)

        @blk.sync
        def _(s):
            wait_finals(s, prev)
            for lh in range(NLH):
                S.wait(s, "sx", 3 * lh + 3)
                S.wait(s, "sri", 16 * lh)
                S.inc(s.dma_start(out=d["stats"], in_=self.s_row[0:nt, :]), "sro", 16)

        @blk.tensor
        def _(t):
            wait_finals(t, prev)
            S.wait(t, "ms", 1)
            b6 = self.pb[6].ap().bitcast(BF16)
            for c in range(8):
                iv = t.transpose(b6[:, c * 64:(c + 1) * 64],
                                 s_pool.ap()[0:64, c * 128:(c + 1) * 128],
                                 self.s_id_bf[0:64, 0:64])
            S.inc(iv, "st", 1)
            S.wait(t, "wl", 32)
            S.wait(t, "se", 1)
            last = None
            for c in range(8):
                last = t.matmul(self.pb[0].ap()[0:64, :], s_poolT[:, c * 64:(c + 1) * 64],
                                s_rkvw[:, c * 512:(c + 1) * 512], start=(c == 0), stop=False)
            last = t.matmul(self.pb[0].ap()[0:64, :], self.s_ones_bf[:, 0:64],
                            s_rkvb[:, :], start=False, stop=True)
            S.inc(last, "rk", 1)
            S.wait(t, "se", 2)
            for lh in range(NLH):
                iv = t.transpose(b6[0:64, 512 + lh * 64:512 + (lh + 1) * 64],
                                 s_rkv.ap()[0:64, lh * 64:(lh + 1) * 64],
                                 self.s_id_bf[0:64, 0:64])
            S.inc(iv, "st", 1)
            S.wait(t, "se", 5)
            S.wait(t, "ms", 1)
            au = 0
            for i in range(nt):
                for lh in range(NLH):
                    if au >= 2:
                        S.wait(t, "rm", au - 1)
                    bank = self.pb[au % 2].ap()[:, 0:64]
                    S.inc(t.matmul(bank,
                                   s_rqT.ap()[(lh % 2) * 64:(lh % 2) * 64 + 64,
                                              (lh // 2) * T + i * 128:(lh // 2) * T + (i + 1) * 128],
                                   s_rkP.ap()[(lh % 2) * 64:(lh % 2) * 64 + 64,
                                              lh * 64:(lh + 1) * 64],
                                   start=True, stop=True), "pa", 1)
                    au += 1
            bu = 0
            for lh in range(NLH):
                S.wait(t, "sx", 3 * lh + 1)
                S.inc(t.transpose(self.pb[6].ap()[0:nt, 0:128], self.s_mneg[:, 0:nt],
                                  self.s_id_f32[:, :]), "sx", 1)
                S.wait(t, "sri", 16 * (lh + 1))
                S.wait(t, "qa", lh + 1)
                for i in range(nt):
                    if bu >= 2:
                        S.wait(t, "ex", bu - 1)
                    bank = self.pb[2 + bu % 2].ap()
                    S.inc(t.matmul(bank[0:64, 0:128],
                                   s_rkA.ap()[0:65, lh * 64:(lh + 1) * 64],
                                   s_rqa[lh % 2].ap()[0:65, i * 128:(i + 1) * 128],
                                   start=True, stop=True), "pbm", 1)
                    S.wait(t, "ex", bu + 1)
                    if bu >= 1:
                        S.wait(t, "nz", 3 * bu)
                    S.inc(t.matmul(self.pb[4].ap()[0:65, 0:128],
                                   s_rva.ap()[0:64, lh * 65:lh * 65 + 65],
                                   s_pt[0:64, (bu % 2) * 128:(bu % 2 + 1) * 128],
                                   start=True, stop=True), "pv", 1)
                    S.wait(t, "nz", 3 * bu + 1)
                    S.inc(t.matmul(self.pb[5].ap()[0:64, 0:128], self.s_ones_f32[0:1, 0:64],
                                   self.s_linv[0:1, 0:128], start=True, stop=True), "lx", 1)
                    bu += 1

        @blk.vector
        def _(v):
            wait_finals(v, prev)
            iv = v.memset(s_rkA.ap()[64:65, :], 1.0)
            iv = v.memset(s_rva.ap().rearrange("p (l c) -> p l c", c=65)[:, :, 64:65], 1.0)
            S.inc(iv, "ms", 1)
            S.wait(v, "st", 1)
            S.inc(v.tensor_copy(s_poolT[:, :], self.pb[6].ap().bitcast(BF16)[:, 0:512]),
                  "se", 1)
            S.wait(v, "rk", 1)
            S.inc(v.tensor_copy(s_rkv[:, :], self.pb[0].ap()[0:64, :]), "se", 1)
            S.wait(v, "st", 2)
            S.inc(v.tensor_copy(s_rkA.ap()[0:64, :],
                                self.pb[6].ap().bitcast(BF16)[0:64, 512:768]), "se", 1)
            for lh in range(NLH):
                iv = v.tensor_copy(s_rkP.ap()[(lh % 2) * 64:(lh % 2) * 64 + 64,
                                              lh * 64:(lh + 1) * 64],
                                   self.pb[6].ap().bitcast(BF16)[0:64, 512 + lh * 64:512 + (lh + 1) * 64])
            S.inc(iv, "se", 1)
            for lh in range(NLH):
                iv = v.tensor_copy(s_rva.ap()[0:64, lh * 65:lh * 65 + 64],
                                   s_rkv[0:64, 256 + lh * 64:256 + (lh + 1) * 64])
            S.inc(iv, "se", 1)
            au = 0
            for i in range(nt):
                S.wait(v, "rm", max(0, au - 1))
                S.edge(v, "e1", v.tensor_single_scalar(s_cmp[:, :], self.s_latC[:, :],
                                                       i * LPT - 0.5, ALU.is_ge))
                for lh in range(NLH):
                    S.wait(v, "pa", au + 1)
                    bank = self.pb[au % 2].ap()[:, 0:64]
                    S.edge(v, "e1", v.scalar_tensor_tensor(s_lm[:, :], s_cmp[:, :], NEG,
                                                           bank, ALU.mult, ALU.add))
                    iv = v.tensor_reduce(self.s_stat[:, lh * nt + i:lh * nt + i + 1],
                                         s_lm[:, :], AX.X, ALU.max)
                    S.inc(iv, "rm", 1)
                    S.wait(v, "rm", S.c["rm"])
                    au += 1
            bu = 0
            for lh in range(NLH):
                if lh >= 1:
                    S.wait(v, "sx", 3 * (lh - 1) + 2)
                S.edge(v, "e1", v.tensor_scalar(self.s_mneg[:, 0:nt],
                                                self.s_stat[:, lh * nt:(lh + 1) * nt],
                                                -1e8, -1.0, ALU.max, ALU.mult))
                S.inc(v.nop(), "sx", 1)
                S.wait(v, "sx", 3 * lh + 2)
                S.wait(v, "sro", 16 * lh)
                S.inc(v.tensor_copy(self.s_row[0:nt, :], self.pb[6].ap()[0:nt, 0:128]),
                      "sx", 1)
                if lh >= 2:
                    S.wait(v, "pbm", (lh - 1) * nt)
                S.inc(v.tensor_copy(s_rqa[lh % 2].ap()[0:64, :],
                                    s_rqT.ap()[(lh % 2) * 64:(lh % 2) * 64 + 64,
                                               (lh // 2) * T:(lh // 2) * T + T]), "qa", 1)
                for i in range(nt):
                    S.wait(v, "pbm", bu + 1)
                    S.edge(v, "e1", v.tensor_single_scalar(s_cmpT[:, :], self.s_latCT[:, :],
                                                           i * LPT - 0.5, ALU.is_ge))
                    bank = self.pb[2 + bu % 2].ap()
                    S.inc(v.scalar_tensor_tensor(bank[0:64, 0:128], s_cmpT[:, :], NEG,
                                                 bank[0:64, 0:128], ALU.mult, ALU.add),
                          "mk", 1)
                    S.wait(v, "mk", S.c["mk"])
                    S.wait(v, "pv", bu + 1)
                    S.edge(v, "e1", v.tensor_scalar(self.s_linv[0:1, 0:128],
                                                    self.pb[4].ap()[64:65, 0:128],
                                                    1e-30, None, ALU.max))
                    S.inc(v.reciprocal(self.s_linv[0:1, 0:128], self.s_linv[0:1, 0:128]),
                          "nz", 1)
                    S.wait(v, "lx", bu + 1)
                    iv = v.tensor_copy(self.s_lexp[0:64, 0:128], self.pb[5].ap()[0:64, 0:128])
                    S.inc(iv, "nz", 1)
                    S.wait(v, "nz", S.c["nz"])
                    dst = s_latout.ap()[(lh % 2) * 64:(lh % 2) * 64 + 64,
                                        (lh // 2) * T + i * 128:(lh // 2) * T + (i + 1) * 128]
                    S.inc(v.tensor_tensor(dst, self.pb[4].ap()[0:64, 0:128],
                                          self.s_lexp[0:64, 0:128], ALU.mult), "nz", 1)
                    bu += 1
            S.wait(v, "nz", S.c["nz"])
            S.inc(v.memset(s_latout.ap().rearrange("p (c t) -> p c t", c=2)[:, :, 0:self.CH],
                           0.0), "nz", 1)

        @blk.scalar
        def _(sc):
            wait_finals(sc, prev)
            bu = 0
            for lh in range(NLH):
                for i in range(nt):
                    S.wait(sc, "pbm", bu + 1)
                    S.wait(sc, "mk", bu + 1)
                    if bu >= 2:
                        S.wait(sc, "pv", bu - 1)
                    bank = self.pb[2 + bu % 2].ap()
                    S.inc(sc.activation(s_pt[0:64, (bu % 2) * 128:(bu % 2 + 1) * 128],
                                        bank[0:64, 0:128], AF.Exp), "ex", 1)
                    bu += 1

    f = S.finals()
    self.prev = f


Prog.latent_attn = _latent_attn


def _local_attn(self, tag, s_local, s_g1, ctx=None):
    nc, nt, T = self.nc, self.nt, self.T
    d = self.d
    S = Sems(self, tag, ["pa", "rm", "sx", "sro", "sri", "pbm", "mk", "ex", "pv",
                         "nz", "lx", "ms", "e1", "g1m", "g1v"], sw=("sri",))
    RK = Ring(self, f"{tag}kld", 2, base_idx=16)
    prev = self.prev
    s_K = [ctx.enter_context(nc.sbuf_tensor(f"{tag}_K{p}", [65, T], BF16)) for p in range(2)]
    s_Q = [ctx.enter_context(nc.sbuf_tensor(f"{tag}_Q{p}", [65, T], BF16)) for p in range(2)]
    s_V = [ctx.enter_context(nc.sbuf_tensor(f"{tag}_V{p}", [128, nt * 65], BF16)) for p in range(2)]
    s_pt = ctx.enter_context(nc.sbuf_tensor(f"{tag}_pt", [128, 2 * 128], BF16))
    s_mskl = ctx.enter_context(nc.sbuf_tensor(f"{tag}_mskl", [128, 256], F32))

    nbu_h = 2 * nt - 1
    pbm_end = [nbu_h * (h + 1) for h in range(NH)]
    pv_end = [nbu_h * (h + 1) for h in range(NH)]

    def bui(h, t, j):
        return h * nbu_h + (0 if t == 0 else 2 * t - 1) + (j if t > 0 else 0)

    with nc.Block(f"b_{tag}") as blk:
        @blk.sync
        def _(s):
            wait_finals(s, prev)
            for p in range(2):
                S.inc(s.dma_start(out=s_K[p].ap()[64:65, :], in_=d["onesT"]), "ms", 16)
            for h in range(NH):
                par = h % 2
                if h >= 2:
                    S.wait(s, "pbm", pbm_end[h - 2])
                    S.wait(s, "pv", pv_end[h - 2])
                RK.inc(s.dma_start(out=s_K[par].ap()[0:64, :],
                                   in_=d["Kh"][h * 64:(h + 1) * 64, :]), h)
                RK.inc(s.dma_start(out=s_Q[par].ap()[0:64, :],
                                   in_=d["Qh"][h * 64:(h + 1) * 64, :]), h)
                RK.inc(s.dma_start(out=s_V[par].ap().rearrange("p (t c) -> p t c", c=65)[:, :, 0:64],
                                   in_=d["Vh"][:, h * 64:(h + 1) * 64].rearrange("(t p) d -> p t d", p=128)),
                       h)
                S.wait(s, "sx", 3 * h + 3)
                S.wait(s, "sri", 16 * h)
                S.inc(s.dma_start(out=d["stats"], in_=self.s_row[0:nt, :]), "sro", 16)

        @blk.gpsimd
        def _(g):
            wait_finals(g, prev)
            for h in range(NH):
                par = h % 2
                S.wait(g, "sro", 16 * (h + 1))
                if h >= 2:
                    S.wait(g, "pbm", pbm_end[h - 2])
                S.inc(g.dma_start(out=s_Q[par].ap()[64:65, :],
                                  in_=d["stats"].rearrange("a b -> (a b)").unsqueeze(0)),
                      "sri", 16)

        @blk.tensor
        def _(t):
            wait_finals(t, prev)
            S.wait(t, "ms", 33)
            au = 0
            bu = 0

            def pe_passA(h):
                nonlocal au
                par = h % 2
                RK.wait_done(t, h)
                for i in range(nt):
                    if au >= 2:
                        S.wait(t, "rm", au - 1)
                    bank = self.pb[au % 2].ap()
                    k0 = max(0, (i - 1) * 128)
                    N = 256 if i > 0 else 128
                    S.inc(t.matmul(bank[:, 0:N], s_Q[par].ap()[0:64, i * 128:(i + 1) * 128],
                                   s_K[par].ap()[0:64, k0:k0 + N],
                                   start=True, stop=True), "pa", 1)
                    au += 1
                S.wait(t, "sx", 3 * h + 1)
                S.inc(t.transpose(self.pb[6].ap()[0:nt, 0:128], self.s_mneg[:, 0:nt],
                                  self.s_id_f32[:, :]), "sx", 1)

            def pe_passB(h):
                nonlocal bu
                par = h % 2
                S.wait(t, "sri", 16 * (h + 1))
                for i in range(nt):
                    kts = (i,) if i == 0 else (i - 1, i)
                    for j, kt in enumerate(kts):
                        if bu >= 2:
                            S.wait(t, "ex", bu - 1)
                        bank = self.pb[2 + bu % 2].ap()
                        S.inc(t.matmul(bank[:, 0:128],
                                       s_K[par].ap()[0:65, kt * 128:(kt + 1) * 128],
                                       s_Q[par].ap()[0:65, i * 128:(i + 1) * 128],
                                       start=True, stop=True), "pbm", 1)
                        S.wait(t, "ex", bu + 1)
                        if j == 0 or i == 0:
                            S.wait(t, "nz", 3 * (h * nt + i))
                        S.inc(t.matmul(self.pb[4].ap()[0:65, 0:128],
                                       s_V[par].ap()[:, kt * 65:kt * 65 + 65],
                                       s_pt[:, (bu % 2) * 128:(bu % 2 + 1) * 128],
                                       start=(j == 0), stop=(j == len(kts) - 1)), "pv", 1)
                        bu += 1
                    S.wait(t, "nz", 3 * (h * nt + i) + 1)
                    S.inc(t.matmul(self.pb[5].ap()[0:64, 0:128], self.s_ones_f32[0:1, 0:64],
                                   self.s_linv[0:1, 0:128], start=True, stop=True), "lx", 1)

            for h in range(NH):
                pe_passA(h)
                if h >= 1:
                    pe_passB(h - 1)
            pe_passB(NH - 1)
            S.wait(t, "nz", 3 * NH * nt + 1)
            for ts in range(T // 512):
                if ts >= 2:
                    S.wait(t, "g1v", ts - 1)
                bank = self.pb[0 + ts % 2].ap()[0:NH, :]
                for c in range(8):
                    iv = t.matmul(bank, self.s_gl[:, c * NH:(c + 1) * NH],
                                  s_local.ap()[:, c * T + ts * 512:c * T + ts * 512 + 512],
                                  start=(c == 0), stop=(c == 7))
                S.inc(iv, "g1m", 1)

        @blk.vector
        def _(v):
            wait_finals(v, prev)
            for p in range(2):
                iv = v.memset(s_V[p].ap().rearrange("p (t c) -> p t c", c=65)[:, :, 64:65], 1.0)
            S.inc(iv, "ms", 1)
            au = 0

            def dve_passA(h):
                nonlocal au
                for i in range(nt):
                    S.wait(v, "pa", au + 1)
                    bank = self.pb[au % 2].ap()
                    N = 256 if i > 0 else 128
                    msk = self.s_maskL[:, 0:256] if i > 0 else self.s_maskL[:, 128:256]
                    S.edge(v, "e1", v.tensor_tensor(s_mskl[:, 0:N], bank[:, 0:N], msk,
                                                    ALU.add))
                    iv = v.tensor_reduce(self.s_m[:, i:i + 1], s_mskl[:, 0:N], AX.X, ALU.max)
                    S.inc(iv, "rm", 1)
                    S.wait(v, "rm", S.c["rm"])
                    au += 1
                if h >= 1:
                    S.wait(v, "sx", 3 * (h - 1) + 2)
                S.inc(v.tensor_scalar(self.s_mneg[:, 0:nt], self.s_m[:, 0:nt],
                                      -1e8, -1.0, ALU.max, ALU.mult), "sx", 1)
                S.wait(v, "sx", 3 * h + 2)
                S.wait(v, "sro", 16 * h)
                S.inc(v.tensor_copy(self.s_row[0:nt, :], self.pb[6].ap()[0:nt, 0:128]),
                      "sx", 1)

            def dve_passB(h):
                for i in range(nt):
                    kts = (i,) if i == 0 else (i - 1, i)
                    for j, kt in enumerate(kts):
                        bu = bui(h, i, j if i > 0 else 0)
                        S.wait(v, "pbm", bu + 1)
                        bank = self.pb[2 + bu % 2].ap()
                        mt = self.s_maskLT1 if (i > 0 and j == 0) else self.s_maskLT2
                        S.inc(v.tensor_tensor(bank[0:128, 0:128], bank[0:128, 0:128],
                                              mt[:, :], ALU.add), "mk", 1)
                        S.wait(v, "mk", S.c["mk"])
                    qn = h * nt + i
                    S.wait(v, "pv", bui(h, i, 1 if i > 0 else 0) + 1)
                    if qn >= 1:
                        S.wait(v, "lx", qn)
                    S.edge(v, "e1", v.tensor_scalar(self.s_linv[0:1, 0:128],
                                                    self.pb[4].ap()[64:65, 0:128],
                                                    1e-30, None, ALU.max))
                    S.inc(v.reciprocal(self.s_linv[0:1, 0:128], self.s_linv[0:1, 0:128]),
                          "nz", 1)
                    S.wait(v, "lx", qn + 1)
                    iv = v.tensor_copy(self.s_lexp[0:64, 0:128], self.pb[5].ap()[0:64, 0:128])
                    S.inc(iv, "nz", 1)
                    S.wait(v, "nz", S.c["nz"])
                    dst = s_local.ap()[(h % 2) * 64:(h % 2) * 64 + 64,
                                       (h // 2) * T + i * 128:(h // 2) * T + (i + 1) * 128]
                    S.inc(v.tensor_tensor(dst, self.pb[4].ap()[0:64, 0:128],
                                          self.s_lexp[0:64, 0:128], ALU.mult), "nz", 1)

            for h in range(NH):
                dve_passA(h)
                if h >= 1:
                    dve_passB(h - 1)
            dve_passB(NH - 1)
            S.wait(v, "nz", S.c["nz"])
            S.inc(v.memset(s_local.ap().rearrange("p (c t) -> p c t", c=8)[:, :, 0:1], 0.0),
                  "nz", 1)
            for ts in range(T // 512):
                S.wait(v, "g1m", ts + 1)
                bank = self.pb[0 + ts % 2].ap()[0:NH, :]
                S.inc(v.tensor_copy(s_g1.ap()[0:NH, ts * 512:(ts + 1) * 512], bank),
                      "g1v", 1)

        @blk.scalar
        def _(sc):
            wait_finals(sc, prev)
            bu = 0
            for h in range(NH):
                for i in range(nt):
                    kts = (i,) if i == 0 else (i - 1, i)
                    for j, kt in enumerate(kts):
                        S.wait(sc, "pbm", bu + 1)
                        S.wait(sc, "mk", bu + 1)
                        if bu >= 2:
                            S.wait(sc, "pv", bu - 1)
                        bank = self.pb[2 + bu % 2].ap()
                        S.inc(sc.activation(s_pt[:, (bu % 2) * 128:(bu % 2 + 1) * 128],
                                            bank[:, 0:128], AF.Exp), "ex", 1)
                        bu += 1
    f = S.finals()
    f.update(RK.finals())
    self.prev = f


Prog.local_attn = _local_attn


def _mix_proj(self, tag, s_latout, s_local, s_g1, Rcur, Rnext, ctx=None):
    nc, nt, T = self.nc, self.nt, self.T
    d = self.d
    S = Sems(self, tag, ["wl", "gq", "mm", "g2", "gt", "ev", "mx", "tr", "mtv",
                         "pm", "ve", "e1", "sg"], sw=("wl",))
    RR = Ring(self, f"{tag}rr", 4, base_idx=16)
    RS = Ring(self, f"{tag}rs", 2, base_idx=20)
    prev = self.prev
    s_ro = ctx.enter_context(nc.sbuf_tensor(f"{tag}_ro", [128, 2 * D], BF16))
    s_rot = ctx.enter_context(nc.sbuf_tensor(f"{tag}_rot", [128, 8 * LATD], BF16))
    s_rogr = ctx.enter_context(nc.sbuf_tensor(f"{tag}_rogr", [128, 2 * NH], BF16))
    s_brocol = ctx.enter_context(nc.sbuf_tensor(f"{tag}_brocol", [128, 8], BF16))
    s_bro = ctx.enter_context(nc.sbuf_tensor(f"{tag}_bro", [1, D], BF16))
    s_bgr = ctx.enter_context(nc.sbuf_tensor(f"{tag}_bgr", [1, NH], BF16))
    s_wo = ctx.enter_context(nc.sbuf_tensor(f"{tag}_wo", [128, 8 * D], BF16))
    s_bo = ctx.enter_context(nc.sbuf_tensor(f"{tag}_bo", [1, D], BF16))
    s_rem = ctx.enter_context(nc.sbuf_tensor(f"{tag}_rem", [128, 2 * D], F32))
    s_g2 = ctx.enter_context(nc.sbuf_tensor(f"{tag}_g2", [128, 2 * NH], F32))
    s_gate = ctx.enter_context(nc.sbuf_tensor(f"{tag}_gate", [128, 2 * NH], F32))
    s_loc = ctx.enter_context(nc.sbuf_tensor(f"{tag}_loc", [128, 2 * D], BF16))
    s_d = ctx.enter_context(nc.sbuf_tensor(f"{tag}_d", [128, D], F32))
    s_gd = ctx.enter_context(nc.sbuf_tensor(f"{tag}_gd", [128, D], F32))
    s_mix = ctx.enter_context(nc.sbuf_tensor(f"{tag}_mix", [128, 2 * D], BF16))
    s_mixT = ctx.enter_context(nc.sbuf_tensor(f"{tag}_mixT", [128, 2 * D], BF16))
    s_res = ctx.enter_context(nc.sbuf_tensor(f"{tag}_res", [128, 2 * D], F32))

    lat_ch = s_latout.ap().rearrange("p (c t) -> p c t", c=2)
    loc_ch = s_local.ap().rearrange("p (c t) -> p c t", c=8)

    with nc.Block(f"b_{tag}") as blk:
        @blk.gpsimd
        def _(g):
            wait_finals(g, prev)
            S.inc(g.dma_start(out=s_ro.ap().rearrange("p (c n) -> p c n", c=2),
                              in_=d["wrout"].rearrange("(c p) n -> p c n", p=128)), "wl", 16)
            S.inc(g.dma_start(out=s_rot.ap().rearrange("p (c n) -> p c n", c=8),
                              in_=rr8(d["wroutT"])), "wl", 16)
            S.inc(g.dma_start(out=s_brocol.ap(),
                              in_=d["brout"].rearrange("o (c p) -> p (o c)", p=128),
                              allow_slow_non_contiguous=True), "wl", 16)
            S.inc(g.dma_start(out=s_bro[:, :], in_=d["brout"]), "wl", 16)
            S.inc(g.dma_start(out=s_wo.ap().rearrange("p (c n) -> p c n", c=8),
                              in_=rr8(d["wout_b"])), "wl", 16)
            S.inc(g.dma_start(out=s_bo[:, :], in_=d["bout_b"]), "wl", 16)

        @blk.sync
        def _(s):
            wait_finals(s, prev)
            def emit_store(i):
                S.wait(s, "ve", 2 * i + 2)
                RS.inc(s.dma_start(out=Rnext[i * 128:(i + 1) * 128, :],
                                   in_=s_res[:, (i % 2) * D:(i % 2 + 1) * D]), i)

            for i in range(nt):
                if i >= 4:
                    emit_store(i - 4)
                    S.wait(s, "ve", 2 * (i - 3))
                RR.inc(s.dma_start(out=self.s_acc[:, (i % 4) * D:(i % 4 + 1) * D],
                                   in_=Rcur[i * 128:(i + 1) * 128, :]), i)
            for i in range(max(0, nt - 4), nt):
                emit_store(i)
            for i in range(nt - 2, nt):
                RS.wait_done(s, i)

        @blk.tensor
        def _(t):
            wait_finals(t, prev)
            S.wait(t, "wl", S.c["wl"])
            for lg in range(2):
                last = None
                for c in range(8):
                    last = t.matmul(self.pb[6].ap()[:, lg * NH:(lg + 1) * NH],
                                    s_rot[:, c * LATD + lg * 128:c * LATD + (lg + 1) * 128],
                                    self.s_gr[:, c * NH:(c + 1) * NH],
                                    start=(c == 0), stop=(c == 7))
                S.inc(last, "gq", 1)
            last = None
            for c in range(8):
                last = t.matmul(self.pb[7].ap()[0:1, 0:NH], s_brocol[:, c:c + 1],
                                self.s_gr[:, c * NH:(c + 1) * NH],
                                start=(c == 0), stop=(c == 7))
            S.inc(last, "gq", 1)
            S.wait(t, "gq", 5)
            for i in range(nt):
                S.wait(t, "gt", max(0, 3 * i))
                for half in range(2):
                    bank = self.pb[half].ap()
                    for c in range(2):
                        t.matmul(bank, lat_ch[:, c, i * 128:(i + 1) * 128],
                                 s_ro[:, c * D + half * 512:c * D + half * 512 + 512],
                                 start=(c == 0), stop=False)
                    last = t.matmul(bank, self.s_ones_bf[:, 0:128],
                                    s_bro[:, half * 512:(half + 1) * 512],
                                    start=False, stop=True)
                    S.inc(last, "mm", 1)
                bank = self.pb[2].ap()[:, 0:NH]
                for c in range(2):
                    t.matmul(bank, lat_ch[:, c, i * 128:(i + 1) * 128],
                             s_rogr[:, c * NH:(c + 1) * NH], start=(c == 0), stop=False)
                last = t.matmul(bank, self.s_ones_bf[:, 0:128], s_bgr[:, :],
                                start=False, stop=True)
                S.inc(last, "mm", 1)
                last = t.transpose(self.pb[3].ap()[0:128, 0:NH],
                                   s_g1.ap()[0:NH, i * 128:(i + 1) * 128],
                                   self.s_id_f32[0:NH, 0:NH])
                S.inc(last, "mm", 1)
                S.wait(t, "mx", max(0, 4 * (i - 1) + 1))
                b6 = self.pb[6].ap().bitcast(BF16)
                for c in range(8):
                    last = t.transpose(b6[:, c * 128:(c + 1) * 128],
                                       loc_ch[:, c, i * 128:(i + 1) * 128], self.s_id_bf[:, :])
                S.inc(last, "tr", 1)
                S.wait(t, "mx", 4 * i + 4)
                S.wait(t, "mtv", max(0, 2 * i))
                b7 = self.pb[7].ap().bitcast(BF16)
                for c in range(8):
                    last = t.transpose(b7[:, c * 128:(c + 1) * 128],
                                       s_mix[:, (i % 2) * D + c * 128:(i % 2) * D + (c + 1) * 128],
                                       self.s_id_bf[:, :])
                S.inc(last, "tr", 1)
                S.wait(t, "mtv", 2 * i + 2)
                for half in range(2):
                    if i >= 1:
                        S.wait(t, "ve", 2 * (i - 1) + half + 1)
                    bank = self.pb[4 + half].ap()
                    mixT = s_mixT.ap().rearrange("p (b c t) -> p b c t", b=2, c=8)[:, i % 2]
                    for c in range(8):
                        t.matmul(bank, mixT[:, c, :],
                                 s_wo[:, c * D + half * 512:c * D + half * 512 + 512],
                                 start=(c == 0), stop=False)
                    last = t.matmul(bank, self.s_ones_bf[:, 0:128],
                                    s_bo[:, half * 512:(half + 1) * 512],
                                    start=False, stop=True)
                    S.inc(last, "pm", 1)

        @blk.vector
        def _(v):
            wait_finals(v, prev)
            S.wait(v, "gq", 2)
            v.tensor_copy(s_rogr[:, 0:NH], self.pb[6].ap()[:, 0:NH])
            S.edge(v, "e1", v.tensor_copy(s_rogr[:, NH:2 * NH], self.pb[6].ap()[:, NH:2 * NH]))
            S.wait(v, "gq", 3)
            S.inc(v.tensor_copy(s_bgr[:, :], self.pb[7].ap()[0:1, 0:NH]), "gq", 1)
            S.inc(v.nop(), "gq", 1)
            for i in range(nt):
                S.wait(v, "mm", 4 * i + 2)
                v.tensor_copy(s_rem[:, (i % 2) * D:(i % 2) * D + 512], self.pb[0].ap())
                S.edge(v, "e1", v.tensor_copy(s_rem[:, (i % 2) * D + 512:(i % 2 + 1) * D],
                                              self.pb[1].ap()))
                S.inc(v.nop(), "gt", 1)
                S.wait(v, "mm", 4 * i + 4)
                if i >= 2:
                    S.wait(v, "sg", i - 1)
                S.edge(v, "e1", v.tensor_copy(s_g2[:, (i % 2) * NH:(i % 2 + 1) * NH],
                                              self.pb[2].ap()[:, 0:NH]))
                S.inc(v.tensor_tensor(s_g2[:, (i % 2) * NH:(i % 2 + 1) * NH],
                                      s_g2[:, (i % 2) * NH:(i % 2 + 1) * NH],
                                      self.pb[3].ap()[:, 0:NH], ALU.add), "gt", 1)
                S.inc(v.nop(), "gt", 1)
                S.wait(v, "tr", 2 * i + 1)
                S.edge(v, "e1", v.tensor_copy(s_loc[:, (i % 2) * D:(i % 2 + 1) * D],
                                              self.pb[6].ap().bitcast(BF16)[:, 0:D]))
                S.inc(v.nop(), "mx", 1)
                S.wait(v, "sg", i + 1)
                S.edge(v, "e1", v.tensor_sub(s_d[:, :], s_loc[:, (i % 2) * D:(i % 2 + 1) * D],
                                             s_rem[:, (i % 2) * D:(i % 2 + 1) * D]))
                S.inc(v.nop(), "mx", 1)
                gb = s_gate.ap()[:, (i % 2) * NH:(i % 2 + 1) * NH].unsqueeze(2).broadcast_to([128, NH, HDIM])
                S.edge(v, "e1", v.tensor_tensor(s_gd.ap().rearrange("p (h e) -> p h e", h=NH),
                                                s_d.ap().rearrange("p (h e) -> p h e", h=NH),
                                                gb, ALU.mult))
                S.inc(v.nop(), "mx", 1)
                S.inc(v.tensor_add(s_mix[:, (i % 2) * D:(i % 2 + 1) * D], s_gd[:, :],
                                   s_rem[:, (i % 2) * D:(i % 2 + 1) * D]), "mx", 1)
                S.wait(v, "mx", S.c["mx"])
                S.wait(v, "tr", 2 * i + 2)
                S.inc(v.tensor_copy(
                    s_mixT.ap().rearrange("p (b c t) -> p b c t", b=2, c=8)[:, i % 2],
                    self.pb[7].ap().bitcast(BF16).rearrange("p (c t) -> p c t", c=8)), "mtv", 2)
                RR.wait_done(v, i)
                if i >= 2:
                    RS.wait_done(v, i - 2)
                for half in range(2):
                    S.wait(v, "pm", 2 * i + half + 1)
                    S.inc(v.tensor_add(s_res[:, (i % 2) * D + half * 512:(i % 2) * D + half * 512 + 512],
                                       self.pb[4 + half].ap(),
                                       self.s_acc[:, (i % 4) * D + half * 512:(i % 4) * D + half * 512 + 512]),
                          "ve", 1)

        @blk.scalar
        def _(sc):
            wait_finals(sc, prev)
            for i in range(nt):
                S.wait(sc, "gt", 3 * i + 3)
                if i >= 2:
                    S.wait(sc, "mx", 4 * (i - 1) + 3)
                S.inc(sc.activation(s_gate[:, (i % 2) * NH:(i % 2 + 1) * NH],
                                    s_g2[:, (i % 2) * NH:(i % 2 + 1) * NH], AF.Sigmoid),
                      "sg", 1)
    f = S.finals()
    f.update(RR.finals())
    f.update(RS.finals())
    self.prev = f


Prog.mix_proj = _mix_proj


# version 05
# speedup vs baseline: 1.0034x; 1.0034x over previous
import math
import numpy as np
import ml_dtypes
import concourse.bass as bass
import concourse.mybir as mybir
from concourse.bass_utils import run_bass_kernel_spmd

F32 = mybir.dt.float32
BF16 = mybir.dt.bfloat16
AF = mybir.ActivationFunctionType
ALU = mybir.AluOpType
AX = mybir.AxisListType
bf = ml_dtypes.bfloat16

D = 1024
NH = 16
HDIM = 64
HID = 2816
NE = 9
LATD = 256
NLH = 4
NLAT = 64
NEG = -1.0e9
EPS_LN = 1e-5


class _Slot:
    __slots__ = ("h", "c")

    def __init__(self, h):
        self.h = h
        self.c = 0


class Sems:

    def __init__(self, prog, tag, names, sw=()):
        self.slots = {}
        self.base = {}
        gi = 0
        si = 0
        for n in names:
            if n in sw:
                slot = prog.sw_slot(si)
                si += 1
            else:
                slot = prog.gen_slot(gi)
                gi += 1
            self.slots[n] = slot
            self.base[n] = slot.c

    @property
    def h(self):
        return {n: s.h for n, s in self.slots.items()}

    @property
    def c(self):
        return {n: sl.c - self.base[n] for n, sl in self.slots.items()}

    def inc(self, instr, name, n=1):
        slot = self.slots[name]
        instr.then_inc(slot.h, n)
        slot.c += n
        return slot.c - self.base[name]

    def wait(self, eng, name, v):
        if v > 0:
            eng.wait_ge(self.slots[name].h, self.base[name] + v)

    def finals(self):
        return {sl.h: sl.c for n, sl in self.slots.items() if sl.c > self.base[n]}

    def edge(self, eng, name, instr):
        v = self.inc(instr, name, 1)
        self.wait(eng, name, v)
        return v


def wait_finals(eng, finals):
    for h, v in finals.items():
        eng.wait_ge(h, v)


class Ring:

    def __init__(self, prog, tag, n, base_idx=8):
        self.ss = [prog.gen_slot(base_idx + j) for j in range(n)]
        self.n = n
        self.target = {}

    def inc(self, instr, item, amt=16):
        s = self.ss[item % self.n]
        instr.then_inc(s.h, amt)
        s.c += amt
        self.target[item] = s.c

    def wait_done(self, eng, item):
        if item < 0 or item not in self.target:
            return
        eng.wait_ge(self.ss[item % self.n].h, self.target[item])

    def finals(self):
        return {s.h: s.c for s in self.ss if s.c > 0}


def rr8(ap, c=8):
    return ap.rearrange("(c p) n -> p c n", p=128)


class Prog:
    def __init__(self, T, debug_outs=()):
        assert T % 512 == 0
        self.T = T
        self.nt = T // 128
        self.ng = T // 512
        self.CH = T // NLAT
        self.nc = bass.Bass("TRN2", target_bir_lowering=False, debug=False)
        self.prev = {}
        self._gen_slots = {}
        self._sw_slots = {}
        self.debug_outs = set(debug_outs)
        self.dbg = {}

    def gen_slot(self, i):
        if i not in self._gen_slots:
            self._gen_slots[i] = _Slot(self.nc.alloc_semaphore(f"g{i}"))
        return self._gen_slots[i]

    def sw_slot(self, i):
        if i not in self._sw_slots:
            self._sw_slots[i] = _Slot(self.nc.alloc_semaphore(f"sw{i}"))
        return self._sw_slots[i]

    @property
    def wslots(self):
        if not hasattr(self, "_wslots"):
            self._wslots = [_Slot(self.nc.alloc_semaphore("wsl0")),
                            _Slot(self.nc.alloc_semaphore("wsl1"))]
            self.wtarget = {}
        return self._wslots

    def make_dram(self):
        nc, T = self.nc, self.T
        din = lambda n, s: nc.dram_tensor(n, s, F32, kind="ExternalInput").ap()
        dbi = lambda n, s: nc.dram_tensor(n, s, BF16, kind="ExternalInput").ap()
        scr = lambda n, s, dt=F32: nc.dram_tensor(n, s, dt, kind="Internal").ap()
        d = {}
        d["x"] = din("x", [T, D])
        d["wqkv_a"] = din("wqkv_a", [D, 3 * D])
        d["wout_a"] = din("wout_a", [D, D])
        d["head_scales"] = din("head_scales", [NH, 1])
        d["hdblk"] = din("hdblk", [D, NH])
        d["wqkv_b"] = din("wqkv_b", [D, 3 * D])
        d["bqkv_b"] = din("bqkv_b", [1, 3 * D])
        d["wrq"] = din("wrq", [D, LATD])
        d["brq"] = din("brq", [1, LATD])
        d["wrkv"] = din("wrkv", [D, 2 * LATD])
        d["brkv"] = din("brkv", [1, 2 * LATD])
        d["wroutT"] = din("wroutT", [D, LATD])
        d["wrout"] = din("wrout", [LATD, D])
        d["brout"] = din("brout", [1, D])
        d["wout_b"] = din("wout_b", [D, D])
        d["bout_b"] = din("bout_b", [1, D])
        d["glblk"] = din("glblk", [D, NH])
        d["grblk"] = din("grblk", [D, NH])
        d["w1T"] = din("w1T", [NE, D, HID])
        d["w3T"] = din("w3T", [NE, D, HID])
        d["w2T"] = din("w2T", [NE, HID, D])
        d["gateT"] = din("gateT", [D, NE])
        d["gbias"] = din("gbias", [1, NE])
        d["id_bf"] = dbi("id_bf", [128, 128])
        d["id_f32"] = din("id_f32", [128, 128])
        d["ones_bf"] = dbi("ones_bf", [1, 512])
        d["onesT"] = dbi("onesT", [1, self.T])
        d["ones_f32"] = din("ones_f32", [1, 64])
        d["maskA"] = dbi("maskA", [4 * 128, 512])
        d["maskBT"] = dbi("maskBT", [128, 128])
        d["maskL"] = dbi("maskL", [128, 256])
        d["maskLT1"] = dbi("maskLT1", [128, 128])
        d["maskLT2"] = dbi("maskLT2", [128, 128])
        d["latC"] = din("latC", [128, NLAT])
        d["latCT"] = din("latCT", [NLAT, 128])
        d["ind"] = dbi("ind", [self.nt * 128, NLAT])
        d["Ra"] = scr("Ra", [T, D])
        d["Rb"] = scr("Rb", [T, D])
        d["Qh"] = scr("Qh", [D, T], BF16)
        d["Kh"] = scr("Kh", [D, T], BF16)
        d["Vh"] = scr("Vh", [T, D], BF16)
        d["qp"] = scr("qp", [NH, T], BF16)
        d["kp"] = scr("kp", [NH, T], BF16)
        d["stats"] = scr("stats", [self.nt, 128])
        d["xfm"] = scr("xfm", [D, T], BF16)
        d["w1b"] = scr("w1b", [NE, D, HID], BF16)
        d["w3b"] = scr("w3b", [NE, D, HID], BF16)
        d["w2b"] = scr("w2b", [NE, HID, D], BF16)
        d["g1"] = scr("g1", [NH, T])
        d["out"] = nc.dram_tensor("out", [T, D], F32, kind="ExternalOutput").ap()
        self.d = d
        for name in self.debug_outs:
            shp, dt = {
                "d_xlnT": ([D, T], BF16), "d_Q": ([D, T], BF16), "d_K": ([D, T], BF16),
                "d_V": ([T, D], BF16), "d_qp": ([NH, T], BF16), "d_kp": ([NH, T], BF16),
                "d_attnout": ([D, T], BF16), "d_m": ([NH * self.nt, 128], F32),
                "d_cw": ([128, self.nt * NE], F32),
                "d_latout": ([LATD, T], BF16), "d_local": ([D, T], BF16),
                "d_pool": ([NLAT, D], BF16), "d_g1": ([NH, T], F32),
                "d_Ra": ([T, D], F32), "d_Rb": ([T, D], F32),
            }[name]
            self.dbg[name] = nc.dram_tensor(name, shp, dt, kind="ExternalOutput").ap()

    def make_persist(self):
        nc, T, nt = self.nc, self.T, self.nt
        import contextlib
        self.top = contextlib.ExitStack()
        sb = lambda n, s, dt: self.top.enter_context(nc.sbuf_tensor(n, s, dt))
        self.s_id_bf = sb("c_id_bf", [128, 128], BF16)
        self.s_id_f32 = sb("c_id_f32", [128, 128], F32)
        self.s_ones_bf = sb("c_ones_bf", [1, 512], BF16)
        self.s_ones_f32 = sb("c_ones_f32", [1, 64], F32)
        self.s_maskA = sb("c_maskA", [128, 4 * 512], BF16)
        self.s_maskBT = sb("c_maskBT", [128, 128], BF16)
        self.s_maskL = sb("c_maskL", [128, 256], BF16)
        self.s_maskLT1 = sb("c_maskLT1", [128, 128], BF16)
        self.s_maskLT2 = sb("c_maskLT2", [128, 128], BF16)
        self.s_latC = sb("c_latC", [128, NLAT], F32)
        self.s_latCT = sb("c_latCT", [NLAT, 128], F32)
        self.s_ind = sb("c_ind", [128, nt * NLAT], BF16)
        self.s_hd = sb("c_hd", [128, 8 * NH], BF16)
        self.s_gl = sb("c_gl", [128, 8 * NH], BF16)
        self.s_gr = sb("c_gr", [128, 8 * NH], BF16)
        self.s_hs8 = sb("c_hs8", [NH, 1], F32)
        self.s_cw = sb("c_cw", [128, nt * NE], F32)
        self.s_stat = sb("c_stat", [128, 4 * nt], F32)
        self.s_m = sb("c_m", [128, nt], F32)
        self.s_mneg = sb("c_mneg", [128, nt], F32)
        self.s_mg = sb("c_mg", [128, 16], F32)
        self.s_acc = sb("c_acc", [128, 4 * D], F32)
        self.s_row = sb("c_row", [64, 128], F32)
        self.s_msk = sb("c_msk", [128, 512], F32)
        self.s_tmp1 = sb("c_tmp1", [128, 2], F32)
        self.s_linv = sb("c_linv", [1, 512], F32)
        self.s_eps = sb("c_eps", [128, 1], F32)
        self.s_lexp = sb("c_lexp", [64, 512], F32)
        self.pb = [self.top.enter_context(nc.psum_tensor(f"pb{i}", [128, 512], F32))
                   for i in range(8)]

    def setup_phase(self):
        nc, d = self.nc, self.d
        S = Sems(self, "set", ["qp", "qs"], sw=("qp",))
        wslot = self.sw_slot(1)
        with nc.Block("setup") as blk:
            @blk.gpsimd
            def _(g):
                sbr = lambda t, c: t.ap().rearrange("p (c n) -> p c n", c=c)
                S.inc(g.dma_start(out=sbr(self.s_hd, 8), in_=rr8(d["hdblk"])), "qp", 16)
                S.inc(g.dma_start(out=sbr(self.s_gl, 8), in_=rr8(d["glblk"])), "qp", 16)
                S.inc(g.dma_start(out=sbr(self.s_gr, 8), in_=rr8(d["grblk"])), "qp", 16)
                S.inc(g.dma_start(out=self.s_hs8[:, :], in_=d["head_scales"]), "qp", 16)
                for e in range(NE):
                    for nm in ("w1", "w3", "w2"):
                        ins = g.dma_start(out=d[nm + "b"][e], in_=d[nm + "T"][e])
                        ins.then_inc(wslot.h, 16)
                        wslot.c += 16
                self.wcast = (wslot.h, wslot.c)

            @blk.sync
            def _(s):
                S.inc(s.dma_start(out=self.s_id_bf[:, :], in_=d["id_bf"]), "qs", 16)
                S.inc(s.dma_start(out=self.s_id_f32[:, :], in_=d["id_f32"]), "qs", 16)
                S.inc(s.dma_start(out=self.s_ones_bf[:, :], in_=d["ones_bf"]), "qs", 16)
                S.inc(s.dma_start(out=self.s_ones_f32[:, :], in_=d["ones_f32"]), "qs", 16)
                S.inc(s.dma_start(out=self.s_maskA.ap().rearrange("p (c n) -> p c n", c=4),
                                  in_=rr8(d["maskA"], 4)), "qs", 16)
                S.inc(s.dma_start(out=self.s_maskBT[:, :], in_=d["maskBT"]), "qs", 16)
                S.inc(s.dma_start(out=self.s_maskL[:, :], in_=d["maskL"]), "qs", 16)
                S.inc(s.dma_start(out=self.s_maskLT1[:, :], in_=d["maskLT1"]), "qs", 16)
                S.inc(s.dma_start(out=self.s_maskLT2[:, :], in_=d["maskLT2"]), "qs", 16)
                S.inc(s.dma_start(out=self.s_latC[:, :], in_=d["latC"]), "qs", 16)
                S.inc(s.dma_start(out=self.s_latCT[:, :], in_=d["latCT"]), "qs", 16)
                S.inc(s.dma_start(out=self.s_ind.ap().rearrange("p (c n) -> p c n", c=self.nt),
                                  in_=rr8(d["ind"], self.nt)), "qs", 16)

            @blk.vector
            def _(v):
                v.memset(self.s_eps[:, :], EPS_LN)
                S.wait(v, "qp", S.c["qp"])
                S.inc(v.tensor_scalar(self.s_hs8[:, :], self.s_hs8[:, :], 8.0, None,
                                      ALU.mult), "qs", 1)
        self.prev = S.finals()

    def ln_pass(self, tag, Rcur, s_xlnT, pooled_out=None, ctx=None):
        nc, nt, T = self.nc, self.nt, self.T
        S = Sems(self, tag, ["sum", "xln", "ev", "sq", "chn", "tr", "pm", "pl"])
        RL = Ring(self, f"{tag}ld", 3)
        prev = self.prev
        import contextlib as _ctl
        lctx = _ctl.ExitStack()
        s_xlnb = lctx.enter_context(nc.sbuf_tensor(f"{tag}_xlnb", [128, 2 * D], BF16))
        s_xin = lctx.enter_context(nc.sbuf_tensor(f"{tag}_xin", [128, 3 * D], F32))
        s_sq = lctx.enter_context(nc.sbuf_tensor(f"{tag}_sq", [128, D], F32))
        s_pacc = None
        if pooled_out is not None:
            s_pacc = lctx.enter_context(nc.sbuf_tensor(f"{tag}_pacc", [64, D], F32))
        with nc.Block(f"b_{tag}") as blk:
            @blk.sync
            def _(s):
                wait_finals(s, prev)
                for i in range(nt):
                    if i >= 3:
                        S.wait(s, "xln", i - 2)
                    RL.inc(s.dma_start(out=s_xin[:, (i % 3) * D:(i % 3 + 1) * D],
                                       in_=Rcur[i * 128:(i + 1) * 128, :]), i)

            @blk.scalar
            def _(sc):
                wait_finals(sc, prev)
                st = self.s_stat
                for i in range(nt):
                    RL.wait_done(sc, i)
                    S.wait(sc, "chn", i)
                    xin = s_xin[:, (i % 3) * D:(i % 3 + 1) * D]
                    S.edge(sc, "sq", sc.activation(s_sq[:, :], xin, AF.Square,
                                                   accum_out=st[:, 2 * nt + i:2 * nt + i + 1]))
                    S.wait(sc, "sum", i + 1)
                    S.edge(sc, "sq", sc.activation(st[:, nt + i:nt + i + 1], st[:, i:i + 1],
                                                   AF.Identity, scale=1.0 / D))
                    S.edge(sc, "sq", sc.activation(s_sq[:, 0:1],
                                                   st[:, nt + i:nt + i + 1], AF.Square))
                    S.edge(sc, "sq", sc.activation(s_sq[:, 1:2], s_sq[:, 0:1],
                                                   AF.Identity, scale=-1.0))
                    S.edge(sc, "sq", sc.activation(s_sq[:, 2:3],
                                                   st[:, 2 * nt + i:2 * nt + i + 1],
                                                   AF.Identity, scale=1.0 / D,
                                                   bias=s_sq[:, 1:2]))
                    S.edge(sc, "sq", sc.activation(s_sq[:, 3:4], s_sq[:, 2:3],
                                                   AF.Ln, bias=self.s_eps[:, :]))
                    S.inc(sc.activation(st[:, 3 * nt + i:3 * nt + i + 1],
                                        s_sq[:, 3:4], AF.Exp, scale=-0.5), "chn", 1)

            @blk.vector
            def _(v):
                wait_finals(v, prev)

                def emit_xln(i):
                    S.wait(v, "chn", i + 1)
                    if i >= 2:
                        S.wait(v, "tr", i - 1)
                    xin = s_xin[:, (i % 3) * D:(i % 3 + 1) * D]
                    mean = self.s_stat[:, nt + i:nt + i + 1]
                    rstd = self.s_stat[:, 3 * nt + i:3 * nt + i + 1]
                    dst = s_xlnb[:, (i % 2) * D:(i % 2 + 1) * D]
                    S.inc(v.tensor_scalar(dst, xin, mean, rstd, ALU.subtract, ALU.mult),
                          "xln", 1)

                def emit_evac(i):
                    S.wait(v, "tr", i + 1)
                    src = self.pb[2 + (i % 2)].ap().bitcast(BF16)
                    dstv = s_xlnT.ap().rearrange("p (c t) -> p c t", c=8)[:, :, i * 128:(i + 1) * 128]
                    S.inc(v.tensor_copy(dstv, src.rearrange("p (c t) -> p c t", c=8)),
                          "ev", 1)
                    if pooled_out is not None:
                        S.wait(v, "pm", i + 1)
                        if i == 0:
                            v.tensor_copy(s_pacc[:, 0:512], self.pb[6].ap()[0:NLAT, :])
                            iv = v.tensor_copy(s_pacc[:, 512:1024], self.pb[7].ap()[0:NLAT, :])
                        else:
                            v.tensor_tensor(s_pacc[:, 0:512], s_pacc[:, 0:512],
                                            self.pb[6].ap()[0:NLAT, :], ALU.add)
                            iv = v.tensor_tensor(s_pacc[:, 512:1024], s_pacc[:, 512:1024],
                                                 self.pb[7].ap()[0:NLAT, :], ALU.add)
                        S.inc(iv, "pl", 1)

                for i in range(nt):
                    RL.wait_done(v, i)
                    xin = s_xin[:, (i % 3) * D:(i % 3 + 1) * D]
                    S.inc(v.tensor_reduce(self.s_stat[:, i:i + 1], xin, AX.X, ALU.add),
                          "sum", 1)
                    if i >= 3:
                        emit_evac(i - 3)
                    if i >= 2:
                        emit_xln(i - 2)
                emit_evac(nt - 3)
                emit_xln(nt - 2)
                emit_evac(nt - 2)
                emit_xln(nt - 1)
                emit_evac(nt - 1)
                if pooled_out is not None:
                    S.wait(v, "pl", S.c["pl"])
                    S.inc(v.tensor_copy(pooled_out[:, :], s_pacc[:, :]), "pl", 1)

            @blk.tensor
            def _(t):
                wait_finals(t, prev)
                for i in range(nt):
                    S.wait(t, "xln", i + 1)
                    src = s_xlnb[:, (i % 2) * D:(i % 2 + 1) * D]
                    if pooled_out is not None:
                        if i >= 1:
                            S.wait(t, "pl", i)
                        t.matmul(self.pb[6].ap()[0:NLAT, :],
                                 self.s_ind[:, i * NLAT:(i + 1) * NLAT],
                                 src[:, 0:512], start=True, stop=True)
                        S.inc(t.matmul(self.pb[7].ap()[0:NLAT, :],
                                       self.s_ind[:, i * NLAT:(i + 1) * NLAT],
                                       src[:, 512:1024], start=True, stop=True), "pm", 1)
                    if i >= 2:
                        S.wait(t, "ev", i - 1)
                    dst = self.pb[2 + (i % 2)].ap().bitcast(BF16)
                    for c in range(8):
                        iv = t.transpose(dst[:, c * 128:(c + 1) * 128],
                                         src[:, c * 128:(c + 1) * 128], self.s_id_bf[:, :])
                    S.inc(iv, "tr", 1)
        lctx.close()
        f = S.finals()
        f.update(RL.finals())
        self.prev = f

    def proj_spill(self, tag, s_xlnT, W_in=None, COLS=0, bias_in=None, fcmodes=(),
                   Qdst=None, Kdst=None, Vdst=None, do_pp=False, do_route=False,
                   rq=None, xfm_spill=None, ctx=None, dbg_xlnT=False):
        nc, nt, T, ng = self.nc, self.nt, self.T, self.ng
        d = self.d
        S = Sems(self, tag, ["wld", "mm", "ev", "tr", "fmv", "sp", "pp", "ppv", "rt", "ra", "ppd", "e2"], sw=("wld",))
        RV = Ring(self, f"{tag}rv", 2, base_idx=16)
        RF = Ring(self, f"{tag}rf", 2, base_idx=18)
        prev = self.prev
        nfc = len(fcmodes)
        W = 512
        s_w = s_bias = s_fm = s_ev = s_vev = None
        if W_in is not None:
            s_w = ctx.enter_context(nc.sbuf_tensor(f"{tag}_w", [128, 2 * 8 * 512], BF16))
            if bias_in is not None:
                s_bias = ctx.enter_context(nc.sbuf_tensor(f"{tag}_bias", [1, COLS], BF16))
            s_fm = ctx.enter_context(nc.sbuf_tensor(f"{tag}_fm", [128, 4 * T], BF16))
            s_vev = ctx.enter_context(nc.sbuf_tensor(f"{tag}_vev", [128, 2 * 512], BF16))
        if W_in is not None or rq is not None:
            s_ev = ctx.enter_context(nc.sbuf_tensor(f"{tag}_ev", [128, 2 * 512], BF16))
        s_pp = s_ppb = None
        if do_pp:
            s_pp = ctx.enter_context(nc.sbuf_tensor(f"{tag}_pp", [NH, T], F32))
            s_ppb = ctx.enter_context(nc.sbuf_tensor(f"{tag}_ppb", [NH, T], BF16))
        s_gt = s_gb = s_rt = None
        if do_route:
            s_gt = ctx.enter_context(nc.sbuf_tensor(f"{tag}_gt", [128, 8 * NE], BF16))
            s_gb = ctx.enter_context(nc.sbuf_tensor(f"{tag}_gb", [1, NE], BF16))
            s_rt = ctx.enter_context(nc.sbuf_tensor(f"{tag}_rt", [128, 4 * NE], F32))
        s_rqw = s_rqb = s_rqT = None
        if rq is not None:
            wrq_in, brq_in, s_rqT = rq
            s_rqw = ctx.enter_context(nc.sbuf_tensor(f"{tag}_rqw", [128, 8 * LATD], BF16))
            s_rqb = ctx.enter_context(nc.sbuf_tensor(f"{tag}_rqb", [1, LATD], BF16))

        xlnT_ch = s_xlnT.ap().rearrange("p (c t) -> p c t", c=8)
        nts = T // 512
        nqk = sum(1 for m in fcmodes if m in ("q", "k"))
        units = nfc * nt + (nt if rq is not None else 0)

        def fcw(fc):
            return 256 if fc == nfc else 512

        def wsrc(fc, c):
            if fc == nfc:
                return s_rqw[:, c * LATD:c * LATD + 256]
            base = (fc % 2) * 8 * 512 + c * 512
            return s_w[:, base:base + 512]

        def bsrc(fc):
            if fc == nfc:
                return s_rqb[:, :]
            return s_bias[:, fc * 512:(fc + 1) * 512] if s_bias is not None else None

        allmodes = list(fcmodes) + (["rq"] if rq is not None else [])

        with nc.Block(f"b_{tag}") as blk:
            @blk.gpsimd
            def _(g):
                wait_finals(g, prev)
                if W_in is not None and bias_in is not None:
                    S.inc(g.dma_start(out=s_bias[:, :], in_=bias_in), "wld", 16)
                if do_route:
                    S.inc(g.dma_start(out=s_gt.ap().rearrange("p (c n) -> p c n", c=8),
                                      in_=rr8(d["gateT"])), "wld", 16)
                    S.inc(g.dma_start(out=s_gb[:, :], in_=d["gbias"]), "wld", 16)
                if rq is not None:
                    S.inc(g.dma_start(out=s_rqw.ap().rearrange("p (c n) -> p c n", c=8),
                                      in_=rr8(wrq_in)), "wld", 16)
                    S.inc(g.dma_start(out=s_rqb[:, :], in_=brq_in), "wld", 16)
                if W_in is not None:
                    for fc in range(nfc):
                        if fc >= 2:
                            S.wait(g, "mm", (fc - 1) * nt)
                        sl = self.wslots[fc % 2]
                        ins = g.dma_start(
                            out=s_w.ap().rearrange("p (b c n) -> p b c n", b=2, c=8)[:, fc % 2],
                            in_=W_in[:, fc * 512:(fc + 1) * 512].rearrange("(c p) n -> p c n", p=128))
                        ins.then_inc(sl.h, 16)
                        sl.c += 16
                        self.wtarget[(tag, fc)] = sl.c

            @blk.sync
            def _(s):
                wait_finals(s, prev)
                u = 0
                vv = 0
                for fc, mode in enumerate(allmodes):
                    if mode in ("q", "k"):
                        qki = sum(1 for f in allmodes[:fc] if f in ("q", "k"))
                        S.wait(s, "fmv", (qki + 1) * nt)
                        dstD = Qdst if mode == "q" else Kdst
                        for c in range(4):
                            gch = (fc % 2) * 4 + c
                            RF.inc(s.dma_start(out=dstD[gch * 128:(gch + 1) * 128, :],
                                               in_=s_fm[:, c * T:(c + 1) * T]), qki)
                        if do_pp and fc == 1:
                            S.wait(s, "ppv", 2 * nts + 1)
                            S.inc(s.dma_start(out=d["qp"], in_=s_ppb[:, :]), "ppd", 16)
                        if do_pp and fc == 3:
                            S.wait(s, "ppv", 4 * nts + 2)
                            S.inc(s.dma_start(out=d["kp"], in_=s_ppb[:, :]), "sp", 16)
                        u += nt
                    elif mode == "rq":
                        u += nt
                    else:
                        vfc = sum(1 for f in allmodes[:fc] if f == "v")
                        for i in range(nt):
                            S.wait(s, "ev", u + 1)
                            RV.inc(s.dma_start(out=Vdst[i * 128:(i + 1) * 128,
                                                        vfc * 512:(vfc + 1) * 512],
                                               in_=s_vev[:, (vv % 2) * 512:(vv % 2 + 1) * 512]),
                                   vv)
                            u += 1
                            vv += 1

                if xfm_spill is not None:
                    for c in range(8):
                        S.inc(s.dma_start(out=xfm_spill[c * 128:(c + 1) * 128, :],
                                          in_=s_xlnT[:, c * T:(c + 1) * T]), "sp", 16)
                if dbg_xlnT and "d_xlnT" in self.dbg:
                    for c in range(8):
                        S.inc(s.dma_start(out=self.dbg["d_xlnT"][c * 128:(c + 1) * 128, :],
                                          in_=s_xlnT[:, c * T:(c + 1) * T]), "sp", 16)
            @blk.tensor
            def _(t):
                wait_finals(t, prev)
                S.wait(t, "wld", S.c["wld"])
                u = 0
                uq = 0
                for fc, mode in enumerate(allmodes):
                    w = fcw(fc)
                    if W_in is not None and fc < nfc:
                        t.wait_ge(self.wslots[fc % 2].h, self.wtarget[(tag, fc)])
                    for i in range(nt):
                        if u >= 2:
                            S.wait(t, "ev", u - 1)
                        bank = self.pb[u % 2].ap()[:, 0:w]
                        hasb = bsrc(fc) is not None
                        last = None
                        for c in range(8):
                            last = t.matmul(bank, xlnT_ch[:, c, i * 128:(i + 1) * 128],
                                            wsrc(fc, c), start=(c == 0),
                                            stop=(c == 7 and not hasb))
                        if hasb:
                            last = t.matmul(bank, self.s_ones_bf[:, 0:128], bsrc(fc),
                                            start=False, stop=True)
                        S.inc(last, "mm", 1)
                        if mode in ("q", "k", "rq"):
                            S.wait(t, "ev", u + 1)
                            if uq >= 2:
                                S.wait(t, "fmv", uq - 1)
                            dstp = self.pb[2 + (uq % 2)].ap().bitcast(BF16)
                            for c in range(w // 128):
                                iv2 = t.transpose(dstp[:, c * 128:(c + 1) * 128],
                                                  s_ev[:, (uq % 2) * 512 + c * 128:(uq % 2) * 512 + (c + 1) * 128],
                                                  self.s_id_bf[:, :])
                            S.inc(iv2, "tr", 1)
                            uq += 1
                        u += 1
                    if do_pp and mode in ("q", "k"):
                        qki = sum(1 for f in allmodes[:fc] if f in ("q", "k"))
                        S.wait(t, "fmv", (qki + 1) * nt)
                        for ts in range(nts):
                            pidx = fc * nts + ts
                            if pidx >= 2:
                                S.wait(t, "ppv", pidx - 1)
                            bank = self.pb[4 + (pidx % 2)].ap()[0:NH, :]
                            for c in range(4):
                                gch = (fc % 2) * 4 + c
                                iv = t.matmul(bank,
                                              self.s_hd[:, gch * NH:(gch + 1) * NH],
                                              s_fm[:, c * T + ts * 512:c * T + ts * 512 + 512],
                                              start=(c == 0), stop=(c == 3))
                            S.inc(iv, "pp", 1)
                if do_route:
                    for i in range(nt):
                        if i >= 2:
                            S.wait(t, "ra", i - 1)
                        bank = self.pb[4 + (i % 2)].ap()[:, 0:NE]
                        for c in range(8):
                            t.matmul(bank, xlnT_ch[:, c, i * 128:(i + 1) * 128],
                                     s_gt[:, c * NE:(c + 1) * NE],
                                     start=(c == 0), stop=False)
                        iv = t.matmul(bank, self.s_ones_bf[:, 0:128], s_gb[:, :],
                                      start=False, stop=True)
                        S.inc(iv, "mm", 1)

            @blk.vector
            def _(v):
                wait_finals(v, prev)
                u = 0
                uq = 0
                vv = 0
                for fc, mode in enumerate(allmodes):
                    w = fcw(fc)
                    for i in range(nt):
                        S.wait(v, "mm", u + 1)
                        bank = self.pb[u % 2].ap()[:, 0:w]
                        if mode in ("q", "rq"):
                            if uq >= 2:
                                S.wait(v, "tr", uq - 1)
                            S.inc(v.tensor_scalar(s_ev[:, (uq % 2) * 512:(uq % 2) * 512 + w],
                                                  bank, 0.125, None, ALU.mult), "ev", 1)
                        elif mode == "k":
                            if uq >= 2:
                                S.wait(v, "tr", uq - 1)
                            S.inc(v.tensor_copy(s_ev[:, (uq % 2) * 512:(uq % 2) * 512 + w],
                                                bank), "ev", 1)
                        else:
                            if vv >= 2:
                                RV.wait_done(v, vv - 2)
                            S.inc(v.tensor_copy(s_vev[:, (vv % 2) * 512:(vv % 2 + 1) * 512],
                                                bank), "ev", 1)
                            vv += 1
                        if mode in ("q", "k", "rq"):
                            if mode in ("q", "k") and i == 0:
                                qki = sum(1 for f in allmodes[:fc] if f in ("q", "k"))
                                RF.wait_done(v, qki - 1)
                            S.wait(v, "tr", uq + 1)
                            src = self.pb[2 + (uq % 2)].ap().bitcast(BF16)
                            nblk = w // 128
                            if mode == "rq":
                                dstv = s_rqT.ap().rearrange("p (c t) -> p c t", c=2)[:, :, i * 128:(i + 1) * 128]
                            else:
                                dstv = s_fm.ap().rearrange("p (c t) -> p c t", c=4)[:, :, i * 128:(i + 1) * 128]
                            S.inc(v.tensor_copy(dstv, src.rearrange("p (c t) -> p c t", c=8)[:, 0:nblk, :]),
                                  "fmv", 1)
                            uq += 1
                        u += 1
                    if do_pp and mode in ("q", "k"):
                        for ts in range(nts):
                            pidx = fc * nts + ts
                            S.wait(v, "pp", pidx + 1)
                            bank = self.pb[4 + (pidx % 2)].ap()[0:NH, :]
                            dst = s_pp[:, ts * 512:(ts + 1) * 512]
                            if fc % 2 == 0:
                                iv = v.tensor_copy(dst, bank)
                            else:
                                iv = v.tensor_tensor(dst, dst, bank, ALU.add)
                            S.inc(iv, "ppv", 1)
                        if fc % 2 == 1:
                            S.wait(v, "ppv", S.c["ppv"])
                            if mode == "q":
                                iv = v.tensor_scalar(s_ppb[:, :], s_pp[:, :], self.s_hs8[:, :],
                                                     None, ALU.mult)
                            else:
                                S.wait(v, "ppd", 16)
                                iv = v.tensor_copy(s_ppb[:, :], s_pp[:, :])
                            S.inc(iv, "ppv", 1)
                if do_route:
                    mmbase = units
                    for i in range(nt):
                        rt = s_rt[:, (i % 2) * 2 * NE:(i % 2) * 2 * NE + NE]
                        rt2 = s_rt[:, (i % 2) * 2 * NE + NE:(i % 2) * 2 * NE + 2 * NE]
                        S.wait(v, "ra", i + 1)
                        S.wait(v, "rt", S.c["rt"])
                        m1 = self.s_tmp1[:, 0:1]
                        S.edge(v, "e2", v.tensor_reduce(m1, rt, AX.X, ALU.max))
                        S.edge(v, "e2", v.tensor_scalar(rt2, rt, m1, None, ALU.is_ge))
                        S.edge(v, "e2", v.scalar_tensor_tensor(rt2, rt2, -1e9, rt, ALU.mult, ALU.add))
                        S.edge(v, "e2", v.tensor_reduce(m1, rt2, AX.X, ALU.max))
                        S.edge(v, "e2", v.tensor_scalar(rt2, rt, m1, None, ALU.is_ge))
                        cw = self.s_cw[:, i * NE:(i + 1) * NE]
                        S.edge(v, "e2", v.tensor_tensor(cw, rt, rt2, ALU.mult))
                        S.edge(v, "e2", v.tensor_reduce(m1, cw, AX.X, ALU.add))
                        S.edge(v, "e2", v.tensor_scalar(m1, m1, 1e-20, None, ALU.add))
                        S.edge(v, "e2", v.reciprocal(m1, m1))
                        S.inc(v.tensor_scalar(cw, cw, m1, None, ALU.mult), "rt", 1)

            if do_route:
                @blk.scalar
                def _(sc):
                    wait_finals(sc, prev)
                    mmbase = units
                    for i in range(nt):
                        S.wait(sc, "mm", mmbase + i + 1)
                        bank = self.pb[4 + (i % 2)].ap()[:, 0:NE]
                        rt = s_rt[:, (i % 2) * 2 * NE:(i % 2) * 2 * NE + NE]
                        if i >= 2:
                            S.wait(sc, "rt", i - 1)
                        S.inc(sc.activation(rt, bank, AF.Sigmoid), "ra", 1)

        f = S.finals()
        f.update(RV.finals())
        f.update(RF.finals())
        self.prev = f

    def attn_a_heads(self, s_ao, ctx=None):
        nc, nt, T, ng = self.nc, self.nt, self.T, self.ng
        d = self.d
        S = Sems(self, "a2", ["pa", "rm", "sx", "sro", "pbm", "mk", "ex", "pv",
                              "nz", "lx", "dbg", "ms", "e1", "sri", "m2"], sw=("sri",))
        RK = Ring(self, "a2kld", 2, base_idx=16)
        prev = self.prev
        s_K = [ctx.enter_context(nc.sbuf_tensor(f"a2_K{p}", [66, T], BF16)) for p in range(2)]
        s_Q = [ctx.enter_context(nc.sbuf_tensor(f"a2_Q{p}", [66, T], BF16)) for p in range(2)]
        s_V = [ctx.enter_context(nc.sbuf_tensor(f"a2_V{p}", [128, nt * 65], BF16)) for p in range(2)]
        s_pt = ctx.enter_context(nc.sbuf_tensor("a2_pt", [128, 2 * 512], BF16))

        npa_h = nt * (nt + 4) // 8
        npa_h = sum(i // 4 + 1 for i in range(nt))
        nbu_h = sum(4 * qg + 4 for qg in range(ng))
        pbm_end = [nbu_h * (h + 1) for h in range(NH)]
        pv_end = [nbu_h * (h + 1) for h in range(NH)]

        with nc.Block("b_a2") as blk:
            @blk.sync
            def _(s):
                wait_finals(s, prev)
                for p in range(2):
                    S.inc(s.dma_start(out=s_K[p].ap()[65:66, :], in_=d["onesT"]), "ms", 16)
                for h in range(NH):
                    par = h % 2
                    if h >= 2:
                        S.wait(s, "pbm", pbm_end[h - 2])
                        S.wait(s, "pv", pv_end[h - 2])
                    RK.inc(s.dma_start(out=s_K[par].ap()[0:64, :],
                                       in_=d["Kh"][h * 64:(h + 1) * 64, :]), h)
                    RK.inc(s.dma_start(out=s_K[par].ap()[64:65, :],
                                       in_=d["kp"][h:h + 1, :]), h)
                    RK.inc(s.dma_start(out=s_Q[par].ap()[0:64, :],
                                       in_=d["Qh"][h * 64:(h + 1) * 64, :]), h)
                    RK.inc(s.dma_start(out=s_Q[par].ap()[64:65, :],
                                       in_=d["qp"][h:h + 1, :]), h)
                    RK.inc(s.dma_start(out=s_V[par].ap().rearrange("p (t c) -> p t c", c=65)[:, :, 0:64],
                                       in_=d["Vh"][:, h * 64:(h + 1) * 64].rearrange("(t p) d -> p t d", p=128)),
                           h)
                    S.wait(s, "sx", 3 * h + 3)
                    S.wait(s, "sri", 16 * h)
                    S.inc(s.dma_start(out=d["stats"], in_=self.s_row[0:nt, :]), "sro", 16)
                    if "d_m" in self.dbg:
                        S.inc(s.dma_start(out=self.dbg["d_m"][h * nt:(h + 1) * nt, :],
                                          in_=self.s_row[0:nt, :]), "dbg", 16)
                if "d_attnout" in self.dbg:
                    S.wait(s, "nz", 3 * ng * NH)
                    for c in range(8):
                        S.inc(s.dma_start(out=self.dbg["d_attnout"][c * 128:(c + 1) * 128, :],
                                          in_=s_ao[:, c * T:(c + 1) * T]), "dbg", 16)

            @blk.gpsimd
            def _(g):
                wait_finals(g, prev)
                for h in range(NH):
                    par = h % 2
                    S.wait(g, "sro", 16 * (h + 1))
                    if h >= 2:
                        S.wait(g, "pbm", pbm_end[h - 2])
                    S.inc(g.dma_start(out=s_Q[par].ap()[65:66, :],
                                      in_=d["stats"].rearrange("a b -> (a b)").unsqueeze(0)),
                          "sri", 16)

            @blk.tensor
            def _(t):
                wait_finals(t, prev)
                au = 0
                bu = 0
                qgi = 0
                S.wait(t, "ms", 33)

                def pe_passA(h):
                    nonlocal au
                    par = h % 2
                    RK.wait_done(t, h)
                    npa = sum(i // 4 + 1 for i in range(nt))
                    done = 0
                    chunk = max(1, (npa + ng - 1) // ng)
                    for i in range(nt):
                        for kg in range(i // 4 + 1):
                            if au >= 3:
                                S.wait(t, "rm", au - 2)
                            bank = self.pb[au % 3].ap() if au % 3 < 2 else self.pb[7].ap()
                            bank = [self.pb[0], self.pb[1], self.pb[7]][au % 3].ap()
                            S.inc(t.matmul(bank, s_Q[par].ap()[0:65, i * 128:(i + 1) * 128],
                                           s_K[par].ap()[0:65, kg * 512:(kg + 1) * 512],
                                           start=True, stop=True), "pa", 1)
                            au += 1
                            done += 1
                            if done % chunk == 0:
                                yield
                    S.wait(t, "sx", 3 * h + 1)
                    S.inc(t.transpose(self.pb[6].ap()[0:nt, 0:128], self.s_mneg[:, 0:nt],
                                      self.s_id_f32[:, :]), "sx", 1)
                    while True:
                        yield

                def pe_passB(h):
                    nonlocal bu, qgi
                    par = h % 2
                    S.wait(t, "sri", 16 * (h + 1))
                    for qg in range(ng):
                        nkt = 4 * qg + 4

                        def emit_pv(kt, bu_kt):
                            qoff = max(0, (kt - 4 * qg) * 128)
                            N = 512 - qoff
                            S.wait(t, "ex", bu_kt + 1)
                            if kt == 0:
                                S.wait(t, "nz", 3 * qgi)
                            S.inc(t.matmul(self.pb[4].ap()[0:65, qoff:512],
                                           s_V[par].ap()[:, kt * 65:kt * 65 + 65],
                                           s_pt[:, (bu_kt % 2) * 512:(bu_kt % 2) * 512 + N],
                                           start=(kt == 0), stop=(kt == nkt - 1)), "pv", 1)

                        for kt in range(nkt):
                            qoff = max(0, (kt - 4 * qg) * 128)
                            N = 512 - qoff
                            if bu >= 2:
                                S.wait(t, "ex", bu - 1)
                            bank = self.pb[2 + bu % 2].ap()
                            S.inc(t.matmul(bank[:, 0:N],
                                           s_K[par].ap()[0:66, kt * 128:(kt + 1) * 128],
                                           s_Q[par].ap()[0:66, qg * 512 + qoff:(qg + 1) * 512],
                                           start=True, stop=True), "pbm", 1)
                            if kt > 0:
                                emit_pv(kt - 1, bu - 1)
                            bu += 1
                        emit_pv(nkt - 1, bu - 1)
                        S.wait(t, "nz", 3 * qgi + 1)
                        S.inc(t.matmul(self.pb[5].ap()[0:64, :], self.s_ones_f32[0:1, 0:64],
                                       self.s_linv[0:1, 0:512], start=True, stop=True),
                              "lx", 1)
                        qgi += 1
                        yield

                def drive(ga, gb):
                    while True:
                        a_done = b_done = False
                        try:
                            next(ga)
                        except StopIteration:
                            a_done = True
                        if gb is not None:
                            try:
                                next(gb)
                            except StopIteration:
                                b_done = True
                        else:
                            b_done = True
                        if b_done and (ga is None or True) and gb is not None:
                            break
                        if gb is None and a_done:
                            break

                prev_gb = None
                for h in range(NH):
                    ga = pe_passA(h)
                    gb = pe_passB(h - 1) if h >= 1 else None
                    if gb is None:
                        for _ in range(ng + 1):
                            next(ga)
                    else:
                        for _ in range(ng):
                            next(ga)
                            try:
                                next(gb)
                            except StopIteration:
                                pass
                        next(ga)
                        for _ in range(2):
                            try:
                                next(gb)
                            except StopIteration:
                                break
                        for _ in range(ng):
                            try:
                                next(gb)
                            except StopIteration:
                                break
                ga = None
                gb = pe_passB(NH - 1)
                for _ in range(ng + 1):
                    try:
                        next(gb)
                    except StopIteration:
                        break

            @blk.vector
            def _(v):
                wait_finals(v, prev)
                for p in range(2):
                    iv = v.memset(s_V[p].ap().rearrange("p (t c) -> p t c", c=65)[:, :, 64:65], 1.0)
                S.inc(iv, "ms", 1)
                au = 0
                qgi = 0

                def dve_passA(h):
                    nonlocal au
                    npa = sum(i // 4 + 1 for i in range(nt))
                    done = 0
                    chunk = max(1, (npa + ng - 1) // ng)
                    for i in range(nt):
                        ngr = i // 4 + 1
                        for kg in range(ngr):
                            S.wait(v, "pa", au + 1)
                            bank = [self.pb[0], self.pb[1], self.pb[7]][au % 3].ap()
                            S.inc(v.tensor_reduce(self.s_mg[:, kg:kg + 1], bank, AX.X,
                                                  ALU.max), "rm", 1)
                            au += 1
                            done += 1
                            if done % chunk == 0:
                                yield
                        S.wait(v, "rm", S.c["rm"])
                        S.inc(v.tensor_reduce(self.s_m[:, i:i + 1], self.s_mg[:, 0:ngr],
                                              AX.X, ALU.max), "m2", 1)
                        S.wait(v, "m2", S.c["m2"])
                    if h >= 1:
                        S.wait(v, "sx", 3 * (h - 1) + 2)
                    S.inc(v.tensor_scalar(self.s_mneg[:, 0:nt], self.s_m[:, 0:nt],
                                          -1e8, -1.0, ALU.max, ALU.mult), "sx", 1)
                    S.wait(v, "sx", 3 * h + 2)
                    S.wait(v, "sro", 16 * h)
                    if "d_m" in self.dbg:
                        S.wait(v, "dbg", 16 * h)
                    S.inc(v.tensor_copy(self.s_row[0:nt, :], self.pb[6].ap()[0:nt, 0:128]),
                          "sx", 1)
                    while True:
                        yield

                def dve_passB(h):
                    nonlocal qgi
                    for qg in range(ng):
                        for kt in range(4 * qg, 4 * qg + 4):
                            buu = h * nbu_h + sum(4 * g + 4 for g in range(qg)) + kt
                            S.wait(v, "pbm", buu + 1)
                            bank = self.pb[2 + buu % 2].ap()
                            S.inc(v.tensor_tensor(bank[0:128, 0:128], bank[0:128, 0:128],
                                                  self.s_maskBT[:, :], ALU.add), "mk", 1)
                        S.wait(v, "pv", h * nbu_h + sum(4 * g + 4 for g in range(qg + 1)))
                        if qgi >= 1:
                            S.wait(v, "lx", qgi)
                        S.edge(v, "e1", v.tensor_scalar(self.s_linv[:, :],
                                                        self.pb[4].ap()[64:65, 0:512],
                                                        1e-30, None, ALU.max))
                        S.inc(v.reciprocal(self.s_linv[:, :], self.s_linv[:, :]), "nz", 1)
                        S.wait(v, "lx", qgi + 1)
                        iv = v.tensor_copy(self.s_lexp[:, :], self.pb[5].ap()[0:64, :])
                        S.inc(iv, "nz", 1)
                        S.wait(v, "nz", S.c["nz"])
                        dst = s_ao.ap()[(h % 2) * 64:(h % 2) * 64 + 64,
                                        (h // 2) * T + qg * 512:(h // 2) * T + (qg + 1) * 512]
                        S.inc(v.tensor_tensor(dst, self.pb[4].ap()[0:64, 0:512],
                                              self.s_lexp[:, :], ALU.mult), "nz", 1)
                        qgi += 1
                        yield

                for h in range(NH):
                    ga = dve_passA(h)
                    gb = dve_passB(h - 1) if h >= 1 else None
                    for _ in range(ng):
                        next(ga)
                        if gb is not None:
                            try:
                                next(gb)
                            except StopIteration:
                                pass
                    next(ga)
                    if gb is not None:
                        for _ in range(ng):
                            try:
                                next(gb)
                            except StopIteration:
                                break
                gb = dve_passB(NH - 1)
                for _ in range(ng + 1):
                    try:
                        next(gb)
                    except StopIteration:
                        break

            @blk.scalar
            def _(sc):
                wait_finals(sc, prev)
                bu = 0
                for h in range(NH):
                    for qg in range(ng):
                        for kt in range(4 * qg + 4):
                            qoff = max(0, (kt - 4 * qg) * 128)
                            N = 512 - qoff
                            S.wait(sc, "pbm", bu + 1)
                            bank = self.pb[2 + bu % 2].ap()
                            if kt >= 4 * qg:
                                S.wait(sc, "mk", self._mkcount(h, qg, kt, ng))
                            if bu >= 2:
                                S.wait(sc, "pv", bu - 1)
                            S.inc(sc.activation(s_pt[:, (bu % 2) * 512:(bu % 2) * 512 + N],
                                                bank[:, 0:N], AF.Exp), "ex", 1)
                            bu += 1

        f = S.finals()
        f.update(RK.finals())
        self.prev = f

    def _mkcount(self, h, qg, kt, ng):
        return h * 4 * ng + qg * 4 + (kt - 4 * qg) + 1

    def proj_residual(self, tag, s_src, W_in, bias_in, Rcur, Rnext, ctx=None,
                      extra_dump=None):
        nc, nt, T = self.nc, self.nt, self.T
        S = Sems(self, tag, ["wld", "mm", "ve", "st"], sw=("wld",))
        RR = Ring(self, f"{tag}rr", 4, base_idx=16)
        RS = Ring(self, f"{tag}rs", 2, base_idx=20)
        prev = self.prev
        s_w = ctx.enter_context(nc.sbuf_tensor(f"{tag}_w", [128, 8 * D], BF16))
        s_res = ctx.enter_context(nc.sbuf_tensor(f"{tag}_res", [128, 2 * D], F32))
        s_bias = None
        if bias_in is not None:
            s_bias = ctx.enter_context(nc.sbuf_tensor(f"{tag}_bias", [1, D], BF16))
        src_ch = s_src.ap().rearrange("p (c t) -> p c t", c=8)

        with nc.Block(f"b_{tag}") as blk:
            @blk.gpsimd
            def _(g):
                wait_finals(g, prev)
                S.inc(g.dma_start(out=s_w.ap().rearrange("p (c n) -> p c n", c=8),
                                  in_=rr8(W_in)), "wld", 16)
                if bias_in is not None:
                    S.inc(g.dma_start(out=s_bias[:, :], in_=bias_in), "wld", 16)

            @blk.tensor
            def _(t):
                wait_finals(t, prev)
                S.wait(t, "wld", S.c["wld"])
                for i in range(nt):
                    for half in range(2):
                        if i >= 1:
                            S.wait(t, "ve", 2 * i - 1 + half)
                        bank = self.pb[half].ap()
                        last = None
                        for c in range(8):
                            last = t.matmul(bank, src_ch[:, c, i * 128:(i + 1) * 128],
                                            s_w[:, c * D + half * 512:c * D + half * 512 + 512],
                                            start=(c == 0), stop=(c == 7 and bias_in is None))
                        if bias_in is not None:
                            last = t.matmul(bank, self.s_ones_bf[:, 0:128],
                                            s_bias[:, half * 512:(half + 1) * 512],
                                            start=False, stop=True)
                        S.inc(last, "mm", 1)

            @blk.sync
            def _(s):
                wait_finals(s, prev)
                def emit_store(i):
                    S.wait(s, "ve", 2 * i + 2)
                    RS.inc(s.dma_start(out=Rnext[i * 128:(i + 1) * 128, :],
                                       in_=s_res[:, (i % 2) * D:(i % 2 + 1) * D]), i)

                for i in range(nt):
                    if i >= 4:
                        emit_store(i - 4)
                        S.wait(s, "ve", 2 * (i - 3))
                    RR.inc(s.dma_start(out=self.s_acc[:, (i % 4) * D:(i % 4 + 1) * D],
                                       in_=Rcur[i * 128:(i + 1) * 128, :]), i)
                for i in range(max(0, nt - 4), nt):
                    emit_store(i)
                if extra_dump is not None:
                    for i in range(nt):
                        RS.wait_done(s, i)
                        S.inc(s.dma_start(out=extra_dump[i * 128:(i + 1) * 128, :],
                                          in_=Rnext[i * 128:(i + 1) * 128, :]), "st", 16)
                for i in range(nt - 2, nt):
                    RS.wait_done(s, i)
                S.wait(s, "st", S.c["st"])

            @blk.vector
            def _(v):
                wait_finals(v, prev)
                for i in range(nt):
                    RR.wait_done(v, i)
                    if i >= 2:
                        RS.wait_done(v, i - 2)
                    for half in range(2):
                        S.wait(v, "mm", 2 * i + half + 1)
                        S.inc(v.tensor_tensor(s_res[:, (i % 2) * D + half * 512:(i % 2) * D + half * 512 + 512],
                                              self.pb[half].ap(),
                                              self.s_acc[:, (i % 4) * D + half * 512:(i % 4) * D + half * 512 + 512],
                                              ALU.add), "ve", 1)
        f = S.finals()
        f.update(RR.finals())
        f.update(RS.finals())
        self.prev = f



def host_consts(T):
    nt = T // 128
    CH = T // NLAT
    c = {}
    c["id_bf"] = np.eye(128, dtype=bf)
    c["id_f32"] = np.eye(128, dtype=np.float32)
    c["ones_bf"] = np.ones((1, 512), dtype=bf)
    c["onesT"] = np.ones((1, T), dtype=bf)
    c["ones_f32"] = np.ones((1, 64), dtype=np.float32)
    mA = np.zeros((4, 128, 512), dtype=np.float32)
    q = np.arange(128)[:, None]
    k = np.arange(128)[None, :]
    tri = np.where(k > q, NEG, 0.0).astype(np.float32)
    for v in range(4):
        for jj in range(4):
            if jj < v:
                pass
            elif jj == v:
                mA[v][:, jj * 128:(jj + 1) * 128] = tri
            else:
                mA[v][:, jj * 128:(jj + 1) * 128] = NEG
    c["maskA"] = mA.reshape(4 * 128, 512).astype(bf)
    c["maskBT"] = np.ascontiguousarray(np.where(k > q, NEG, 0.0).T).astype(bf)
    mL = np.zeros((128, 256), dtype=np.float32)
    mL[:, 0:128] = np.where(k >= q, 0.0, NEG)
    mL[:, 128:256] = np.where(k < q, 0.0, NEG)
    c["maskL"] = mL.astype(bf)
    c["maskLT1"] = np.ascontiguousarray(np.where(k >= q, 0.0, NEG).T).astype(bf)
    c["maskLT2"] = np.ascontiguousarray(np.where(k < q, 0.0, NEG).T).astype(bf)
    lpt = max(1, 128 // CH)
    l = np.arange(NLAT)[None, :]
    p = np.arange(128)[:, None]
    c["latC"] = (l - p // CH).astype(np.float32)
    c["latCT"] = np.ascontiguousarray(c["latC"].T)
    tok = np.arange(nt * 128)[:, None]
    c["ind"] = np.where(np.arange(NLAT)[None, :] == tok // CH, 1.0 / CH, 0.0).astype(bf)
    return c


def host_inputs(inputs, core, T):
    g = lambda n: np.asarray(inputs[n])
    m = {}
    m["x"] = np.ascontiguousarray(g("L0_attn_a_x")[core])
    m["wqkv_a"] = g("L0_attn_a_w_qkv")
    m["wout_a"] = g("L0_attn_a_w_out")
    m["head_scales"] = g("L0_attn_a_head_scales").reshape(NH, 1)
    hd = np.zeros((D, NH), dtype=np.float32)
    hdirs = g("L0_attn_a_head_dirs")
    for h in range(NH):
        hd[h * 64:h * 64 + 8, h] = hdirs[h]
    m["hdblk"] = hd
    m["wqkv_b"] = g("L0_attn_b_qkv_w")
    m["bqkv_b"] = g("L0_attn_b_qkv_b").reshape(1, 3 * D)
    m["wrq"] = g("L0_attn_b_rq_w")
    m["brq"] = g("L0_attn_b_rq_b").reshape(1, LATD)
    m["wrkv"] = g("L0_attn_b_rkv_w")
    m["brkv"] = g("L0_attn_b_rkv_b").reshape(1, 2 * LATD)
    m["wrout"] = g("L0_attn_b_rout_w")
    m["wroutT"] = np.ascontiguousarray(g("L0_attn_b_rout_w").T)
    m["brout"] = g("L0_attn_b_rout_b").reshape(1, D)
    m["wout_b"] = g("L0_attn_b_out_w")
    m["bout_b"] = g("L0_attn_b_out_b").reshape(1, D)
    glb = np.zeros((D, NH), dtype=np.float32)
    grb = np.zeros((D, NH), dtype=np.float32)
    for h in range(NH):
        glb[h * 64:(h + 1) * 64, h] = g("L0_attn_b_gl_w")[:, 0]
        grb[h * 64:(h + 1) * 64, h] = g("L0_attn_b_gr_w")[:, 0]
    m["glblk"] = glb
    m["grblk"] = grb
    w1 = np.concatenate([g("L0_ffn_w1"), g("L0_ffn_new_w1")], axis=0)
    w2 = np.concatenate([g("L0_ffn_w2"), g("L0_ffn_new_w2")], axis=0)
    w3 = np.concatenate([g("L0_ffn_w3"), g("L0_ffn_new_w3")], axis=0)
    m["w1T"] = np.ascontiguousarray(w1.transpose(0, 2, 1))
    m["w3T"] = np.ascontiguousarray(w3.transpose(0, 2, 1))
    m["w2T"] = np.ascontiguousarray(w2.transpose(0, 2, 1))
    gate = np.concatenate([g("L0_ffn_gate_weight"), g("L0_ffn_new_gate_weight")], axis=0)
    m["gateT"] = np.ascontiguousarray(gate.T)
    gb = np.zeros((1, NE), dtype=np.float32)
    gb[0, 8] = g("L0_ffn_gate_bias")[0]
    m["gbias"] = gb
    m.update(host_consts(T))
    return m


def build_program(T, phases=("a", "f1", "b", "f2"), debug_outs=()):
    import contextlib
    P = Prog(T, debug_outs=debug_outs)
    P.make_dram()
    P.make_persist()
    P.setup_phase()
    d = P.d

    def ffn_block(tagp, Rcur, Rnext):
        with contextlib.ExitStack() as c3:
            s_xlnT = c3.enter_context(P.nc.sbuf_tensor(f"{tagp}_xlnT", [128, 8 * T], BF16))
            P.ln_pass(f"{tagp}l", Rcur, s_xlnT, ctx=c3)
            P.proj_spill(f"{tagp}p", s_xlnT, do_route=True, xfm_spill=d["xfm"], ctx=c3)
        with contextlib.ExitStack() as c4:
            P.ffn_experts(f"{tagp}e", Rcur, Rnext, ctx=c4)

    def attnb_block(Rcur, Rnext):
        with contextlib.ExitStack() as cb:
            s_latout = cb.enter_context(P.nc.sbuf_tensor("b_latout", [128, 2 * T], BF16))
            with contextlib.ExitStack() as cmid:
                s_rqT = cmid.enter_context(P.nc.sbuf_tensor("b_rqT", [128, 2 * T], BF16))
                s_pool = cmid.enter_context(P.nc.sbuf_tensor("b_pool", [64, D], BF16))
                with contextlib.ExitStack() as c5:
                    s_xlnT = c5.enter_context(P.nc.sbuf_tensor("b_xlnT", [128, 8 * T], BF16))
                    P.ln_pass("b1l", Rcur, s_xlnT, pooled_out=s_pool, ctx=c5)
                    P.proj_spill("b1p", s_xlnT, W_in=d["wqkv_b"], COLS=3 * D,
                                 bias_in=d["bqkv_b"], fcmodes=("q", "q", "k", "k", "v", "v"),
                                 Qdst=d["Qh"], Kdst=d["Kh"], Vdst=d["Vh"], ctx=c5)
                    P.proj_spill("b1r", s_xlnT, rq=(d["wrq"], d["brq"], s_rqT), ctx=c5)
                with contextlib.ExitStack() as c6:
                    P.latent_attn("bl", s_rqT, s_pool, s_latout, ctx=c6)
            with contextlib.ExitStack() as cloc:
                s_local = cloc.enter_context(P.nc.sbuf_tensor("b_local", [128, 8 * T], BF16))
                s_g1 = cloc.enter_context(P.nc.sbuf_tensor("b_g1", [NH, T], F32))
                with contextlib.ExitStack() as c7:
                    P.local_attn("bo", s_local, s_g1, ctx=c7)
                if P.dbg:
                    dump = []
                    if "d_latout" in P.dbg:
                        dump.append((P.dbg["d_latout"], s_latout, 2))
                    if "d_local" in P.dbg:
                        dump.append((P.dbg["d_local"], s_local, 8))
                    if dump or "d_g1" in P.dbg or "d_pool" in P.dbg:
                        Sd = Sems(P, "bdbg", ["x"])
                        pv = P.prev
                        with P.nc.Block("b_bdbg") as blk:
                            @blk.sync
                            def _(s):
                                wait_finals(s, pv)
                                for dst, srct, ncch in dump:
                                    for c in range(ncch):
                                        Sd.inc(s.dma_start(out=dst[c * 128:(c + 1) * 128, :],
                                                           in_=srct.ap()[:, c * T:(c + 1) * T]),
                                               "x", 16)
                                if "d_g1" in P.dbg:
                                    Sd.inc(s.dma_start(out=P.dbg["d_g1"], in_=s_g1.ap()),
                                           "x", 16)
                                Sd.wait(s, "x", Sd.c["x"])
                        pvf = P.prev.copy()
                        pvf.update(Sd.finals())
                        P.prev = pvf
                with contextlib.ExitStack() as c8:
                    P.mix_proj("b3", s_latout, s_local, s_g1, Rcur, Rnext, ctx=c8)

    with contextlib.ExitStack() as ctx:
        if "a" in phases or "a1" in phases:
            with contextlib.ExitStack() as c1:
                s_xlnT = c1.enter_context(P.nc.sbuf_tensor("a_xlnT", [128, 8 * T], BF16))
                P.ln_pass("a1l", d["x"], s_xlnT, ctx=c1)
                P.proj_spill("a1p", s_xlnT, W_in=d["wqkv_a"], COLS=3 * D, bias_in=None,
                             fcmodes=("q", "q", "k", "k", "v", "v"),
                             Qdst=d["Qh"], Kdst=d["Kh"], Vdst=d["Vh"], do_pp=True,
                             ctx=c1, dbg_xlnT=("d_xlnT" in P.dbg))
        if "a3x" in phases:
            with contextlib.ExitStack() as c2b:
                s_ao = c2b.enter_context(P.nc.sbuf_tensor("a_ao", [128, 8 * T], BF16))
                Si = Sems(P, "ini", ["x"])
                pv = P.prev
                with P.nc.Block("b_ini") as blk:
                    @blk.vector
                    def _(v):
                        wait_finals(v, pv)
                        Si.inc(v.memset(s_ao.ap(), 0.25), "x", 1)
                pp = P.prev.copy()
                pp.update(Si.finals())
                P.prev = pp
                P.proj_residual("a3", s_ao, d["wout_a"], None, d["x"], d["Ra"], ctx=c2b)
        if "a" in phases or "a2" in phases:
            with contextlib.ExitStack() as c2:
                s_ao = c2.enter_context(P.nc.sbuf_tensor("a_ao", [128, 8 * T], BF16))
                P.attn_a_heads(s_ao, ctx=c2)
                if "a" in phases or "a3" in phases:
                    P.proj_residual("a3", s_ao, d["wout_a"], None, d["x"], d["Ra"], ctx=c2,
                                    extra_dump=P.dbg.get("d_Ra"))
        if "f1r" in phases:
            with contextlib.ExitStack() as c3:
                s_xlnT = c3.enter_context(P.nc.sbuf_tensor("f1_xlnT", [128, 8 * T], BF16))
                P.ln_pass("f1l", d["Ra"], s_xlnT, ctx=c3)
                P.proj_spill("f1p", s_xlnT, do_route=True, xfm_spill=d["xfm"], ctx=c3)
        if "f2o" in phases:
            with contextlib.ExitStack() as c4:
                P.ffn_experts("f2x", d["Ra"], d["Rb"], ctx=c4)
        if "f1" in phases:
            ffn_block("f1", d["Ra"], d["Rb"])
            if "d_cw" in P.dbg:
                S = Sems(P, "fdbg", ["x"])
                pv = P.prev
                with P.nc.Block("b_fdbg") as blk:
                    @blk.sync
                    def _(s):
                        wait_finals(s, pv)
                        S.inc(s.dma_start(out=P.dbg["d_cw"], in_=P.s_cw[:, 0:P.nt * NE]), "x", 16)
                        S.wait(s, "x", 16)
                pvf = P.prev.copy()
                pvf.update(S.finals())
                P.prev = pvf
        if "b" in phases:
            Rcur = d["Rb"] if "f1" in phases else d["Ra"]
            attnb_block(Rcur, d["Ra"] if "f1" in phases else d["Rb"])
        if "f2" in phases:
            ffn_block("f2", d["Ra"], d["out"])
    P.top.close()
    return P


def kernel(**inputs):
    T = 4096
    P = build_program(T, phases=("a", "f1", "b", "f2"))
    in_maps = [host_inputs(inputs, c, T) for c in range(8)]
    res = run_bass_kernel_spmd(P.nc, in_maps, core_ids=list(range(8)))
    return np.stack([np.asarray(r["out"]) for r in res.results], axis=0)


def _ffn_experts(self, tag, Rcur, Rnext, ctx=None):
    nc, nt, T, ng = self.nc, self.nt, self.T, self.ng
    d = self.d
    S = Sems(self, tag, ["wl", "wl2", "hm", "sl", "ml", "ym", "yv", "e1"],
             sw=("wl", "wl2"))
    RX = Ring(self, f"{tag}rx", 2, base_idx=16)
    RA = Ring(self, f"{tag}ra", 4, base_idx=18)
    RST = Ring(self, f"{tag}rs", 4, base_idx=22)
    prev = self.prev
    s_w1 = ctx.enter_context(nc.sbuf_tensor(f"{tag}_w1", [128, 8 * HID], BF16))
    s_w3 = ctx.enter_context(nc.sbuf_tensor(f"{tag}_w3", [128, 8 * HID], BF16))
    s_w2 = ctx.enter_context(nc.sbuf_tensor(f"{tag}_w2", [128, 22 * D], BF16))
    s_x = ctx.enter_context(nc.sbuf_tensor(f"{tag}_x", [128, 2 * 8 * 512], BF16))
    s_h = ctx.enter_context(nc.sbuf_tensor(f"{tag}_h", [128, 11 * 512], BF16))
    s_sl = ctx.enter_context(nc.sbuf_tensor(f"{tag}_sl", [128, 2 * 512], F32))

    def hm_end(e):
        return (e + 1) * ng * 44

    def ym_end(e):
        return (e + 1) * ng * 8

    with nc.Block(f"b_{tag}") as blk:
        @blk.gpsimd
        def _(g):
            wait_finals(g, prev)
            g.wait_ge(*self.wcast)
            for e in range(NE):
                if e >= 1:
                    S.wait(g, "hm", hm_end(e - 1))
                S.inc(g.dma_start(out=s_w1.ap().rearrange("p (c n) -> p c n", c=8),
                                  in_=rr8(d["w1b"][e])), "wl", 16)
                S.inc(g.dma_start(out=s_w3.ap().rearrange("p (c n) -> p c n", c=8),
                                  in_=rr8(d["w3b"][e])), "wl", 16)
                if e >= 1:
                    S.wait(g, "ym", ym_end(e - 1))
                S.inc(g.dma_start(out=s_w2.ap().rearrange("p (c n) -> p c n", c=22),
                                  in_=rr8(d["w2b"][e], 22)), "wl2", 16)

        @blk.sync
        def _(s):
            wait_finals(s, prev)
            def stv(e, i):
                return e * 16 * ng + (i // 4) * 16 + 8 + 2 * ((i % 4) + 1)

            def emit_store(s, e, i):
                it = e * nt + i
                S.wait(s, "yv", stv(e, i))
                RST.inc(s.dma_start(out=Rnext[i * 128:(i + 1) * 128, :],
                                    in_=self.s_acc[:, (i % 4) * D:(i % 4 + 1) * D]), it)

            for e in range(NE):
                rsrc = Rcur if e == 0 else Rnext
                for gi in range(ng):
                    it = e * ng + gi
                    if it >= 2:
                        S.wait(s, "hm", 44 * (it - 1))
                    RX.inc(s.dma_start(
                        out=s_x.ap().rearrange("p (b c n) -> p b c n", b=2, c=8)[:, it % 2],
                        in_=d["xfm"][:, gi * 512:(gi + 1) * 512].rearrange("(c p) n -> p c n", p=128)), it)
                    for ii in range(4):
                        i = 4 * gi + ii
                        if i >= 4:
                            emit_store(s, e, i - 4)
                            RST.wait_done(s, e * nt + i - 4)
                        elif e > 0:
                            emit_store(s, e - 1, nt - 4 + ii)
                            RST.wait_done(s, (e - 1) * nt + nt - 4 + ii)
                        if e > 0:
                            RST.wait_done(s, (e - 1) * nt + i)
                        RA.inc(s.dma_start(out=self.s_acc[:, (i % 4) * D:(i % 4 + 1) * D],
                                           in_=rsrc[i * 128:(i + 1) * 128, :]), e * nt + i)
            for i in range(nt - 4, nt):
                emit_store(s, NE - 1, i)
            for it in range(NE * nt - 4, NE * nt):
                RST.wait_done(s, it)

        @blk.tensor
        def _(t):
            wait_finals(t, prev)
            jj_u = 0
            y_u = 0
            for e in range(NE):
                S.wait(t, "wl", 32 * (e + 1))
                w2_waited = False
                for gi in range(ng):
                    it = e * ng + gi
                    RX.wait_done(t, it)
                    xs = s_x.ap().rearrange("p (b c n) -> p b c n", b=2, c=8)[:, it % 2]
                    for jh in range(2):
                        for jj in range(11):
                            j = jh * 11 + jj
                            bA = self.pb[2 * (jj_u % 2)].ap()
                            bB = self.pb[2 * (jj_u % 2) + 1].ap()
                            if jj_u >= 2:
                                S.wait(t, "ml", jj_u - 1)
                            last = None
                            for c in range(8):
                                last = t.matmul(bA, s_w1[:, c * HID + j * 128:c * HID + (j + 1) * 128],
                                                xs[:, c, :], start=(c == 0), stop=(c == 7))
                            S.inc(last, "hm", 1)
                            for c in range(8):
                                last = t.matmul(bB, s_w3[:, c * HID + j * 128:c * HID + (j + 1) * 128],
                                                xs[:, c, :], start=(c == 0), stop=(c == 7))
                            S.inc(last, "hm", 1)
                            jj_u += 1
                        if not w2_waited:
                            S.wait(t, "wl2", 16 * (e + 1))
                            w2_waited = True
                        S.wait(t, "ml", jj_u)
                        for ii in range(4):
                            bY0 = self.pb[4 + 2 * (ii % 2)].ap()
                            bY1 = self.pb[5 + 2 * (ii % 2)].ap()
                            if y_u >= 2:
                                S.wait(t, "yv", 2 * (y_u - 1))
                            for jj in range(11):
                                j = jh * 11 + jj
                                lh = s_h[:, jj * 512 + ii * 128:jj * 512 + (ii + 1) * 128]
                                t.matmul(bY0, lh, s_w2[:, j * D:j * D + 512],
                                         start=(jj == 0), stop=(jj == 10))
                                last = t.matmul(bY1, lh, s_w2[:, j * D + 512:(j + 1) * D],
                                                start=(jj == 0), stop=(jj == 10))
                            S.inc(last, "ym", 1)
                            y_u += 1

        @blk.scalar
        def _(sc):
            wait_finals(sc, prev)
            jj_u = 0
            for e in range(NE):
                for gi in range(ng):
                    for jh in range(2):
                        for jj in range(11):
                            S.wait(sc, "hm", 2 * jj_u + 1)
                            if jj_u >= 2:
                                S.wait(sc, "ml", jj_u - 1)
                            bA = self.pb[2 * (jj_u % 2)].ap()
                            S.inc(sc.activation(s_sl[:, (jj_u % 2) * 512:(jj_u % 2 + 1) * 512],
                                                bA, AF.Silu), "sl", 1)
                            jj_u += 1

        @blk.vector
        def _(v):
            wait_finals(v, prev)
            jj_u = 0
            y_u = 0
            for e in range(NE):
                for gi in range(ng):
                    for jh in range(2):
                        for jj in range(11):
                            if jj == 0:
                                S.wait(v, "ym", y_u)
                            S.wait(v, "sl", jj_u + 1)
                            S.wait(v, "hm", 2 * jj_u + 2)
                            bB = self.pb[2 * (jj_u % 2) + 1].ap()
                            S.inc(v.tensor_tensor(s_h[:, jj * 512:(jj + 1) * 512],
                                                  s_sl[:, (jj_u % 2) * 512:(jj_u % 2 + 1) * 512],
                                                  bB, ALU.mult), "ml", 1)
                            jj_u += 1
                        for ii in range(4):
                            i = gi * 4 + ii
                            S.wait(v, "ym", y_u + 1)
                            RA.wait_done(v, e * nt + i)
                            bY0 = self.pb[4 + 2 * (ii % 2)].ap()
                            bY1 = self.pb[5 + 2 * (ii % 2)].ap()
                            cwc = self.s_cw[:, i * NE + e:i * NE + e + 1]
                            accl = self.s_acc[:, (i % 4) * D:(i % 4) * D + 512]
                            acch = self.s_acc[:, (i % 4) * D + 512:(i % 4 + 1) * D]
                            iv = v.scalar_tensor_tensor(accl, bY0, cwc, accl, ALU.mult, ALU.add)
                            S.inc(iv, "yv", 1)
                            S.wait(v, "yv", S.c["yv"])
                            iv = v.scalar_tensor_tensor(acch, bY1, cwc, acch, ALU.mult, ALU.add)
                            S.inc(iv, "yv", 1)
                            S.wait(v, "yv", S.c["yv"])
                            y_u += 1
    f = S.finals()
    for r in (RX, RA, RST):
        f.update(r.finals())
    self.prev = f


Prog.ffn_experts = _ffn_experts


def _latent_attn(self, tag, s_rqT, s_pool, s_latout, ctx=None):
    nc, nt, T = self.nc, self.nt, self.T
    d = self.d
    LPT = 128 // self.CH if self.CH <= 128 else 0
    S = Sems(self, tag, ["wl", "st", "se", "rk", "pa", "rm", "sx", "sro", "sri",
                         "pbm", "mk", "ex", "pv", "nz", "lx", "e1", "ms", "qa"],
             sw=("wl", "sri"))
    prev = self.prev
    s_rkvw = ctx.enter_context(nc.sbuf_tensor(f"{tag}_rkvw", [128, 8 * 512], BF16))
    s_rkvb = ctx.enter_context(nc.sbuf_tensor(f"{tag}_rkvb", [1, 512], BF16))
    s_poolT = ctx.enter_context(nc.sbuf_tensor(f"{tag}_poolT", [128, 512], BF16))
    s_rkv = ctx.enter_context(nc.sbuf_tensor(f"{tag}_rkv", [64, 512], BF16))
    s_rkA = ctx.enter_context(nc.sbuf_tensor(f"{tag}_rkA", [65, 4 * 64], BF16))
    s_rkP = ctx.enter_context(nc.sbuf_tensor(f"{tag}_rkP", [128, 4 * 64], BF16))
    s_rva = ctx.enter_context(nc.sbuf_tensor(f"{tag}_rva", [64, 4 * 65], BF16))
    s_rqa = [ctx.enter_context(nc.sbuf_tensor(f"{tag}_rqa{p}", [65, T], BF16)) for p in range(2)]
    s_cmpT = ctx.enter_context(nc.sbuf_tensor(f"{tag}_cmpT", [64, 128], F32))
    s_lm = ctx.enter_context(nc.sbuf_tensor(f"{tag}_lm", [128, 64], F32))
    s_cmp = ctx.enter_context(nc.sbuf_tensor(f"{tag}_cmp", [128, 64], F32))
    s_pt = ctx.enter_context(nc.sbuf_tensor(f"{tag}_pt", [64, 2 * 128], BF16))

    with nc.Block(f"b_{tag}") as blk:
        @blk.gpsimd
        def _(g):
            wait_finals(g, prev)
            S.inc(g.dma_start(out=s_rkvw.ap().rearrange("p (c n) -> p c n", c=8),
                              in_=rr8(d["wrkv"])), "wl", 16)
            S.inc(g.dma_start(out=s_rkvb[:, :], in_=d["brkv"]), "wl", 16)
            for lh in range(NLH):
                S.wait(g, "sro", 16 * (lh + 1))
                if lh >= 2:
                    S.wait(g, "pbm", (lh - 1) * nt)
                S.inc(g.dma_start(out=s_rqa[lh % 2].ap()[64:65, :],
                                  in_=d["stats"].rearrange("a b -> (a b)").unsqueeze(0)),
                      "sri", 16)

        @blk.sync
        def _(s):
            wait_finals(s, prev)
            for lh in range(NLH):
                S.wait(s, "sx", 3 * lh + 3)
                S.wait(s, "sri", 16 * lh)
                S.inc(s.dma_start(out=d["stats"], in_=self.s_row[0:nt, :]), "sro", 16)

        @blk.tensor
        def _(t):
            wait_finals(t, prev)
            S.wait(t, "ms", 1)
            b6 = self.pb[6].ap().bitcast(BF16)
            for c in range(8):
                iv = t.transpose(b6[:, c * 64:(c + 1) * 64],
                                 s_pool.ap()[0:64, c * 128:(c + 1) * 128],
                                 self.s_id_bf[0:64, 0:64])
            S.inc(iv, "st", 1)
            S.wait(t, "wl", 32)
            S.wait(t, "se", 1)
            last = None
            for c in range(8):
                last = t.matmul(self.pb[0].ap()[0:64, :], s_poolT[:, c * 64:(c + 1) * 64],
                                s_rkvw[:, c * 512:(c + 1) * 512], start=(c == 0), stop=False)
            last = t.matmul(self.pb[0].ap()[0:64, :], self.s_ones_bf[:, 0:64],
                            s_rkvb[:, :], start=False, stop=True)
            S.inc(last, "rk", 1)
            S.wait(t, "se", 2)
            for lh in range(NLH):
                iv = t.transpose(b6[0:64, 512 + lh * 64:512 + (lh + 1) * 64],
                                 s_rkv.ap()[0:64, lh * 64:(lh + 1) * 64],
                                 self.s_id_bf[0:64, 0:64])
            S.inc(iv, "st", 1)
            S.wait(t, "se", 5)
            S.wait(t, "ms", 1)
            au = 0
            for i in range(nt):
                for lh in range(NLH):
                    if au >= 2:
                        S.wait(t, "rm", au - 1)
                    bank = self.pb[au % 2].ap()[:, 0:64]
                    S.inc(t.matmul(bank,
                                   s_rqT.ap()[(lh % 2) * 64:(lh % 2) * 64 + 64,
                                              (lh // 2) * T + i * 128:(lh // 2) * T + (i + 1) * 128],
                                   s_rkP.ap()[(lh % 2) * 64:(lh % 2) * 64 + 64,
                                              lh * 64:(lh + 1) * 64],
                                   start=True, stop=True), "pa", 1)
                    au += 1
            bu = 0
            for lh in range(NLH):
                S.wait(t, "sx", 3 * lh + 1)
                S.inc(t.transpose(self.pb[6].ap()[0:nt, 0:128], self.s_mneg[:, 0:nt],
                                  self.s_id_f32[:, :]), "sx", 1)
                S.wait(t, "sri", 16 * (lh + 1))
                S.wait(t, "qa", lh + 1)
                for i in range(nt):
                    if bu >= 2:
                        S.wait(t, "ex", bu - 1)
                    bank = self.pb[2 + bu % 2].ap()
                    S.inc(t.matmul(bank[0:64, 0:128],
                                   s_rkA.ap()[0:65, lh * 64:(lh + 1) * 64],
                                   s_rqa[lh % 2].ap()[0:65, i * 128:(i + 1) * 128],
                                   start=True, stop=True), "pbm", 1)
                    S.wait(t, "ex", bu + 1)
                    if bu >= 1:
                        S.wait(t, "nz", 3 * bu)
                    S.inc(t.matmul(self.pb[4].ap()[0:65, 0:128],
                                   s_rva.ap()[0:64, lh * 65:lh * 65 + 65],
                                   s_pt[0:64, (bu % 2) * 128:(bu % 2 + 1) * 128],
                                   start=True, stop=True), "pv", 1)
                    S.wait(t, "nz", 3 * bu + 1)
                    S.inc(t.matmul(self.pb[5].ap()[0:64, 0:128], self.s_ones_f32[0:1, 0:64],
                                   self.s_linv[0:1, 0:128], start=True, stop=True), "lx", 1)
                    bu += 1

        @blk.vector
        def _(v):
            wait_finals(v, prev)
            iv = v.memset(s_rkA.ap()[64:65, :], 1.0)
            iv = v.memset(s_rva.ap().rearrange("p (l c) -> p l c", c=65)[:, :, 64:65], 1.0)
            S.inc(iv, "ms", 1)
            S.wait(v, "st", 1)
            S.inc(v.tensor_copy(s_poolT[:, :], self.pb[6].ap().bitcast(BF16)[:, 0:512]),
                  "se", 1)
            S.wait(v, "rk", 1)
            S.inc(v.tensor_copy(s_rkv[:, :], self.pb[0].ap()[0:64, :]), "se", 1)
            S.wait(v, "st", 2)
            S.inc(v.tensor_copy(s_rkA.ap()[0:64, :],
                                self.pb[6].ap().bitcast(BF16)[0:64, 512:768]), "se", 1)
            for lh in range(NLH):
                iv = v.tensor_copy(s_rkP.ap()[(lh % 2) * 64:(lh % 2) * 64 + 64,
                                              lh * 64:(lh + 1) * 64],
                                   self.pb[6].ap().bitcast(BF16)[0:64, 512 + lh * 64:512 + (lh + 1) * 64])
            S.inc(iv, "se", 1)
            for lh in range(NLH):
                iv = v.tensor_copy(s_rva.ap()[0:64, lh * 65:lh * 65 + 64],
                                   s_rkv[0:64, 256 + lh * 64:256 + (lh + 1) * 64])
            S.inc(iv, "se", 1)
            au = 0
            for i in range(nt):
                S.wait(v, "rm", max(0, au - 1))
                S.edge(v, "e1", v.tensor_single_scalar(s_cmp[:, :], self.s_latC[:, :],
                                                       i * LPT - 0.5, ALU.is_ge))
                for lh in range(NLH):
                    S.wait(v, "pa", au + 1)
                    bank = self.pb[au % 2].ap()[:, 0:64]
                    S.edge(v, "e1", v.scalar_tensor_tensor(s_lm[:, :], s_cmp[:, :], NEG,
                                                           bank, ALU.mult, ALU.add))
                    iv = v.tensor_reduce(self.s_stat[:, lh * nt + i:lh * nt + i + 1],
                                         s_lm[:, :], AX.X, ALU.max)
                    S.inc(iv, "rm", 1)
                    S.wait(v, "rm", S.c["rm"])
                    au += 1
            bu = 0
            for lh in range(NLH):
                if lh >= 1:
                    S.wait(v, "sx", 3 * (lh - 1) + 2)
                S.edge(v, "e1", v.tensor_scalar(self.s_mneg[:, 0:nt],
                                                self.s_stat[:, lh * nt:(lh + 1) * nt],
                                                -1e8, -1.0, ALU.max, ALU.mult))
                S.inc(v.nop(), "sx", 1)
                S.wait(v, "sx", 3 * lh + 2)
                S.wait(v, "sro", 16 * lh)
                S.inc(v.tensor_copy(self.s_row[0:nt, :], self.pb[6].ap()[0:nt, 0:128]),
                      "sx", 1)
                if lh >= 2:
                    S.wait(v, "pbm", (lh - 1) * nt)
                S.inc(v.tensor_copy(s_rqa[lh % 2].ap()[0:64, :],
                                    s_rqT.ap()[(lh % 2) * 64:(lh % 2) * 64 + 64,
                                               (lh // 2) * T:(lh // 2) * T + T]), "qa", 1)
                for i in range(nt):
                    S.wait(v, "pbm", bu + 1)
                    S.edge(v, "e1", v.tensor_single_scalar(s_cmpT[:, :], self.s_latCT[:, :],
                                                           i * LPT - 0.5, ALU.is_ge))
                    bank = self.pb[2 + bu % 2].ap()
                    S.inc(v.scalar_tensor_tensor(bank[0:64, 0:128], s_cmpT[:, :], NEG,
                                                 bank[0:64, 0:128], ALU.mult, ALU.add),
                          "mk", 1)
                    S.wait(v, "mk", S.c["mk"])
                    S.wait(v, "pv", bu + 1)
                    S.edge(v, "e1", v.tensor_scalar(self.s_linv[0:1, 0:128],
                                                    self.pb[4].ap()[64:65, 0:128],
                                                    1e-30, None, ALU.max))
                    S.inc(v.reciprocal(self.s_linv[0:1, 0:128], self.s_linv[0:1, 0:128]),
                          "nz", 1)
                    S.wait(v, "lx", bu + 1)
                    iv = v.tensor_copy(self.s_lexp[0:64, 0:128], self.pb[5].ap()[0:64, 0:128])
                    S.inc(iv, "nz", 1)
                    S.wait(v, "nz", S.c["nz"])
                    dst = s_latout.ap()[(lh % 2) * 64:(lh % 2) * 64 + 64,
                                        (lh // 2) * T + i * 128:(lh // 2) * T + (i + 1) * 128]
                    S.inc(v.tensor_tensor(dst, self.pb[4].ap()[0:64, 0:128],
                                          self.s_lexp[0:64, 0:128], ALU.mult), "nz", 1)
                    bu += 1
            S.wait(v, "nz", S.c["nz"])
            S.inc(v.memset(s_latout.ap().rearrange("p (c t) -> p c t", c=2)[:, :, 0:self.CH],
                           0.0), "nz", 1)

        @blk.scalar
        def _(sc):
            wait_finals(sc, prev)
            bu = 0
            for lh in range(NLH):
                for i in range(nt):
                    S.wait(sc, "pbm", bu + 1)
                    S.wait(sc, "mk", bu + 1)
                    if bu >= 2:
                        S.wait(sc, "pv", bu - 1)
                    bank = self.pb[2 + bu % 2].ap()
                    S.inc(sc.activation(s_pt[0:64, (bu % 2) * 128:(bu % 2 + 1) * 128],
                                        bank[0:64, 0:128], AF.Exp), "ex", 1)
                    bu += 1

    f = S.finals()
    self.prev = f


Prog.latent_attn = _latent_attn


def _local_attn(self, tag, s_local, s_g1, ctx=None):
    nc, nt, T = self.nc, self.nt, self.T
    d = self.d
    S = Sems(self, tag, ["pa", "rm", "sx", "sro", "sri", "pbm", "mk", "ex", "pv",
                         "nz", "lx", "ms", "e1", "g1m", "g1v"], sw=("sri",))
    RK = Ring(self, f"{tag}kld", 2, base_idx=16)
    prev = self.prev
    s_K = [ctx.enter_context(nc.sbuf_tensor(f"{tag}_K{p}", [65, T], BF16)) for p in range(2)]
    s_Q = [ctx.enter_context(nc.sbuf_tensor(f"{tag}_Q{p}", [65, T], BF16)) for p in range(2)]
    s_V = [ctx.enter_context(nc.sbuf_tensor(f"{tag}_V{p}", [128, nt * 65], BF16)) for p in range(2)]
    s_pt = ctx.enter_context(nc.sbuf_tensor(f"{tag}_pt", [128, 2 * 128], BF16))
    s_mskl = ctx.enter_context(nc.sbuf_tensor(f"{tag}_mskl", [128, 256], F32))

    nbu_h = 2 * nt - 1
    pbm_end = [nbu_h * (h + 1) for h in range(NH)]
    pv_end = [nbu_h * (h + 1) for h in range(NH)]

    def bui(h, t, j):
        return h * nbu_h + (0 if t == 0 else 2 * t - 1) + (j if t > 0 else 0)

    with nc.Block(f"b_{tag}") as blk:
        @blk.sync
        def _(s):
            wait_finals(s, prev)
            for p in range(2):
                S.inc(s.dma_start(out=s_K[p].ap()[64:65, :], in_=d["onesT"]), "ms", 16)
            for h in range(NH):
                par = h % 2
                if h >= 2:
                    S.wait(s, "pbm", pbm_end[h - 2])
                    S.wait(s, "pv", pv_end[h - 2])
                RK.inc(s.dma_start(out=s_K[par].ap()[0:64, :],
                                   in_=d["Kh"][h * 64:(h + 1) * 64, :]), h)
                RK.inc(s.dma_start(out=s_Q[par].ap()[0:64, :],
                                   in_=d["Qh"][h * 64:(h + 1) * 64, :]), h)
                RK.inc(s.dma_start(out=s_V[par].ap().rearrange("p (t c) -> p t c", c=65)[:, :, 0:64],
                                   in_=d["Vh"][:, h * 64:(h + 1) * 64].rearrange("(t p) d -> p t d", p=128)),
                       h)
                S.wait(s, "sx", 3 * h + 3)
                S.wait(s, "sri", 16 * h)
                S.inc(s.dma_start(out=d["stats"], in_=self.s_row[0:nt, :]), "sro", 16)

        @blk.gpsimd
        def _(g):
            wait_finals(g, prev)
            for h in range(NH):
                par = h % 2
                S.wait(g, "sro", 16 * (h + 1))
                if h >= 2:
                    S.wait(g, "pbm", pbm_end[h - 2])
                S.inc(g.dma_start(out=s_Q[par].ap()[64:65, :],
                                  in_=d["stats"].rearrange("a b -> (a b)").unsqueeze(0)),
                      "sri", 16)

        @blk.tensor
        def _(t):
            wait_finals(t, prev)
            S.wait(t, "ms", 33)
            au = 0
            bu = 0

            def pe_passA(h):
                nonlocal au
                par = h % 2
                RK.wait_done(t, h)
                chunk = max(1, (nt + 7) // 8)
                for i in range(nt):
                    if au >= 3:
                        S.wait(t, "rm", au - 2)
                    bank = [self.pb[0], self.pb[1], self.pb[7]][au % 3].ap()
                    k0 = max(0, (i - 1) * 128)
                    N = 256 if i > 0 else 128
                    S.inc(t.matmul(bank[:, 0:N], s_Q[par].ap()[0:64, i * 128:(i + 1) * 128],
                                   s_K[par].ap()[0:64, k0:k0 + N],
                                   start=True, stop=True), "pa", 1)
                    au += 1
                    if (i + 1) % chunk == 0:
                        yield
                S.wait(t, "sx", 3 * h + 1)
                S.inc(t.transpose(self.pb[6].ap()[0:nt, 0:128], self.s_mneg[:, 0:nt],
                                  self.s_id_f32[:, :]), "sx", 1)
                while True:
                    yield

            def pe_passB(h):
                nonlocal bu
                par = h % 2
                S.wait(t, "sri", 16 * (h + 1))
                chunk = max(1, (nt + 7) // 8)
                for i in range(nt):
                    kts = (i,) if i == 0 else (i - 1, i)
                    for j, kt in enumerate(kts):
                        if bu >= 2:
                            S.wait(t, "ex", bu - 1)
                        bank = self.pb[2 + bu % 2].ap()
                        S.inc(t.matmul(bank[:, 0:128],
                                       s_K[par].ap()[0:65, kt * 128:(kt + 1) * 128],
                                       s_Q[par].ap()[0:65, i * 128:(i + 1) * 128],
                                       start=True, stop=True), "pbm", 1)
                        S.wait(t, "ex", bu + 1)
                        if j == 0 or i == 0:
                            S.wait(t, "nz", 3 * (h * nt + i))
                        S.inc(t.matmul(self.pb[4].ap()[0:65, 0:128],
                                       s_V[par].ap()[:, kt * 65:kt * 65 + 65],
                                       s_pt[:, (bu % 2) * 128:(bu % 2 + 1) * 128],
                                       start=(j == 0), stop=(j == len(kts) - 1)), "pv", 1)
                        bu += 1
                    S.wait(t, "nz", 3 * (h * nt + i) + 1)
                    S.inc(t.matmul(self.pb[5].ap()[0:64, 0:128], self.s_ones_f32[0:1, 0:64],
                                   self.s_linv[0:1, 0:128], start=True, stop=True), "lx", 1)
                    if (i + 1) % chunk == 0:
                        yield

            for h in range(NH):
                ga = pe_passA(h)
                gb = pe_passB(h - 1) if h >= 1 else None
                for _ in range(8):
                    next(ga)
                    if gb is not None:
                        try:
                            next(gb)
                        except StopIteration:
                            pass
                next(ga)
                if gb is not None:
                    for _ in range(8):
                        try:
                            next(gb)
                        except StopIteration:
                            break
            gb = pe_passB(NH - 1)
            for _ in range(9):
                try:
                    next(gb)
                except StopIteration:
                    break
            S.wait(t, "nz", 3 * NH * nt + 1)
            for ts in range(T // 512):
                if ts >= 2:
                    S.wait(t, "g1v", ts - 1)
                bank = self.pb[0 + ts % 2].ap()[0:NH, :]
                for c in range(8):
                    iv = t.matmul(bank, self.s_gl[:, c * NH:(c + 1) * NH],
                                  s_local.ap()[:, c * T + ts * 512:c * T + ts * 512 + 512],
                                  start=(c == 0), stop=(c == 7))
                S.inc(iv, "g1m", 1)

        @blk.vector
        def _(v):
            wait_finals(v, prev)
            for p in range(2):
                iv = v.memset(s_V[p].ap().rearrange("p (t c) -> p t c", c=65)[:, :, 64:65], 1.0)
            S.inc(iv, "ms", 1)
            au = 0

            def dve_passA(h):
                nonlocal au
                chunk = max(1, (nt + 7) // 8)
                for i in range(nt):
                    S.wait(v, "pa", au + 1)
                    bank = [self.pb[0], self.pb[1], self.pb[7]][au % 3].ap()
                    N = 256 if i > 0 else 128
                    iv = v.tensor_reduce(self.s_m[:, i:i + 1], bank[:, 0:N], AX.X, ALU.max)
                    S.inc(iv, "rm", 1)
                    S.wait(v, "rm", S.c["rm"])
                    au += 1
                    if (i + 1) % chunk == 0:
                        yield
                if h >= 1:
                    S.wait(v, "sx", 3 * (h - 1) + 2)
                S.inc(v.tensor_scalar(self.s_mneg[:, 0:nt], self.s_m[:, 0:nt],
                                      -1e8, -1.0, ALU.max, ALU.mult), "sx", 1)
                S.wait(v, "sx", 3 * h + 2)
                S.wait(v, "sro", 16 * h)
                S.inc(v.tensor_copy(self.s_row[0:nt, :], self.pb[6].ap()[0:nt, 0:128]),
                      "sx", 1)
                while True:
                    yield

            def dve_passB(h):
                chunk = max(1, (nt + 7) // 8)
                for i in range(nt):
                    kts = (i,) if i == 0 else (i - 1, i)
                    for j, kt in enumerate(kts):
                        bu = bui(h, i, j if i > 0 else 0)
                        S.wait(v, "pbm", bu + 1)
                        bank = self.pb[2 + bu % 2].ap()
                        mt = self.s_maskLT1 if (i > 0 and j == 0) else self.s_maskLT2
                        S.inc(v.tensor_tensor(bank[0:128, 0:128], bank[0:128, 0:128],
                                              mt[:, :], ALU.add), "mk", 1)
                        S.wait(v, "mk", S.c["mk"])
                    qn = h * nt + i
                    S.wait(v, "pv", bui(h, i, 1 if i > 0 else 0) + 1)
                    if qn >= 1:
                        S.wait(v, "lx", qn)
                    S.edge(v, "e1", v.tensor_scalar(self.s_linv[0:1, 0:128],
                                                    self.pb[4].ap()[64:65, 0:128],
                                                    1e-30, None, ALU.max))
                    S.inc(v.reciprocal(self.s_linv[0:1, 0:128], self.s_linv[0:1, 0:128]),
                          "nz", 1)
                    S.wait(v, "lx", qn + 1)
                    iv = v.tensor_copy(self.s_lexp[0:64, 0:128], self.pb[5].ap()[0:64, 0:128])
                    S.inc(iv, "nz", 1)
                    S.wait(v, "nz", S.c["nz"])
                    dst = s_local.ap()[(h % 2) * 64:(h % 2) * 64 + 64,
                                       (h // 2) * T + i * 128:(h // 2) * T + (i + 1) * 128]
                    S.inc(v.tensor_tensor(dst, self.pb[4].ap()[0:64, 0:128],
                                          self.s_lexp[0:64, 0:128], ALU.mult), "nz", 1)
                    if (i + 1) % chunk == 0:
                        yield

            for h in range(NH):
                ga = dve_passA(h)
                gb = dve_passB(h - 1) if h >= 1 else None
                for _ in range(8):
                    next(ga)
                    if gb is not None:
                        try:
                            next(gb)
                        except StopIteration:
                            pass
                next(ga)
                if gb is not None:
                    for _ in range(8):
                        try:
                            next(gb)
                        except StopIteration:
                            break
            gb = dve_passB(NH - 1)
            for _ in range(9):
                try:
                    next(gb)
                except StopIteration:
                    break
            S.wait(v, "nz", S.c["nz"])
            S.inc(v.memset(s_local.ap().rearrange("p (c t) -> p c t", c=8)[:, :, 0:1], 0.0),
                  "nz", 1)
            for ts in range(T // 512):
                S.wait(v, "g1m", ts + 1)
                bank = self.pb[0 + ts % 2].ap()[0:NH, :]
                S.inc(v.tensor_copy(s_g1.ap()[0:NH, ts * 512:(ts + 1) * 512], bank),
                      "g1v", 1)

        @blk.scalar
        def _(sc):
            wait_finals(sc, prev)
            bu = 0
            for h in range(NH):
                for i in range(nt):
                    kts = (i,) if i == 0 else (i - 1, i)
                    for j, kt in enumerate(kts):
                        S.wait(sc, "pbm", bu + 1)
                        S.wait(sc, "mk", bu + 1)
                        if bu >= 2:
                            S.wait(sc, "pv", bu - 1)
                        bank = self.pb[2 + bu % 2].ap()
                        S.inc(sc.activation(s_pt[:, (bu % 2) * 128:(bu % 2 + 1) * 128],
                                            bank[:, 0:128], AF.Exp), "ex", 1)
                        bu += 1
    f = S.finals()
    f.update(RK.finals())
    self.prev = f


Prog.local_attn = _local_attn


def _mix_proj(self, tag, s_latout, s_local, s_g1, Rcur, Rnext, ctx=None):
    nc, nt, T = self.nc, self.nt, self.T
    d = self.d
    S = Sems(self, tag, ["wl", "gq", "mm", "g2", "gt", "ev", "mx", "tr", "mtv",
                         "pm", "ve", "e1", "sg"], sw=("wl",))
    RR = Ring(self, f"{tag}rr", 4, base_idx=16)
    RS = Ring(self, f"{tag}rs", 2, base_idx=20)
    prev = self.prev
    s_ro = ctx.enter_context(nc.sbuf_tensor(f"{tag}_ro", [128, 2 * D], BF16))
    s_rot = ctx.enter_context(nc.sbuf_tensor(f"{tag}_rot", [128, 8 * LATD], BF16))
    s_rogr = ctx.enter_context(nc.sbuf_tensor(f"{tag}_rogr", [128, 2 * NH], BF16))
    s_brocol = ctx.enter_context(nc.sbuf_tensor(f"{tag}_brocol", [128, 8], BF16))
    s_bro = ctx.enter_context(nc.sbuf_tensor(f"{tag}_bro", [1, D], BF16))
    s_bgr = ctx.enter_context(nc.sbuf_tensor(f"{tag}_bgr", [1, NH], BF16))
    s_wo = ctx.enter_context(nc.sbuf_tensor(f"{tag}_wo", [128, 8 * D], BF16))
    s_bo = ctx.enter_context(nc.sbuf_tensor(f"{tag}_bo", [1, D], BF16))
    s_rem = ctx.enter_context(nc.sbuf_tensor(f"{tag}_rem", [128, 2 * D], F32))
    s_g2 = ctx.enter_context(nc.sbuf_tensor(f"{tag}_g2", [128, 2 * NH], F32))
    s_gate = ctx.enter_context(nc.sbuf_tensor(f"{tag}_gate", [128, 2 * NH], F32))
    s_loc = ctx.enter_context(nc.sbuf_tensor(f"{tag}_loc", [128, 2 * D], BF16))
    s_d = ctx.enter_context(nc.sbuf_tensor(f"{tag}_d", [128, D], F32))
    s_gd = ctx.enter_context(nc.sbuf_tensor(f"{tag}_gd", [128, D], F32))
    s_mix = ctx.enter_context(nc.sbuf_tensor(f"{tag}_mix", [128, 2 * D], BF16))
    s_mixT = ctx.enter_context(nc.sbuf_tensor(f"{tag}_mixT", [128, 2 * D], BF16))
    s_res = ctx.enter_context(nc.sbuf_tensor(f"{tag}_res", [128, 2 * D], F32))

    lat_ch = s_latout.ap().rearrange("p (c t) -> p c t", c=2)
    loc_ch = s_local.ap().rearrange("p (c t) -> p c t", c=8)

    with nc.Block(f"b_{tag}") as blk:
        @blk.gpsimd
        def _(g):
            wait_finals(g, prev)
            S.inc(g.dma_start(out=s_ro.ap().rearrange("p (c n) -> p c n", c=2),
                              in_=d["wrout"].rearrange("(c p) n -> p c n", p=128)), "wl", 16)
            S.inc(g.dma_start(out=s_rot.ap().rearrange("p (c n) -> p c n", c=8),
                              in_=rr8(d["wroutT"])), "wl", 16)
            S.inc(g.dma_start(out=s_brocol.ap(),
                              in_=d["brout"].rearrange("o (c p) -> p (o c)", p=128),
                              allow_slow_non_contiguous=True), "wl", 16)
            S.inc(g.dma_start(out=s_bro[:, :], in_=d["brout"]), "wl", 16)
            S.inc(g.dma_start(out=s_wo.ap().rearrange("p (c n) -> p c n", c=8),
                              in_=rr8(d["wout_b"])), "wl", 16)
            S.inc(g.dma_start(out=s_bo[:, :], in_=d["bout_b"]), "wl", 16)

        @blk.sync
        def _(s):
            wait_finals(s, prev)
            def emit_store(i):
                S.wait(s, "ve", 2 * i + 2)
                RS.inc(s.dma_start(out=Rnext[i * 128:(i + 1) * 128, :],
                                   in_=s_res[:, (i % 2) * D:(i % 2 + 1) * D]), i)

            for i in range(nt):
                if i >= 4:
                    emit_store(i - 4)
                    S.wait(s, "ve", 2 * (i - 3))
                RR.inc(s.dma_start(out=self.s_acc[:, (i % 4) * D:(i % 4 + 1) * D],
                                   in_=Rcur[i * 128:(i + 1) * 128, :]), i)
            for i in range(max(0, nt - 4), nt):
                emit_store(i)
            for i in range(nt - 2, nt):
                RS.wait_done(s, i)

        @blk.tensor
        def _(t):
            wait_finals(t, prev)
            S.wait(t, "wl", S.c["wl"])
            for lg in range(2):
                last = None
                for c in range(8):
                    last = t.matmul(self.pb[6].ap()[:, lg * NH:(lg + 1) * NH],
                                    s_rot[:, c * LATD + lg * 128:c * LATD + (lg + 1) * 128],
                                    self.s_gr[:, c * NH:(c + 1) * NH],
                                    start=(c == 0), stop=(c == 7))
                S.inc(last, "gq", 1)
            last = None
            for c in range(8):
                last = t.matmul(self.pb[7].ap()[0:1, 0:NH], s_brocol[:, c:c + 1],
                                self.s_gr[:, c * NH:(c + 1) * NH],
                                start=(c == 0), stop=(c == 7))
            S.inc(last, "gq", 1)
            S.wait(t, "gq", 5)
            for i in range(nt):
                S.wait(t, "gt", max(0, 3 * i))
                for half in range(2):
                    bank = self.pb[half].ap()
                    for c in range(2):
                        t.matmul(bank, lat_ch[:, c, i * 128:(i + 1) * 128],
                                 s_ro[:, c * D + half * 512:c * D + half * 512 + 512],
                                 start=(c == 0), stop=False)
                    last = t.matmul(bank, self.s_ones_bf[:, 0:128],
                                    s_bro[:, half * 512:(half + 1) * 512],
                                    start=False, stop=True)
                    S.inc(last, "mm", 1)
                bank = self.pb[2].ap()[:, 0:NH]
                for c in range(2):
                    t.matmul(bank, lat_ch[:, c, i * 128:(i + 1) * 128],
                             s_rogr[:, c * NH:(c + 1) * NH], start=(c == 0), stop=False)
                last = t.matmul(bank, self.s_ones_bf[:, 0:128], s_bgr[:, :],
                                start=False, stop=True)
                S.inc(last, "mm", 1)
                last = t.transpose(self.pb[3].ap()[0:128, 0:NH],
                                   s_g1.ap()[0:NH, i * 128:(i + 1) * 128],
                                   self.s_id_f32[0:NH, 0:NH])
                S.inc(last, "mm", 1)
                S.wait(t, "mx", max(0, 4 * (i - 1) + 1))
                b6 = self.pb[6].ap().bitcast(BF16)
                for c in range(8):
                    last = t.transpose(b6[:, c * 128:(c + 1) * 128],
                                       loc_ch[:, c, i * 128:(i + 1) * 128], self.s_id_bf[:, :])
                S.inc(last, "tr", 1)
                S.wait(t, "mx", 4 * i + 4)
                S.wait(t, "mtv", max(0, 2 * i))
                b7 = self.pb[7].ap().bitcast(BF16)
                for c in range(8):
                    last = t.transpose(b7[:, c * 128:(c + 1) * 128],
                                       s_mix[:, (i % 2) * D + c * 128:(i % 2) * D + (c + 1) * 128],
                                       self.s_id_bf[:, :])
                S.inc(last, "tr", 1)
                S.wait(t, "mtv", 2 * i + 2)
                for half in range(2):
                    if i >= 1:
                        S.wait(t, "ve", 2 * (i - 1) + half + 1)
                    bank = self.pb[4 + half].ap()
                    mixT = s_mixT.ap().rearrange("p (b c t) -> p b c t", b=2, c=8)[:, i % 2]
                    for c in range(8):
                        t.matmul(bank, mixT[:, c, :],
                                 s_wo[:, c * D + half * 512:c * D + half * 512 + 512],
                                 start=(c == 0), stop=False)
                    last = t.matmul(bank, self.s_ones_bf[:, 0:128],
                                    s_bo[:, half * 512:(half + 1) * 512],
                                    start=False, stop=True)
                    S.inc(last, "pm", 1)

        @blk.vector
        def _(v):
            wait_finals(v, prev)
            S.wait(v, "gq", 2)
            v.tensor_copy(s_rogr[:, 0:NH], self.pb[6].ap()[:, 0:NH])
            S.edge(v, "e1", v.tensor_copy(s_rogr[:, NH:2 * NH], self.pb[6].ap()[:, NH:2 * NH]))
            S.wait(v, "gq", 3)
            S.inc(v.tensor_copy(s_bgr[:, :], self.pb[7].ap()[0:1, 0:NH]), "gq", 1)
            S.inc(v.nop(), "gq", 1)
            for i in range(nt):
                S.wait(v, "mm", 4 * i + 2)
                v.tensor_copy(s_rem[:, (i % 2) * D:(i % 2) * D + 512], self.pb[0].ap())
                S.edge(v, "e1", v.tensor_copy(s_rem[:, (i % 2) * D + 512:(i % 2 + 1) * D],
                                              self.pb[1].ap()))
                S.inc(v.nop(), "gt", 1)
                S.wait(v, "mm", 4 * i + 4)
                if i >= 2:
                    S.wait(v, "sg", i - 1)
                S.edge(v, "e1", v.tensor_copy(s_g2[:, (i % 2) * NH:(i % 2 + 1) * NH],
                                              self.pb[2].ap()[:, 0:NH]))
                S.inc(v.tensor_tensor(s_g2[:, (i % 2) * NH:(i % 2 + 1) * NH],
                                      s_g2[:, (i % 2) * NH:(i % 2 + 1) * NH],
                                      self.pb[3].ap()[:, 0:NH], ALU.add), "gt", 1)
                S.inc(v.nop(), "gt", 1)
                S.wait(v, "tr", 2 * i + 1)
                S.edge(v, "e1", v.tensor_copy(s_loc[:, (i % 2) * D:(i % 2 + 1) * D],
                                              self.pb[6].ap().bitcast(BF16)[:, 0:D]))
                S.inc(v.nop(), "mx", 1)
                S.wait(v, "sg", i + 1)
                S.edge(v, "e1", v.tensor_sub(s_d[:, :], s_loc[:, (i % 2) * D:(i % 2 + 1) * D],
                                             s_rem[:, (i % 2) * D:(i % 2 + 1) * D]))
                S.inc(v.nop(), "mx", 1)
                gb = s_gate.ap()[:, (i % 2) * NH:(i % 2 + 1) * NH].unsqueeze(2).broadcast_to([128, NH, HDIM])
                S.edge(v, "e1", v.tensor_tensor(s_gd.ap().rearrange("p (h e) -> p h e", h=NH),
                                                s_d.ap().rearrange("p (h e) -> p h e", h=NH),
                                                gb, ALU.mult))
                S.inc(v.nop(), "mx", 1)
                S.inc(v.tensor_add(s_mix[:, (i % 2) * D:(i % 2 + 1) * D], s_gd[:, :],
                                   s_rem[:, (i % 2) * D:(i % 2 + 1) * D]), "mx", 1)
                S.wait(v, "mx", S.c["mx"])
                S.wait(v, "tr", 2 * i + 2)
                S.inc(v.tensor_copy(
                    s_mixT.ap().rearrange("p (b c t) -> p b c t", b=2, c=8)[:, i % 2],
                    self.pb[7].ap().bitcast(BF16).rearrange("p (c t) -> p c t", c=8)), "mtv", 2)
                RR.wait_done(v, i)
                if i >= 2:
                    RS.wait_done(v, i - 2)
                for half in range(2):
                    S.wait(v, "pm", 2 * i + half + 1)
                    S.inc(v.tensor_add(s_res[:, (i % 2) * D + half * 512:(i % 2) * D + half * 512 + 512],
                                       self.pb[4 + half].ap(),
                                       self.s_acc[:, (i % 4) * D + half * 512:(i % 4) * D + half * 512 + 512]),
                          "ve", 1)

        @blk.scalar
        def _(sc):
            wait_finals(sc, prev)
            for i in range(nt):
                S.wait(sc, "gt", 3 * i + 3)
                if i >= 2:
                    S.wait(sc, "mx", 4 * (i - 1) + 3)
                S.inc(sc.activation(s_gate[:, (i % 2) * NH:(i % 2 + 1) * NH],
                                    s_g2[:, (i % 2) * NH:(i % 2 + 1) * NH], AF.Sigmoid),
                      "sg", 1)
    f = S.finals()
    f.update(RR.finals())
    f.update(RS.finals())
    self.prev = f


Prog.mix_proj = _mix_proj


# version 06
# speedup vs baseline: 1.0968x; 1.0931x over previous
import math
import numpy as np
import ml_dtypes
import concourse.bass as bass
import concourse.mybir as mybir
from concourse.bass_utils import run_bass_kernel_spmd

F32 = mybir.dt.float32
BF16 = mybir.dt.bfloat16
AF = mybir.ActivationFunctionType
ALU = mybir.AluOpType
AX = mybir.AxisListType
bf = ml_dtypes.bfloat16

D = 1024
NH = 16
HDIM = 64
HID = 2816
NE = 9
LATD = 256
NLH = 4
NLAT = 64
NEG = -1.0e9
EPS_LN = 1e-5


class _Slot:
    __slots__ = ("h", "c")

    def __init__(self, h):
        self.h = h
        self.c = 0


class Sems:

    def __init__(self, prog, tag, names, sw=()):
        self.slots = {}
        self.base = {}
        gi = 0
        si = 0
        for n in names:
            if n in sw:
                slot = prog.sw_slot(si)
                si += 1
            else:
                slot = prog.gen_slot(gi)
                gi += 1
            self.slots[n] = slot
            self.base[n] = slot.c

    @property
    def h(self):
        return {n: s.h for n, s in self.slots.items()}

    @property
    def c(self):
        return {n: sl.c - self.base[n] for n, sl in self.slots.items()}

    def inc(self, instr, name, n=1):
        slot = self.slots[name]
        instr.then_inc(slot.h, n)
        slot.c += n
        return slot.c - self.base[name]

    def wait(self, eng, name, v):
        if v > 0:
            eng.wait_ge(self.slots[name].h, self.base[name] + v)

    def finals(self):
        return {sl.h: sl.c for n, sl in self.slots.items() if sl.c > self.base[n]}

    def edge(self, eng, name, instr):
        v = self.inc(instr, name, 1)
        self.wait(eng, name, v)
        return v


def wait_finals(eng, finals):
    for h, v in finals.items():
        eng.wait_ge(h, v)


class Ring:

    def __init__(self, prog, tag, n, base_idx=8):
        self.ss = [prog.gen_slot(base_idx + j) for j in range(n)]
        self.n = n
        self.target = {}

    def inc(self, instr, item, amt=16):
        s = self.ss[item % self.n]
        instr.then_inc(s.h, amt)
        s.c += amt
        self.target[item] = s.c

    def wait_done(self, eng, item):
        if item < 0 or item not in self.target:
            return
        eng.wait_ge(self.ss[item % self.n].h, self.target[item])

    def finals(self):
        return {s.h: s.c for s in self.ss if s.c > 0}


def rr8(ap, c=8):
    return ap.rearrange("(c p) n -> p c n", p=128)


class Prog:
    def __init__(self, T, debug_outs=()):
        assert T % 512 == 0
        self.T = T
        self.nt = T // 128
        self.ng = T // 512
        self.CH = T // NLAT
        self.nc = bass.Bass("TRN2", target_bir_lowering=False, debug=False)
        self.prev = {}
        self._gen_slots = {}
        self._sw_slots = {}
        self.debug_outs = set(debug_outs)
        self.dbg = {}

    def gen_slot(self, i):
        if i not in self._gen_slots:
            self._gen_slots[i] = _Slot(self.nc.alloc_semaphore(f"g{i}"))
        return self._gen_slots[i]

    def sw_slot(self, i):
        if i not in self._sw_slots:
            self._sw_slots[i] = _Slot(self.nc.alloc_semaphore(f"sw{i}"))
        return self._sw_slots[i]

    @property
    def wslots(self):
        if not hasattr(self, "_wslots"):
            self._wslots = [_Slot(self.nc.alloc_semaphore("wsl0")),
                            _Slot(self.nc.alloc_semaphore("wsl1"))]
            self.wtarget = {}
        return self._wslots

    def make_dram(self):
        nc, T = self.nc, self.T
        din = lambda n, s: nc.dram_tensor(n, s, F32, kind="ExternalInput").ap()
        dbi = lambda n, s: nc.dram_tensor(n, s, BF16, kind="ExternalInput").ap()
        scr = lambda n, s, dt=F32: nc.dram_tensor(n, s, dt, kind="Internal").ap()
        d = {}
        d["x"] = din("x", [T, D])
        d["wqkv_a"] = din("wqkv_a", [D, 3 * D])
        d["wout_a"] = din("wout_a", [D, D])
        d["head_scales"] = din("head_scales", [NH, 1])
        d["hdblk"] = din("hdblk", [D, NH])
        d["wqkv_b"] = din("wqkv_b", [D, 3 * D])
        d["bqkv_b"] = din("bqkv_b", [1, 3 * D])
        d["wrq"] = din("wrq", [D, LATD])
        d["brq"] = din("brq", [1, LATD])
        d["wrkv"] = din("wrkv", [D, 2 * LATD])
        d["brkv"] = din("brkv", [1, 2 * LATD])
        d["wroutT"] = din("wroutT", [D, LATD])
        d["wrout"] = din("wrout", [LATD, D])
        d["brout"] = din("brout", [1, D])
        d["wout_b"] = din("wout_b", [D, D])
        d["bout_b"] = din("bout_b", [1, D])
        d["glblk"] = din("glblk", [D, NH])
        d["grblk"] = din("grblk", [D, NH])
        d["w1T"] = din("w1T", [NE, D, HID])
        d["w3T"] = din("w3T", [NE, D, HID])
        d["w2T"] = din("w2T", [NE, HID, D])
        d["gateT"] = din("gateT", [D, NE])
        d["gbias"] = din("gbias", [1, NE])
        d["id_bf"] = dbi("id_bf", [128, 128])
        d["id_f32"] = din("id_f32", [128, 128])
        d["ones_bf"] = dbi("ones_bf", [1, 512])
        d["onesT"] = dbi("onesT", [1, self.T])
        d["ones_f32"] = din("ones_f32", [1, 64])
        d["maskA"] = dbi("maskA", [4 * 128, 512])
        d["maskBT"] = dbi("maskBT", [128, 128])
        d["maskL"] = dbi("maskL", [128, 256])
        d["maskLT1"] = dbi("maskLT1", [128, 128])
        d["maskLT2"] = dbi("maskLT2", [128, 128])
        d["latC"] = din("latC", [128, NLAT])
        d["latCT"] = din("latCT", [NLAT, 128])
        d["ind"] = dbi("ind", [self.nt * 128, NLAT])
        d["Ra"] = scr("Ra", [T, D])
        d["Rb"] = scr("Rb", [T, D])
        d["Qh"] = scr("Qh", [D, T], BF16)
        d["Kh"] = scr("Kh", [D, T], BF16)
        d["Vh"] = scr("Vh", [T, D], BF16)
        d["qp"] = scr("qp", [NH, T], BF16)
        d["kp"] = scr("kp", [NH, T], BF16)
        d["stats"] = scr("stats", [self.nt, 128])
        d["xfm"] = scr("xfm", [D, T], BF16)
        d["w1b"] = scr("w1b", [NE, D, HID], BF16)
        d["w3b"] = scr("w3b", [NE, D, HID], BF16)
        d["w2b"] = scr("w2b", [NE, HID, D], BF16)
        d["g1"] = scr("g1", [NH, T])
        d["out"] = nc.dram_tensor("out", [T, D], F32, kind="ExternalOutput").ap()
        self.d = d
        for name in self.debug_outs:
            shp, dt = {
                "d_xlnT": ([D, T], BF16), "d_Q": ([D, T], BF16), "d_K": ([D, T], BF16),
                "d_V": ([T, D], BF16), "d_qp": ([NH, T], BF16), "d_kp": ([NH, T], BF16),
                "d_attnout": ([D, T], BF16), "d_m": ([NH * self.nt, 128], F32),
                "d_cw": ([128, self.nt * NE], F32),
                "d_latout": ([LATD, T], BF16), "d_local": ([D, T], BF16),
                "d_pool": ([NLAT, D], BF16), "d_g1": ([NH, T], F32),
                "d_Ra": ([T, D], F32), "d_Rb": ([T, D], F32),
            }[name]
            self.dbg[name] = nc.dram_tensor(name, shp, dt, kind="ExternalOutput").ap()

    def make_persist(self):
        nc, T, nt = self.nc, self.T, self.nt
        import contextlib
        self.top = contextlib.ExitStack()
        sb = lambda n, s, dt: self.top.enter_context(nc.sbuf_tensor(n, s, dt))
        self.s_id_bf = sb("c_id_bf", [128, 128], BF16)
        self.s_id_f32 = sb("c_id_f32", [128, 128], F32)
        self.s_ones_bf = sb("c_ones_bf", [1, 512], BF16)
        self.s_ones_f32 = sb("c_ones_f32", [1, 64], F32)
        self.s_maskA = sb("c_maskA", [128, 4 * 512], BF16)
        self.s_maskBT = sb("c_maskBT", [128, 128], BF16)
        self.s_maskL = sb("c_maskL", [128, 256], BF16)
        self.s_maskLT1 = sb("c_maskLT1", [128, 128], BF16)
        self.s_maskLT2 = sb("c_maskLT2", [128, 128], BF16)
        self.s_latC = sb("c_latC", [128, NLAT], F32)
        self.s_latCT = sb("c_latCT", [NLAT, 128], F32)
        self.s_ind = sb("c_ind", [128, nt * NLAT], BF16)
        self.s_hd = sb("c_hd", [128, 8 * NH], BF16)
        self.s_gl = sb("c_gl", [128, 8 * NH], BF16)
        self.s_gr = sb("c_gr", [128, 8 * NH], BF16)
        self.s_hs8 = sb("c_hs8", [NH, 1], F32)
        self.s_cw = sb("c_cw", [128, nt * NE], F32)
        self.s_stat = sb("c_stat", [128, 4 * nt], F32)
        self.s_m = sb("c_m", [128, nt], F32)
        self.s_mneg = sb("c_mneg", [128, nt], F32)
        self.s_mg = sb("c_mg", [128, 16], F32)
        self.s_acc = sb("c_acc", [128, 4 * D], F32)
        self.s_row = sb("c_row", [64, 128], F32)
        self.s_msk = sb("c_msk", [128, 512], F32)
        self.s_tmp1 = sb("c_tmp1", [128, 2], F32)
        self.s_linv = sb("c_linv", [1, 512], F32)
        self.s_linvb = sb("c_linvb", [1, 512], BF16)
        self.s_lt = sb("c_lt", [1, 512], F32)
        self.s_eps = sb("c_eps", [128, 1], F32)
        self.s_lexp = sb("c_lexp", [64, 512], F32)
        self.pb = [self.top.enter_context(nc.psum_tensor(f"pb{i}", [128, 512], F32))
                   for i in range(8)]

    def setup_phase(self):
        nc, d = self.nc, self.d
        S = Sems(self, "set", ["qp", "qs"], sw=("qp",))
        wslot = self.sw_slot(1)
        with nc.Block("setup") as blk:
            @blk.gpsimd
            def _(g):
                sbr = lambda t, c: t.ap().rearrange("p (c n) -> p c n", c=c)
                S.inc(g.dma_start(out=sbr(self.s_hd, 8), in_=rr8(d["hdblk"])), "qp", 16)
                S.inc(g.dma_start(out=sbr(self.s_gl, 8), in_=rr8(d["glblk"])), "qp", 16)
                S.inc(g.dma_start(out=sbr(self.s_gr, 8), in_=rr8(d["grblk"])), "qp", 16)
                S.inc(g.dma_start(out=self.s_hs8[:, :], in_=d["head_scales"]), "qp", 16)
                for e in range(NE):
                    for nm in ("w1", "w3", "w2"):
                        ins = g.dma_start(out=d[nm + "b"][e], in_=d[nm + "T"][e])
                        ins.then_inc(wslot.h, 16)
                        wslot.c += 16
                self.wcast = (wslot.h, wslot.c)

            @blk.sync
            def _(s):
                S.inc(s.dma_start(out=self.s_id_bf[:, :], in_=d["id_bf"]), "qs", 16)
                S.inc(s.dma_start(out=self.s_id_f32[:, :], in_=d["id_f32"]), "qs", 16)
                S.inc(s.dma_start(out=self.s_ones_bf[:, :], in_=d["ones_bf"]), "qs", 16)
                S.inc(s.dma_start(out=self.s_ones_f32[:, :], in_=d["ones_f32"]), "qs", 16)
                S.inc(s.dma_start(out=self.s_maskA.ap().rearrange("p (c n) -> p c n", c=4),
                                  in_=rr8(d["maskA"], 4)), "qs", 16)
                S.inc(s.dma_start(out=self.s_maskBT[:, :], in_=d["maskBT"]), "qs", 16)
                S.inc(s.dma_start(out=self.s_maskL[:, :], in_=d["maskL"]), "qs", 16)
                S.inc(s.dma_start(out=self.s_maskLT1[:, :], in_=d["maskLT1"]), "qs", 16)
                S.inc(s.dma_start(out=self.s_maskLT2[:, :], in_=d["maskLT2"]), "qs", 16)
                S.inc(s.dma_start(out=self.s_latC[:, :], in_=d["latC"]), "qs", 16)
                S.inc(s.dma_start(out=self.s_latCT[:, :], in_=d["latCT"]), "qs", 16)
                S.inc(s.dma_start(out=self.s_ind.ap().rearrange("p (c n) -> p c n", c=self.nt),
                                  in_=rr8(d["ind"], self.nt)), "qs", 16)

            @blk.vector
            def _(v):
                v.memset(self.s_eps[:, :], EPS_LN)
                S.wait(v, "qp", S.c["qp"])
                S.inc(v.tensor_scalar(self.s_hs8[:, :], self.s_hs8[:, :], 8.0, None,
                                      ALU.mult), "qs", 1)
        self.prev = S.finals()

    def ln_pass(self, tag, Rcur, s_xlnT, pooled_out=None, ctx=None):
        nc, nt, T = self.nc, self.nt, self.T
        S = Sems(self, tag, ["sum", "xln", "ev", "sq", "chn", "tr", "pm", "pl"])
        RL = Ring(self, f"{tag}ld", 3)
        prev = self.prev
        import contextlib as _ctl
        lctx = _ctl.ExitStack()
        s_xlnb = lctx.enter_context(nc.sbuf_tensor(f"{tag}_xlnb", [128, 2 * D], BF16))
        s_xin = lctx.enter_context(nc.sbuf_tensor(f"{tag}_xin", [128, 3 * D], F32))
        s_sq = lctx.enter_context(nc.sbuf_tensor(f"{tag}_sq", [128, D], F32))
        s_pacc = None
        if pooled_out is not None:
            s_pacc = lctx.enter_context(nc.sbuf_tensor(f"{tag}_pacc", [64, D], F32))
        with nc.Block(f"b_{tag}") as blk:
            @blk.sync
            def _(s):
                wait_finals(s, prev)
                for i in range(nt):
                    if i >= 3:
                        S.wait(s, "xln", i - 2)
                    RL.inc(s.dma_start(out=s_xin[:, (i % 3) * D:(i % 3 + 1) * D],
                                       in_=Rcur[i * 128:(i + 1) * 128, :]), i)

            @blk.scalar
            def _(sc):
                wait_finals(sc, prev)
                st = self.s_stat
                for i in range(nt):
                    RL.wait_done(sc, i)
                    S.wait(sc, "chn", i)
                    xin = s_xin[:, (i % 3) * D:(i % 3 + 1) * D]
                    S.edge(sc, "sq", sc.activation(s_sq[:, :], xin, AF.Square,
                                                   accum_out=st[:, 2 * nt + i:2 * nt + i + 1]))
                    S.wait(sc, "sum", i + 1)
                    S.edge(sc, "sq", sc.activation(st[:, nt + i:nt + i + 1], st[:, i:i + 1],
                                                   AF.Identity, scale=1.0 / D))
                    S.edge(sc, "sq", sc.activation(s_sq[:, 0:1],
                                                   st[:, nt + i:nt + i + 1], AF.Square))
                    S.edge(sc, "sq", sc.activation(s_sq[:, 1:2], s_sq[:, 0:1],
                                                   AF.Identity, scale=-1.0))
                    S.edge(sc, "sq", sc.activation(s_sq[:, 2:3],
                                                   st[:, 2 * nt + i:2 * nt + i + 1],
                                                   AF.Identity, scale=1.0 / D,
                                                   bias=s_sq[:, 1:2]))
                    S.edge(sc, "sq", sc.activation(s_sq[:, 3:4], s_sq[:, 2:3],
                                                   AF.Ln, bias=self.s_eps[:, :]))
                    S.inc(sc.activation(st[:, 3 * nt + i:3 * nt + i + 1],
                                        s_sq[:, 3:4], AF.Exp, scale=-0.5), "chn", 1)

            @blk.vector
            def _(v):
                wait_finals(v, prev)

                def emit_xln(i):
                    S.wait(v, "chn", i + 1)
                    if i >= 2:
                        S.wait(v, "tr", i - 1)
                    xin = s_xin[:, (i % 3) * D:(i % 3 + 1) * D]
                    mean = self.s_stat[:, nt + i:nt + i + 1]
                    rstd = self.s_stat[:, 3 * nt + i:3 * nt + i + 1]
                    dst = s_xlnb[:, (i % 2) * D:(i % 2 + 1) * D]
                    S.inc(v.tensor_scalar(dst, xin, mean, rstd, ALU.subtract, ALU.mult),
                          "xln", 1)

                def emit_evac(i):
                    S.wait(v, "tr", i + 1)
                    src = self.pb[2 + (i % 2)].ap().bitcast(BF16)
                    dstv = s_xlnT.ap().rearrange("p (c t) -> p c t", c=8)[:, :, i * 128:(i + 1) * 128]
                    S.inc(v.tensor_copy(dstv, src.rearrange("p (c t) -> p c t", c=8)),
                          "ev", 1)
                    if pooled_out is not None:
                        S.wait(v, "pm", i + 1)
                        if i == 0:
                            v.tensor_copy(s_pacc[:, 0:512], self.pb[6].ap()[0:NLAT, :])
                            iv = v.tensor_copy(s_pacc[:, 512:1024], self.pb[7].ap()[0:NLAT, :])
                        else:
                            v.tensor_tensor(s_pacc[:, 0:512], s_pacc[:, 0:512],
                                            self.pb[6].ap()[0:NLAT, :], ALU.add)
                            iv = v.tensor_tensor(s_pacc[:, 512:1024], s_pacc[:, 512:1024],
                                                 self.pb[7].ap()[0:NLAT, :], ALU.add)
                        S.inc(iv, "pl", 1)

                for i in range(nt):
                    RL.wait_done(v, i)
                    xin = s_xin[:, (i % 3) * D:(i % 3 + 1) * D]
                    S.inc(v.tensor_reduce(self.s_stat[:, i:i + 1], xin, AX.X, ALU.add),
                          "sum", 1)
                    if i >= 3:
                        emit_evac(i - 3)
                    if i >= 2:
                        emit_xln(i - 2)
                emit_evac(nt - 3)
                emit_xln(nt - 2)
                emit_evac(nt - 2)
                emit_xln(nt - 1)
                emit_evac(nt - 1)
                if pooled_out is not None:
                    S.wait(v, "pl", S.c["pl"])
                    S.inc(v.tensor_copy(pooled_out[:, :], s_pacc[:, :]), "pl", 1)

            @blk.tensor
            def _(t):
                wait_finals(t, prev)
                for i in range(nt):
                    S.wait(t, "xln", i + 1)
                    src = s_xlnb[:, (i % 2) * D:(i % 2 + 1) * D]
                    if pooled_out is not None:
                        if i >= 1:
                            S.wait(t, "pl", i)
                        t.matmul(self.pb[6].ap()[0:NLAT, :],
                                 self.s_ind[:, i * NLAT:(i + 1) * NLAT],
                                 src[:, 0:512], start=True, stop=True)
                        S.inc(t.matmul(self.pb[7].ap()[0:NLAT, :],
                                       self.s_ind[:, i * NLAT:(i + 1) * NLAT],
                                       src[:, 512:1024], start=True, stop=True), "pm", 1)
                    if i >= 2:
                        S.wait(t, "ev", i - 1)
                    dst = self.pb[2 + (i % 2)].ap().bitcast(BF16)
                    for c in range(8):
                        iv = t.transpose(dst[:, c * 128:(c + 1) * 128],
                                         src[:, c * 128:(c + 1) * 128], self.s_id_bf[:, :])
                    S.inc(iv, "tr", 1)
        lctx.close()
        f = S.finals()
        f.update(RL.finals())
        self.prev = f

    def proj_spill(self, tag, s_xlnT, W_in=None, COLS=0, bias_in=None, fcmodes=(),
                   Qdst=None, Kdst=None, Vdst=None, do_pp=False, do_route=False,
                   rq=None, xfm_spill=None, ctx=None, dbg_xlnT=False):
        nc, nt, T, ng = self.nc, self.nt, self.T, self.ng
        d = self.d
        S = Sems(self, tag, ["wld", "mm", "ev", "tr", "fmv", "sp", "pp", "ppv", "rt", "ra", "ppd", "e2"], sw=("wld",))
        RV = Ring(self, f"{tag}rv", 2, base_idx=16)
        RF = Ring(self, f"{tag}rf", 2, base_idx=18)
        prev = self.prev
        nfc = len(fcmodes)
        W = 512
        s_w = s_bias = s_fm = s_ev = s_vev = None
        if W_in is not None:
            s_w = ctx.enter_context(nc.sbuf_tensor(f"{tag}_w", [128, 2 * 8 * 512], BF16))
            if bias_in is not None:
                s_bias = ctx.enter_context(nc.sbuf_tensor(f"{tag}_bias", [1, COLS], BF16))
            s_fm = ctx.enter_context(nc.sbuf_tensor(f"{tag}_fm", [128, 4 * T], BF16))
            s_vev = ctx.enter_context(nc.sbuf_tensor(f"{tag}_vev", [128, 2 * 512], BF16))
        if W_in is not None or rq is not None:
            s_ev = ctx.enter_context(nc.sbuf_tensor(f"{tag}_ev", [128, 2 * 512], BF16))
        s_pp = s_ppb = None
        if do_pp:
            s_pp = ctx.enter_context(nc.sbuf_tensor(f"{tag}_pp", [NH, T], F32))
            s_ppb = ctx.enter_context(nc.sbuf_tensor(f"{tag}_ppb", [NH, T], BF16))
        s_gt = s_gb = s_rt = None
        if do_route:
            s_gt = ctx.enter_context(nc.sbuf_tensor(f"{tag}_gt", [128, 8 * NE], BF16))
            s_gb = ctx.enter_context(nc.sbuf_tensor(f"{tag}_gb", [1, NE], BF16))
            s_rt = ctx.enter_context(nc.sbuf_tensor(f"{tag}_rt", [128, 4 * NE], F32))
        s_rqw = s_rqb = s_rqT = None
        if rq is not None:
            wrq_in, brq_in, s_rqT = rq
            s_rqw = ctx.enter_context(nc.sbuf_tensor(f"{tag}_rqw", [128, 8 * LATD], BF16))
            s_rqb = ctx.enter_context(nc.sbuf_tensor(f"{tag}_rqb", [1, LATD], BF16))

        xlnT_ch = s_xlnT.ap().rearrange("p (c t) -> p c t", c=8)
        nts = T // 512
        nqk = sum(1 for m in fcmodes if m in ("q", "k"))
        units = nfc * nt + (nt if rq is not None else 0)

        def fcw(fc):
            return 256 if fc == nfc else 512

        def wsrc(fc, c):
            if fc == nfc:
                return s_rqw[:, c * LATD:c * LATD + 256]
            base = (fc % 2) * 8 * 512 + c * 512
            return s_w[:, base:base + 512]

        def bsrc(fc):
            if fc == nfc:
                return s_rqb[:, :]
            return s_bias[:, fc * 512:(fc + 1) * 512] if s_bias is not None else None

        allmodes = list(fcmodes) + (["rq"] if rq is not None else [])

        with nc.Block(f"b_{tag}") as blk:
            @blk.gpsimd
            def _(g):
                wait_finals(g, prev)
                if W_in is not None and bias_in is not None:
                    S.inc(g.dma_start(out=s_bias[:, :], in_=bias_in), "wld", 16)
                if do_route:
                    S.inc(g.dma_start(out=s_gt.ap().rearrange("p (c n) -> p c n", c=8),
                                      in_=rr8(d["gateT"])), "wld", 16)
                    S.inc(g.dma_start(out=s_gb[:, :], in_=d["gbias"]), "wld", 16)
                if rq is not None:
                    S.inc(g.dma_start(out=s_rqw.ap().rearrange("p (c n) -> p c n", c=8),
                                      in_=rr8(wrq_in)), "wld", 16)
                    S.inc(g.dma_start(out=s_rqb[:, :], in_=brq_in), "wld", 16)
                if W_in is not None:
                    for fc in range(nfc):
                        if fc >= 2:
                            S.wait(g, "mm", (fc - 1) * nt)
                        sl = self.wslots[fc % 2]
                        ins = g.dma_start(
                            out=s_w.ap().rearrange("p (b c n) -> p b c n", b=2, c=8)[:, fc % 2],
                            in_=W_in[:, fc * 512:(fc + 1) * 512].rearrange("(c p) n -> p c n", p=128))
                        ins.then_inc(sl.h, 16)
                        sl.c += 16
                        self.wtarget[(tag, fc)] = sl.c

            @blk.sync
            def _(s):
                wait_finals(s, prev)
                u = 0
                vv = 0
                for fc, mode in enumerate(allmodes):
                    if mode in ("q", "k"):
                        qki = sum(1 for f in allmodes[:fc] if f in ("q", "k"))
                        S.wait(s, "fmv", (qki + 1) * nt)
                        dstD = Qdst if mode == "q" else Kdst
                        for c in range(4):
                            gch = (fc % 2) * 4 + c
                            RF.inc(s.dma_start(out=dstD[gch * 128:(gch + 1) * 128, :],
                                               in_=s_fm[:, c * T:(c + 1) * T]), qki)
                        if do_pp and fc == 1:
                            S.wait(s, "ppv", 2 * nts + 1)
                            S.inc(s.dma_start(out=d["qp"], in_=s_ppb[:, :]), "ppd", 16)
                        if do_pp and fc == 3:
                            S.wait(s, "ppv", 4 * nts + 2)
                            S.inc(s.dma_start(out=d["kp"], in_=s_ppb[:, :]), "sp", 16)
                        u += nt
                    elif mode == "rq":
                        u += nt
                    else:
                        vfc = sum(1 for f in allmodes[:fc] if f == "v")
                        for i in range(nt):
                            S.wait(s, "ev", u + 1)
                            RV.inc(s.dma_start(out=Vdst[i * 128:(i + 1) * 128,
                                                        vfc * 512:(vfc + 1) * 512],
                                               in_=s_vev[:, (vv % 2) * 512:(vv % 2 + 1) * 512]),
                                   vv)
                            u += 1
                            vv += 1

                if xfm_spill is not None:
                    for c in range(8):
                        S.inc(s.dma_start(out=xfm_spill[c * 128:(c + 1) * 128, :],
                                          in_=s_xlnT[:, c * T:(c + 1) * T]), "sp", 16)
                if dbg_xlnT and "d_xlnT" in self.dbg:
                    for c in range(8):
                        S.inc(s.dma_start(out=self.dbg["d_xlnT"][c * 128:(c + 1) * 128, :],
                                          in_=s_xlnT[:, c * T:(c + 1) * T]), "sp", 16)
            @blk.tensor
            def _(t):
                wait_finals(t, prev)
                S.wait(t, "wld", S.c["wld"])
                u = 0
                uq = 0
                for fc, mode in enumerate(allmodes):
                    w = fcw(fc)
                    if W_in is not None and fc < nfc:
                        t.wait_ge(self.wslots[fc % 2].h, self.wtarget[(tag, fc)])
                    for i in range(nt):
                        if u >= 2:
                            S.wait(t, "ev", u - 1)
                        bank = self.pb[u % 2].ap()[:, 0:w]
                        hasb = bsrc(fc) is not None
                        last = None
                        for c in range(8):
                            last = t.matmul(bank, xlnT_ch[:, c, i * 128:(i + 1) * 128],
                                            wsrc(fc, c), start=(c == 0),
                                            stop=(c == 7 and not hasb))
                        if hasb:
                            last = t.matmul(bank, self.s_ones_bf[:, 0:128], bsrc(fc),
                                            start=False, stop=True)
                        S.inc(last, "mm", 1)
                        if mode in ("q", "k", "rq"):
                            S.wait(t, "ev", u + 1)
                            if uq >= 2:
                                S.wait(t, "fmv", uq - 1)
                            dstp = self.pb[2 + (uq % 2)].ap().bitcast(BF16)
                            for c in range(w // 128):
                                iv2 = t.transpose(dstp[:, c * 128:(c + 1) * 128],
                                                  s_ev[:, (uq % 2) * 512 + c * 128:(uq % 2) * 512 + (c + 1) * 128],
                                                  self.s_id_bf[:, :])
                            S.inc(iv2, "tr", 1)
                            uq += 1
                        u += 1
                    if do_pp and mode in ("q", "k"):
                        qki = sum(1 for f in allmodes[:fc] if f in ("q", "k"))
                        S.wait(t, "fmv", (qki + 1) * nt)
                        for ts in range(nts):
                            pidx = fc * nts + ts
                            if pidx >= 2:
                                S.wait(t, "ppv", pidx - 1)
                            bank = self.pb[4 + (pidx % 2)].ap()[0:NH, :]
                            for c in range(4):
                                gch = (fc % 2) * 4 + c
                                iv = t.matmul(bank,
                                              self.s_hd[:, gch * NH:(gch + 1) * NH],
                                              s_fm[:, c * T + ts * 512:c * T + ts * 512 + 512],
                                              start=(c == 0), stop=(c == 3))
                            S.inc(iv, "pp", 1)
                if do_route:
                    for i in range(nt):
                        if i >= 2:
                            S.wait(t, "ra", i - 1)
                        bank = self.pb[4 + (i % 2)].ap()[:, 0:NE]
                        for c in range(8):
                            t.matmul(bank, xlnT_ch[:, c, i * 128:(i + 1) * 128],
                                     s_gt[:, c * NE:(c + 1) * NE],
                                     start=(c == 0), stop=False)
                        iv = t.matmul(bank, self.s_ones_bf[:, 0:128], s_gb[:, :],
                                      start=False, stop=True)
                        S.inc(iv, "mm", 1)

            @blk.vector
            def _(v):
                wait_finals(v, prev)
                u = 0
                uq = 0
                vv = 0
                for fc, mode in enumerate(allmodes):
                    w = fcw(fc)
                    for i in range(nt):
                        S.wait(v, "mm", u + 1)
                        bank = self.pb[u % 2].ap()[:, 0:w]
                        if mode in ("q", "rq"):
                            if uq >= 2:
                                S.wait(v, "tr", uq - 1)
                            S.inc(v.tensor_scalar(s_ev[:, (uq % 2) * 512:(uq % 2) * 512 + w],
                                                  bank, 0.125, None, ALU.mult), "ev", 1)
                        elif mode == "k":
                            if uq >= 2:
                                S.wait(v, "tr", uq - 1)
                            S.inc(v.tensor_copy(s_ev[:, (uq % 2) * 512:(uq % 2) * 512 + w],
                                                bank), "ev", 1)
                        else:
                            if vv >= 2:
                                RV.wait_done(v, vv - 2)
                            S.inc(v.tensor_copy(s_vev[:, (vv % 2) * 512:(vv % 2 + 1) * 512],
                                                bank), "ev", 1)
                            vv += 1
                        if mode in ("q", "k", "rq"):
                            if mode in ("q", "k") and i == 0:
                                qki = sum(1 for f in allmodes[:fc] if f in ("q", "k"))
                                RF.wait_done(v, qki - 1)
                            S.wait(v, "tr", uq + 1)
                            src = self.pb[2 + (uq % 2)].ap().bitcast(BF16)
                            nblk = w // 128
                            if mode == "rq":
                                dstv = s_rqT.ap().rearrange("p (c t) -> p c t", c=2)[:, :, i * 128:(i + 1) * 128]
                            else:
                                dstv = s_fm.ap().rearrange("p (c t) -> p c t", c=4)[:, :, i * 128:(i + 1) * 128]
                            S.inc(v.tensor_copy(dstv, src.rearrange("p (c t) -> p c t", c=8)[:, 0:nblk, :]),
                                  "fmv", 1)
                            uq += 1
                        u += 1
                    if do_pp and mode in ("q", "k"):
                        for ts in range(nts):
                            pidx = fc * nts + ts
                            S.wait(v, "pp", pidx + 1)
                            bank = self.pb[4 + (pidx % 2)].ap()[0:NH, :]
                            dst = s_pp[:, ts * 512:(ts + 1) * 512]
                            if fc % 2 == 0:
                                iv = v.tensor_copy(dst, bank)
                            else:
                                iv = v.tensor_tensor(dst, dst, bank, ALU.add)
                            S.inc(iv, "ppv", 1)
                        if fc % 2 == 1:
                            S.wait(v, "ppv", S.c["ppv"])
                            if mode == "q":
                                iv = v.tensor_scalar(s_ppb[:, :], s_pp[:, :], self.s_hs8[:, :],
                                                     None, ALU.mult)
                            else:
                                S.wait(v, "ppd", 16)
                                iv = v.tensor_copy(s_ppb[:, :], s_pp[:, :])
                            S.inc(iv, "ppv", 1)
                if do_route:
                    mmbase = units
                    for i in range(nt):
                        rt = s_rt[:, (i % 2) * 2 * NE:(i % 2) * 2 * NE + NE]
                        rt2 = s_rt[:, (i % 2) * 2 * NE + NE:(i % 2) * 2 * NE + 2 * NE]
                        S.wait(v, "ra", i + 1)
                        S.wait(v, "rt", S.c["rt"])
                        m1 = self.s_tmp1[:, 0:1]
                        S.edge(v, "e2", v.tensor_reduce(m1, rt, AX.X, ALU.max))
                        S.edge(v, "e2", v.tensor_scalar(rt2, rt, m1, None, ALU.is_ge))
                        S.edge(v, "e2", v.scalar_tensor_tensor(rt2, rt2, -1e9, rt, ALU.mult, ALU.add))
                        S.edge(v, "e2", v.tensor_reduce(m1, rt2, AX.X, ALU.max))
                        S.edge(v, "e2", v.tensor_scalar(rt2, rt, m1, None, ALU.is_ge))
                        cw = self.s_cw[:, i * NE:(i + 1) * NE]
                        S.edge(v, "e2", v.tensor_tensor(cw, rt, rt2, ALU.mult))
                        S.edge(v, "e2", v.tensor_reduce(m1, cw, AX.X, ALU.add))
                        S.edge(v, "e2", v.tensor_scalar(m1, m1, 1e-20, None, ALU.add))
                        S.edge(v, "e2", v.reciprocal(m1, m1))
                        S.inc(v.tensor_scalar(cw, cw, m1, None, ALU.mult), "rt", 1)

            if do_route:
                @blk.scalar
                def _(sc):
                    wait_finals(sc, prev)
                    mmbase = units
                    for i in range(nt):
                        S.wait(sc, "mm", mmbase + i + 1)
                        bank = self.pb[4 + (i % 2)].ap()[:, 0:NE]
                        rt = s_rt[:, (i % 2) * 2 * NE:(i % 2) * 2 * NE + NE]
                        if i >= 2:
                            S.wait(sc, "rt", i - 1)
                        S.inc(sc.activation(rt, bank, AF.Sigmoid), "ra", 1)

        f = S.finals()
        f.update(RV.finals())
        f.update(RF.finals())
        self.prev = f

    def attn_a_heads(self, s_ao, ctx=None):
        nc, nt, T, ng = self.nc, self.nt, self.T, self.ng
        d = self.d
        S = Sems(self, "a2", ["pa", "rm", "sx", "sro", "pbm", "mk", "ex", "pv",
                              "nz", "lx", "dbg", "ms", "e1", "sri", "m2", "lnx"], sw=("sri",))
        RK = Ring(self, "a2kld", 2, base_idx=16)
        prev = self.prev
        s_K = [ctx.enter_context(nc.sbuf_tensor(f"a2_K{p}", [66, T], BF16)) for p in range(2)]
        s_Q = [ctx.enter_context(nc.sbuf_tensor(f"a2_Q{p}", [66, T], BF16)) for p in range(2)]
        s_V = [ctx.enter_context(nc.sbuf_tensor(f"a2_V{p}", [128, nt * 65], BF16)) for p in range(2)]
        s_pt = ctx.enter_context(nc.sbuf_tensor("a2_pt", [128, 2 * 512], BF16))

        npa_h = nt * (nt + 4) // 8
        npa_h = sum(i // 4 + 1 for i in range(nt))
        nbu_h = sum(4 * qg + 4 for qg in range(ng))
        pbm_end = [nbu_h * (h + 1) for h in range(NH)]
        pv_end = [nbu_h * (h + 1) for h in range(NH)]

        with nc.Block("b_a2") as blk:
            @blk.sync
            def _(s):
                wait_finals(s, prev)
                for p in range(2):
                    S.inc(s.dma_start(out=s_K[p].ap()[65:66, :], in_=d["onesT"]), "ms", 16)
                for h in range(NH):
                    par = h % 2
                    if h >= 2:
                        S.wait(s, "pbm", pbm_end[h - 2])
                        S.wait(s, "pv", pv_end[h - 2])
                    RK.inc(s.dma_start(out=s_K[par].ap()[0:64, :],
                                       in_=d["Kh"][h * 64:(h + 1) * 64, :]), h)
                    RK.inc(s.dma_start(out=s_K[par].ap()[64:65, :],
                                       in_=d["kp"][h:h + 1, :]), h)
                    RK.inc(s.dma_start(out=s_Q[par].ap()[0:64, :],
                                       in_=d["Qh"][h * 64:(h + 1) * 64, :]), h)
                    RK.inc(s.dma_start(out=s_Q[par].ap()[64:65, :],
                                       in_=d["qp"][h:h + 1, :]), h)
                    RK.inc(s.dma_start(out=s_V[par].ap().rearrange("p (t c) -> p t c", c=65)[:, :, 0:64],
                                       in_=d["Vh"][:, h * 64:(h + 1) * 64].rearrange("(t p) d -> p t d", p=128)),
                           h)
                    S.wait(s, "sx", 3 * h + 3)
                    S.wait(s, "sri", 16 * h)
                    S.inc(s.dma_start(out=d["stats"], in_=self.s_row[0:nt, :]), "sro", 16)
                    if "d_m" in self.dbg:
                        S.inc(s.dma_start(out=self.dbg["d_m"][h * nt:(h + 1) * nt, :],
                                          in_=self.s_row[0:nt, :]), "dbg", 16)
                if "d_attnout" in self.dbg:
                    S.wait(s, "nz", 3 * ng * NH)
                    for c in range(8):
                        S.inc(s.dma_start(out=self.dbg["d_attnout"][c * 128:(c + 1) * 128, :],
                                          in_=s_ao[:, c * T:(c + 1) * T]), "dbg", 16)

            @blk.gpsimd
            def _(g):
                wait_finals(g, prev)
                for h in range(NH):
                    par = h % 2
                    S.wait(g, "sro", 16 * (h + 1))
                    if h >= 2:
                        S.wait(g, "pbm", pbm_end[h - 2])
                    S.inc(g.dma_start(out=s_Q[par].ap()[65:66, :],
                                      in_=d["stats"].rearrange("a b -> (a b)").unsqueeze(0)),
                          "sri", 16)

            @blk.tensor
            def _(t):
                wait_finals(t, prev)
                au = 0
                bu = 0
                qgi = 0
                S.wait(t, "ms", 33)

                def pe_passA(h):
                    nonlocal au
                    par = h % 2
                    RK.wait_done(t, h)
                    npa = sum(i // 4 + 1 for i in range(nt))
                    done = 0
                    chunk = max(1, (npa + ng - 1) // ng)
                    for i in range(nt):
                        for kg in range(i // 4 + 1):
                            if au >= 3:
                                S.wait(t, "rm", au - 2)
                            bank = self.pb[au % 3].ap() if au % 3 < 2 else self.pb[7].ap()
                            bank = [self.pb[0], self.pb[1], self.pb[7]][au % 3].ap()
                            S.inc(t.matmul(bank, s_Q[par].ap()[0:65, i * 128:(i + 1) * 128],
                                           s_K[par].ap()[0:65, kg * 512:(kg + 1) * 512],
                                           start=True, stop=True), "pa", 1)
                            au += 1
                            done += 1
                            if done % chunk == 0:
                                yield
                    S.wait(t, "sx", 3 * h + 1)
                    S.inc(t.transpose(self.pb[6].ap()[0:nt, 0:128], self.s_mneg[:, 0:nt],
                                      self.s_id_f32[:, :]), "sx", 1)
                    while True:
                        yield

                def pe_passB(h):
                    nonlocal bu, qgi
                    par = h % 2
                    S.wait(t, "sri", 16 * (h + 1))
                    for qg in range(ng):
                        nkt = 4 * qg + 4

                        def emit_pv(kt, bu_kt):
                            qoff = max(0, (kt - 4 * qg) * 128)
                            N = 512 - qoff
                            S.wait(t, "ex", bu_kt + 1)
                            if kt == 0:
                                S.wait(t, "nz", 3 * qgi)
                            S.inc(t.matmul(self.pb[4].ap()[0:65, qoff:512],
                                           s_V[par].ap()[:, kt * 65:kt * 65 + 65],
                                           s_pt[:, (bu_kt % 2) * 512:(bu_kt % 2) * 512 + N],
                                           start=(kt == 0), stop=(kt == nkt - 1)), "pv", 1)

                        for kt in range(nkt):
                            qoff = max(0, (kt - 4 * qg) * 128)
                            N = 512 - qoff
                            if bu >= 2:
                                S.wait(t, "ex", bu - 1)
                            bank = self.pb[2 + bu % 2].ap()
                            S.inc(t.matmul(bank[:, 0:N],
                                           s_K[par].ap()[0:66, kt * 128:(kt + 1) * 128],
                                           s_Q[par].ap()[0:66, qg * 512 + qoff:(qg + 1) * 512],
                                           start=True, stop=True), "pbm", 1)
                            if kt > 0:
                                emit_pv(kt - 1, bu - 1)
                            bu += 1
                        emit_pv(nkt - 1, bu - 1)
                        S.wait(t, "lnx", 2 * (qgi + 1))
                        S.inc(t.matmul(self.pb[5].ap()[0:64, :], self.s_ones_bf[0:1, 0:64],
                                       self.s_linvb[0:1, 0:512], start=True, stop=True),
                              "lx", 1)
                        qgi += 1
                        yield

                def drive(ga, gb):
                    while True:
                        a_done = b_done = False
                        try:
                            next(ga)
                        except StopIteration:
                            a_done = True
                        if gb is not None:
                            try:
                                next(gb)
                            except StopIteration:
                                b_done = True
                        else:
                            b_done = True
                        if b_done and (ga is None or True) and gb is not None:
                            break
                        if gb is None and a_done:
                            break

                prev_gb = None
                for h in range(NH):
                    ga = pe_passA(h)
                    gb = pe_passB(h - 1) if h >= 1 else None
                    if gb is None:
                        for _ in range(ng + 1):
                            next(ga)
                    else:
                        for _ in range(ng):
                            next(ga)
                            try:
                                next(gb)
                            except StopIteration:
                                pass
                        next(ga)
                        for _ in range(2):
                            try:
                                next(gb)
                            except StopIteration:
                                break
                        for _ in range(ng):
                            try:
                                next(gb)
                            except StopIteration:
                                break
                ga = None
                gb = pe_passB(NH - 1)
                for _ in range(ng + 1):
                    try:
                        next(gb)
                    except StopIteration:
                        break

            @blk.vector
            def _(v):
                wait_finals(v, prev)
                for p in range(2):
                    iv = v.memset(s_V[p].ap().rearrange("p (t c) -> p t c", c=65)[:, :, 64:65], 1.0)
                S.inc(iv, "ms", 1)
                au = 0
                qgi = 0

                def dve_passA(h):
                    nonlocal au
                    npa = sum(i // 4 + 1 for i in range(nt))
                    done = 0
                    chunk = max(1, (npa + ng - 1) // ng)
                    for i in range(nt):
                        ngr = i // 4 + 1
                        for kg in range(ngr):
                            S.wait(v, "pa", au + 1)
                            bank = [self.pb[0], self.pb[1], self.pb[7]][au % 3].ap()
                            S.inc(v.tensor_reduce(self.s_mg[:, kg:kg + 1], bank, AX.X,
                                                  ALU.max), "rm", 1)
                            au += 1
                            done += 1
                            if done % chunk == 0:
                                yield
                        S.wait(v, "rm", S.c["rm"])
                        S.inc(v.tensor_reduce(self.s_m[:, i:i + 1], self.s_mg[:, 0:ngr],
                                              AX.X, ALU.max), "m2", 1)
                        S.wait(v, "m2", S.c["m2"])
                    if h >= 1:
                        S.wait(v, "sx", 3 * (h - 1) + 2)
                    S.inc(v.tensor_scalar(self.s_mneg[:, 0:nt], self.s_m[:, 0:nt],
                                          -1e8, -1.0, ALU.max, ALU.mult), "sx", 1)
                    S.wait(v, "sx", 3 * h + 2)
                    S.wait(v, "sro", 16 * h)
                    if "d_m" in self.dbg:
                        S.wait(v, "dbg", 16 * h)
                    S.inc(v.tensor_copy(self.s_row[0:nt, :], self.pb[6].ap()[0:nt, 0:128]),
                          "sx", 1)
                    while True:
                        yield

                def dve_passB(h):
                    nonlocal qgi
                    for qg in range(ng):
                        for kt in range(4 * qg, 4 * qg + 4):
                            buu = h * nbu_h + sum(4 * g + 4 for g in range(qg)) + kt
                            S.wait(v, "pbm", buu + 1)
                            bank = self.pb[2 + buu % 2].ap()
                            S.inc(v.tensor_tensor(bank[0:128, 0:128], bank[0:128, 0:128],
                                                  self.s_maskBT[:, :], ALU.add), "mk", 1)
                        S.wait(v, "pv", h * nbu_h + sum(4 * g + 4 for g in range(qg + 1)))
                        if qgi >= 1:
                            S.wait(v, "lnx", 2 * qgi - 1)
                        S.inc(v.tensor_scalar(self.s_linv[:, :],
                                              self.pb[4].ap()[64:65, 0:512],
                                              1e-30, None, ALU.max), "nz", 1)
                        S.wait(v, "lx", qgi + 1)
                        iv = v.tensor_copy(self.s_lexp[:, :], self.pb[5].ap()[0:64, :])
                        S.inc(iv, "nz", 1)
                        S.wait(v, "nz", S.c["nz"])
                        dst = s_ao.ap()[(h % 2) * 64:(h % 2) * 64 + 64,
                                        (h // 2) * T + qg * 512:(h // 2) * T + (qg + 1) * 512]
                        S.inc(v.tensor_tensor(dst, self.pb[4].ap()[0:64, 0:512],
                                              self.s_lexp[:, :], ALU.mult), "nz", 1)
                        qgi += 1
                        yield

                for h in range(NH):
                    ga = dve_passA(h)
                    gb = dve_passB(h - 1) if h >= 1 else None
                    for _ in range(ng):
                        next(ga)
                        if gb is not None:
                            try:
                                next(gb)
                            except StopIteration:
                                pass
                    next(ga)
                    if gb is not None:
                        for _ in range(ng):
                            try:
                                next(gb)
                            except StopIteration:
                                break
                gb = dve_passB(NH - 1)
                for _ in range(ng + 1):
                    try:
                        next(gb)
                    except StopIteration:
                        break

            @blk.scalar
            def _(sc):
                wait_finals(sc, prev)
                bu = 0
                qgi2 = 0
                for h in range(NH):
                    for qg in range(ng):
                        for kt in range(4 * qg + 4):
                            qoff = max(0, (kt - 4 * qg) * 128)
                            N = 512 - qoff
                            S.wait(sc, "pbm", bu + 1)
                            bank = self.pb[2 + bu % 2].ap()
                            if kt >= 4 * qg:
                                S.wait(sc, "mk", self._mkcount(h, qg, kt, ng))
                            if bu >= 2:
                                S.wait(sc, "pv", bu - 1)
                            S.inc(sc.activation(s_pt[:, (bu % 2) * 512:(bu % 2) * 512 + N],
                                                bank[:, 0:N], AF.Exp), "ex", 1)
                            bu += 1
                        S.wait(sc, "nz", 3 * qgi2 + 1)
                        S.inc(sc.activation(self.s_lt[:, :], self.s_linv[:, :], AF.Ln),
                              "lnx", 1)
                        S.wait(sc, "lnx", 2 * qgi2 + 1)
                        S.inc(sc.activation(self.s_linvb[:, :], self.s_lt[:, :], AF.Exp,
                                            scale=-1.0), "lnx", 1)
                        S.wait(sc, "lnx", 2 * qgi2 + 2)
                        qgi2 += 1

        f = S.finals()
        f.update(RK.finals())
        self.prev = f

    def _mkcount(self, h, qg, kt, ng):
        return h * 4 * ng + qg * 4 + (kt - 4 * qg) + 1

    def proj_residual(self, tag, s_src, W_in, bias_in, Rcur, Rnext, ctx=None,
                      extra_dump=None):
        nc, nt, T = self.nc, self.nt, self.T
        S = Sems(self, tag, ["wld", "mm", "ve", "st"], sw=("wld",))
        RR = Ring(self, f"{tag}rr", 4, base_idx=16)
        RS = Ring(self, f"{tag}rs", 2, base_idx=20)
        prev = self.prev
        s_w = ctx.enter_context(nc.sbuf_tensor(f"{tag}_w", [128, 8 * D], BF16))
        s_res = ctx.enter_context(nc.sbuf_tensor(f"{tag}_res", [128, 2 * D], F32))
        s_bias = None
        if bias_in is not None:
            s_bias = ctx.enter_context(nc.sbuf_tensor(f"{tag}_bias", [1, D], BF16))
        src_ch = s_src.ap().rearrange("p (c t) -> p c t", c=8)

        with nc.Block(f"b_{tag}") as blk:
            @blk.gpsimd
            def _(g):
                wait_finals(g, prev)
                S.inc(g.dma_start(out=s_w.ap().rearrange("p (c n) -> p c n", c=8),
                                  in_=rr8(W_in)), "wld", 16)
                if bias_in is not None:
                    S.inc(g.dma_start(out=s_bias[:, :], in_=bias_in), "wld", 16)

            @blk.tensor
            def _(t):
                wait_finals(t, prev)
                S.wait(t, "wld", S.c["wld"])
                for i in range(nt):
                    for half in range(2):
                        if i >= 1:
                            S.wait(t, "ve", 2 * i - 1 + half)
                        bank = self.pb[half].ap()
                        last = None
                        for c in range(8):
                            last = t.matmul(bank, src_ch[:, c, i * 128:(i + 1) * 128],
                                            s_w[:, c * D + half * 512:c * D + half * 512 + 512],
                                            start=(c == 0), stop=(c == 7 and bias_in is None))
                        if bias_in is not None:
                            last = t.matmul(bank, self.s_ones_bf[:, 0:128],
                                            s_bias[:, half * 512:(half + 1) * 512],
                                            start=False, stop=True)
                        S.inc(last, "mm", 1)

            @blk.sync
            def _(s):
                wait_finals(s, prev)
                def emit_store(i):
                    S.wait(s, "ve", 2 * i + 2)
                    RS.inc(s.dma_start(out=Rnext[i * 128:(i + 1) * 128, :],
                                       in_=s_res[:, (i % 2) * D:(i % 2 + 1) * D]), i)

                for i in range(nt):
                    if i >= 4:
                        emit_store(i - 4)
                        S.wait(s, "ve", 2 * (i - 3))
                    RR.inc(s.dma_start(out=self.s_acc[:, (i % 4) * D:(i % 4 + 1) * D],
                                       in_=Rcur[i * 128:(i + 1) * 128, :]), i)
                for i in range(max(0, nt - 4), nt):
                    emit_store(i)
                if extra_dump is not None:
                    for i in range(nt):
                        RS.wait_done(s, i)
                        S.inc(s.dma_start(out=extra_dump[i * 128:(i + 1) * 128, :],
                                          in_=Rnext[i * 128:(i + 1) * 128, :]), "st", 16)
                for i in range(nt - 2, nt):
                    RS.wait_done(s, i)
                S.wait(s, "st", S.c["st"])

            @blk.vector
            def _(v):
                wait_finals(v, prev)
                for i in range(nt):
                    RR.wait_done(v, i)
                    if i >= 2:
                        RS.wait_done(v, i - 2)
                    for half in range(2):
                        S.wait(v, "mm", 2 * i + half + 1)
                        S.inc(v.tensor_tensor(s_res[:, (i % 2) * D + half * 512:(i % 2) * D + half * 512 + 512],
                                              self.pb[half].ap(),
                                              self.s_acc[:, (i % 4) * D + half * 512:(i % 4) * D + half * 512 + 512],
                                              ALU.add), "ve", 1)
        f = S.finals()
        f.update(RR.finals())
        f.update(RS.finals())
        self.prev = f



def host_consts(T):
    nt = T // 128
    CH = T // NLAT
    c = {}
    c["id_bf"] = np.eye(128, dtype=bf)
    c["id_f32"] = np.eye(128, dtype=np.float32)
    c["ones_bf"] = np.ones((1, 512), dtype=bf)
    c["onesT"] = np.ones((1, T), dtype=bf)
    c["ones_f32"] = np.ones((1, 64), dtype=np.float32)
    mA = np.zeros((4, 128, 512), dtype=np.float32)
    q = np.arange(128)[:, None]
    k = np.arange(128)[None, :]
    tri = np.where(k > q, NEG, 0.0).astype(np.float32)
    for v in range(4):
        for jj in range(4):
            if jj < v:
                pass
            elif jj == v:
                mA[v][:, jj * 128:(jj + 1) * 128] = tri
            else:
                mA[v][:, jj * 128:(jj + 1) * 128] = NEG
    c["maskA"] = mA.reshape(4 * 128, 512).astype(bf)
    c["maskBT"] = np.ascontiguousarray(np.where(k > q, NEG, 0.0).T).astype(bf)
    mL = np.zeros((128, 256), dtype=np.float32)
    mL[:, 0:128] = np.where(k >= q, 0.0, NEG)
    mL[:, 128:256] = np.where(k < q, 0.0, NEG)
    c["maskL"] = mL.astype(bf)
    c["maskLT1"] = np.ascontiguousarray(np.where(k >= q, 0.0, NEG).T).astype(bf)
    c["maskLT2"] = np.ascontiguousarray(np.where(k < q, 0.0, NEG).T).astype(bf)
    lpt = max(1, 128 // CH)
    l = np.arange(NLAT)[None, :]
    p = np.arange(128)[:, None]
    c["latC"] = (l - p // CH).astype(np.float32)
    c["latCT"] = np.ascontiguousarray(c["latC"].T)
    tok = np.arange(nt * 128)[:, None]
    c["ind"] = np.where(np.arange(NLAT)[None, :] == tok // CH, 1.0 / CH, 0.0).astype(bf)
    return c


def host_inputs(inputs, core, T):
    g = lambda n: np.asarray(inputs[n])
    m = {}
    m["x"] = np.ascontiguousarray(g("L0_attn_a_x")[core])
    m["wqkv_a"] = g("L0_attn_a_w_qkv")
    m["wout_a"] = g("L0_attn_a_w_out")
    m["head_scales"] = g("L0_attn_a_head_scales").reshape(NH, 1)
    hd = np.zeros((D, NH), dtype=np.float32)
    hdirs = g("L0_attn_a_head_dirs")
    for h in range(NH):
        hd[h * 64:h * 64 + 8, h] = hdirs[h]
    m["hdblk"] = hd
    m["wqkv_b"] = g("L0_attn_b_qkv_w")
    m["bqkv_b"] = g("L0_attn_b_qkv_b").reshape(1, 3 * D)
    m["wrq"] = g("L0_attn_b_rq_w")
    m["brq"] = g("L0_attn_b_rq_b").reshape(1, LATD)
    m["wrkv"] = g("L0_attn_b_rkv_w")
    m["brkv"] = g("L0_attn_b_rkv_b").reshape(1, 2 * LATD)
    m["wrout"] = g("L0_attn_b_rout_w")
    m["wroutT"] = np.ascontiguousarray(g("L0_attn_b_rout_w").T)
    m["brout"] = g("L0_attn_b_rout_b").reshape(1, D)
    m["wout_b"] = g("L0_attn_b_out_w")
    m["bout_b"] = g("L0_attn_b_out_b").reshape(1, D)
    glb = np.zeros((D, NH), dtype=np.float32)
    grb = np.zeros((D, NH), dtype=np.float32)
    for h in range(NH):
        glb[h * 64:(h + 1) * 64, h] = g("L0_attn_b_gl_w")[:, 0]
        grb[h * 64:(h + 1) * 64, h] = g("L0_attn_b_gr_w")[:, 0]
    m["glblk"] = glb
    m["grblk"] = grb
    w1 = np.concatenate([g("L0_ffn_w1"), g("L0_ffn_new_w1")], axis=0)
    w2 = np.concatenate([g("L0_ffn_w2"), g("L0_ffn_new_w2")], axis=0)
    w3 = np.concatenate([g("L0_ffn_w3"), g("L0_ffn_new_w3")], axis=0)
    m["w1T"] = np.ascontiguousarray(w1.transpose(0, 2, 1))
    m["w3T"] = np.ascontiguousarray(w3.transpose(0, 2, 1))
    m["w2T"] = np.ascontiguousarray(w2.transpose(0, 2, 1))
    gate = np.concatenate([g("L0_ffn_gate_weight"), g("L0_ffn_new_gate_weight")], axis=0)
    m["gateT"] = np.ascontiguousarray(gate.T)
    gb = np.zeros((1, NE), dtype=np.float32)
    gb[0, 8] = g("L0_ffn_gate_bias")[0]
    m["gbias"] = gb
    m.update(host_consts(T))
    return m


def build_program(T, phases=("a", "f1", "b", "f2"), debug_outs=()):
    import contextlib
    P = Prog(T, debug_outs=debug_outs)
    P.make_dram()
    P.make_persist()
    P.setup_phase()
    d = P.d

    def ffn_block(tagp, Rcur, Rnext):
        with contextlib.ExitStack() as c3:
            s_xlnT = c3.enter_context(P.nc.sbuf_tensor(f"{tagp}_xlnT", [128, 8 * T], BF16))
            P.ln_pass(f"{tagp}l", Rcur, s_xlnT, ctx=c3)
            P.proj_spill(f"{tagp}p", s_xlnT, do_route=True, xfm_spill=d["xfm"], ctx=c3)
        with contextlib.ExitStack() as c4:
            P.ffn_experts(f"{tagp}e", Rcur, Rnext, ctx=c4)

    def attnb_block(Rcur, Rnext):
        with contextlib.ExitStack() as cb:
            s_latout = cb.enter_context(P.nc.sbuf_tensor("b_latout", [128, 2 * T], BF16))
            with contextlib.ExitStack() as cmid:
                s_rqT = cmid.enter_context(P.nc.sbuf_tensor("b_rqT", [128, 2 * T], BF16))
                s_pool = cmid.enter_context(P.nc.sbuf_tensor("b_pool", [64, D], BF16))
                with contextlib.ExitStack() as c5:
                    s_xlnT = c5.enter_context(P.nc.sbuf_tensor("b_xlnT", [128, 8 * T], BF16))
                    P.ln_pass("b1l", Rcur, s_xlnT, pooled_out=s_pool, ctx=c5)
                    P.proj_spill("b1p", s_xlnT, W_in=d["wqkv_b"], COLS=3 * D,
                                 bias_in=d["bqkv_b"], fcmodes=("q", "q", "k", "k", "v", "v"),
                                 Qdst=d["Qh"], Kdst=d["Kh"], Vdst=d["Vh"], ctx=c5)
                    P.proj_spill("b1r", s_xlnT, rq=(d["wrq"], d["brq"], s_rqT), ctx=c5)
                with contextlib.ExitStack() as c6:
                    P.latent_attn("bl", s_rqT, s_pool, s_latout, ctx=c6)
            with contextlib.ExitStack() as cloc:
                s_local = cloc.enter_context(P.nc.sbuf_tensor("b_local", [128, 8 * T], BF16))
                s_g1 = cloc.enter_context(P.nc.sbuf_tensor("b_g1", [NH, T], F32))
                with contextlib.ExitStack() as c7:
                    P.local_attn("bo", s_local, s_g1, ctx=c7)
                if P.dbg:
                    dump = []
                    if "d_latout" in P.dbg:
                        dump.append((P.dbg["d_latout"], s_latout, 2))
                    if "d_local" in P.dbg:
                        dump.append((P.dbg["d_local"], s_local, 8))
                    if dump or "d_g1" in P.dbg or "d_pool" in P.dbg:
                        Sd = Sems(P, "bdbg", ["x"])
                        pv = P.prev
                        with P.nc.Block("b_bdbg") as blk:
                            @blk.sync
                            def _(s):
                                wait_finals(s, pv)
                                for dst, srct, ncch in dump:
                                    for c in range(ncch):
                                        Sd.inc(s.dma_start(out=dst[c * 128:(c + 1) * 128, :],
                                                           in_=srct.ap()[:, c * T:(c + 1) * T]),
                                               "x", 16)
                                if "d_g1" in P.dbg:
                                    Sd.inc(s.dma_start(out=P.dbg["d_g1"], in_=s_g1.ap()),
                                           "x", 16)
                                Sd.wait(s, "x", Sd.c["x"])
                        pvf = P.prev.copy()
                        pvf.update(Sd.finals())
                        P.prev = pvf
                with contextlib.ExitStack() as c8:
                    P.mix_proj("b3", s_latout, s_local, s_g1, Rcur, Rnext, ctx=c8)

    with contextlib.ExitStack() as ctx:
        if "a" in phases or "a1" in phases:
            with contextlib.ExitStack() as c1:
                s_xlnT = c1.enter_context(P.nc.sbuf_tensor("a_xlnT", [128, 8 * T], BF16))
                P.ln_pass("a1l", d["x"], s_xlnT, ctx=c1)
                P.proj_spill("a1p", s_xlnT, W_in=d["wqkv_a"], COLS=3 * D, bias_in=None,
                             fcmodes=("q", "q", "k", "k", "v", "v"),
                             Qdst=d["Qh"], Kdst=d["Kh"], Vdst=d["Vh"], do_pp=True,
                             ctx=c1, dbg_xlnT=("d_xlnT" in P.dbg))
        if "a3x" in phases:
            with contextlib.ExitStack() as c2b:
                s_ao = c2b.enter_context(P.nc.sbuf_tensor("a_ao", [128, 8 * T], BF16))
                Si = Sems(P, "ini", ["x"])
                pv = P.prev
                with P.nc.Block("b_ini") as blk:
                    @blk.vector
                    def _(v):
                        wait_finals(v, pv)
                        Si.inc(v.memset(s_ao.ap(), 0.25), "x", 1)
                pp = P.prev.copy()
                pp.update(Si.finals())
                P.prev = pp
                P.proj_residual("a3", s_ao, d["wout_a"], None, d["x"], d["Ra"], ctx=c2b)
        if "a" in phases or "a2" in phases:
            with contextlib.ExitStack() as c2:
                s_ao = c2.enter_context(P.nc.sbuf_tensor("a_ao", [128, 8 * T], BF16))
                P.attn_a_heads(s_ao, ctx=c2)
                if "a" in phases or "a3" in phases:
                    P.proj_residual("a3", s_ao, d["wout_a"], None, d["x"], d["Ra"], ctx=c2,
                                    extra_dump=P.dbg.get("d_Ra"))
        if "f1r" in phases:
            with contextlib.ExitStack() as c3:
                s_xlnT = c3.enter_context(P.nc.sbuf_tensor("f1_xlnT", [128, 8 * T], BF16))
                P.ln_pass("f1l", d["Ra"], s_xlnT, ctx=c3)
                P.proj_spill("f1p", s_xlnT, do_route=True, xfm_spill=d["xfm"], ctx=c3)
        if "f2o" in phases:
            with contextlib.ExitStack() as c4:
                P.ffn_experts("f2x", d["Ra"], d["Rb"], ctx=c4)
        if "f1" in phases:
            ffn_block("f1", d["Ra"], d["Rb"])
            if "d_cw" in P.dbg:
                S = Sems(P, "fdbg", ["x"])
                pv = P.prev
                with P.nc.Block("b_fdbg") as blk:
                    @blk.sync
                    def _(s):
                        wait_finals(s, pv)
                        S.inc(s.dma_start(out=P.dbg["d_cw"], in_=P.s_cw[:, 0:P.nt * NE]), "x", 16)
                        S.wait(s, "x", 16)
                pvf = P.prev.copy()
                pvf.update(S.finals())
                P.prev = pvf
        if "b" in phases:
            Rcur = d["Rb"] if "f1" in phases else d["Ra"]
            attnb_block(Rcur, d["Ra"] if "f1" in phases else d["Rb"])
        if "f2" in phases:
            ffn_block("f2", d["Ra"], d["out"])
    P.top.close()
    return P


def kernel(**inputs):
    T = 4096
    P = build_program(T, phases=("a", "f1", "b", "f2"))
    in_maps = [host_inputs(inputs, c, T) for c in range(8)]
    res = run_bass_kernel_spmd(P.nc, in_maps, core_ids=list(range(8)))
    return np.stack([np.asarray(r["out"]) for r in res.results], axis=0)


def _ffn_experts(self, tag, Rcur, Rnext, ctx=None):
    nc, nt, T, ng = self.nc, self.nt, self.T, self.ng
    d = self.d
    S = Sems(self, tag, ["wl", "wl2", "hm", "sl", "ml", "ym", "yv", "e1"],
             sw=("wl", "wl2"))
    RX = Ring(self, f"{tag}rx", 2, base_idx=16)
    RA = Ring(self, f"{tag}ra", 4, base_idx=18)
    RST = Ring(self, f"{tag}rs", 4, base_idx=22)
    prev = self.prev
    s_w1 = ctx.enter_context(nc.sbuf_tensor(f"{tag}_w1", [128, 8 * HID], BF16))
    s_w3 = ctx.enter_context(nc.sbuf_tensor(f"{tag}_w3", [128, 8 * HID], BF16))
    s_w2 = ctx.enter_context(nc.sbuf_tensor(f"{tag}_w2", [128, 22 * D], BF16))
    s_x = ctx.enter_context(nc.sbuf_tensor(f"{tag}_x", [128, 2 * 8 * 512], BF16))
    s_h = ctx.enter_context(nc.sbuf_tensor(f"{tag}_h", [128, 11 * 512], BF16))
    s_sl = ctx.enter_context(nc.sbuf_tensor(f"{tag}_sl", [128, 2 * 512], F32))

    def hm_end(e):
        return (e + 1) * ng * 44

    def ym_end(e):
        return (e + 1) * ng * 8

    with nc.Block(f"b_{tag}") as blk:
        @blk.gpsimd
        def _(g):
            wait_finals(g, prev)
            g.wait_ge(*self.wcast)
            for e in range(NE):
                if e >= 1:
                    S.wait(g, "hm", hm_end(e - 1))
                S.inc(g.dma_start(out=s_w1.ap().rearrange("p (c n) -> p c n", c=8),
                                  in_=rr8(d["w1b"][e])), "wl", 16)
                S.inc(g.dma_start(out=s_w3.ap().rearrange("p (c n) -> p c n", c=8),
                                  in_=rr8(d["w3b"][e])), "wl", 16)
                if e >= 1:
                    S.wait(g, "ym", ym_end(e - 1))
                S.inc(g.dma_start(out=s_w2.ap().rearrange("p (c n) -> p c n", c=22),
                                  in_=rr8(d["w2b"][e], 22)), "wl2", 16)

        @blk.sync
        def _(s):
            wait_finals(s, prev)
            def stv(e, i):
                return e * 16 * ng + (i // 4) * 16 + 8 + 2 * ((i % 4) + 1)

            def emit_store(s, e, i):
                it = e * nt + i
                S.wait(s, "yv", stv(e, i))
                RST.inc(s.dma_start(out=Rnext[i * 128:(i + 1) * 128, :],
                                    in_=self.s_acc[:, (i % 4) * D:(i % 4 + 1) * D]), it)

            for e in range(NE):
                rsrc = Rcur if e == 0 else Rnext
                for gi in range(ng):
                    it = e * ng + gi
                    if it >= 2:
                        S.wait(s, "hm", 44 * (it - 1))
                    RX.inc(s.dma_start(
                        out=s_x.ap().rearrange("p (b c n) -> p b c n", b=2, c=8)[:, it % 2],
                        in_=d["xfm"][:, gi * 512:(gi + 1) * 512].rearrange("(c p) n -> p c n", p=128)), it)
                    for ii in range(4):
                        i = 4 * gi + ii
                        if i >= 4:
                            emit_store(s, e, i - 4)
                            RST.wait_done(s, e * nt + i - 4)
                        elif e > 0:
                            emit_store(s, e - 1, nt - 4 + ii)
                            RST.wait_done(s, (e - 1) * nt + nt - 4 + ii)
                        if e > 0:
                            RST.wait_done(s, (e - 1) * nt + i)
                        RA.inc(s.dma_start(out=self.s_acc[:, (i % 4) * D:(i % 4 + 1) * D],
                                           in_=rsrc[i * 128:(i + 1) * 128, :]), e * nt + i)
            for i in range(nt - 4, nt):
                emit_store(s, NE - 1, i)
            for it in range(NE * nt - 4, NE * nt):
                RST.wait_done(s, it)

        @blk.tensor
        def _(t):
            wait_finals(t, prev)
            jj_u = 0
            y_u = 0
            for e in range(NE):
                S.wait(t, "wl", 32 * (e + 1))
                w2_waited = False
                for gi in range(ng):
                    it = e * ng + gi
                    RX.wait_done(t, it)
                    xs = s_x.ap().rearrange("p (b c n) -> p b c n", b=2, c=8)[:, it % 2]
                    for jh in range(2):
                        for jj in range(11):
                            j = jh * 11 + jj
                            bA = self.pb[2 * (jj_u % 2)].ap()
                            bB = self.pb[2 * (jj_u % 2) + 1].ap()
                            if jj_u >= 2:
                                S.wait(t, "ml", jj_u - 1)
                            last = None
                            for c in range(8):
                                last = t.matmul(bA, s_w1[:, c * HID + j * 128:c * HID + (j + 1) * 128],
                                                xs[:, c, :], start=(c == 0), stop=(c == 7))
                            S.inc(last, "hm", 1)
                            for c in range(8):
                                last = t.matmul(bB, s_w3[:, c * HID + j * 128:c * HID + (j + 1) * 128],
                                                xs[:, c, :], start=(c == 0), stop=(c == 7))
                            S.inc(last, "hm", 1)
                            jj_u += 1
                        if not w2_waited:
                            S.wait(t, "wl2", 16 * (e + 1))
                            w2_waited = True
                        S.wait(t, "ml", jj_u)
                        for ii in range(4):
                            bY0 = self.pb[4 + 2 * (ii % 2)].ap()
                            bY1 = self.pb[5 + 2 * (ii % 2)].ap()
                            if y_u >= 2:
                                S.wait(t, "yv", 2 * (y_u - 1))
                            for jj in range(11):
                                j = jh * 11 + jj
                                lh = s_h[:, jj * 512 + ii * 128:jj * 512 + (ii + 1) * 128]
                                t.matmul(bY0, lh, s_w2[:, j * D:j * D + 512],
                                         start=(jj == 0), stop=(jj == 10))
                                last = t.matmul(bY1, lh, s_w2[:, j * D + 512:(j + 1) * D],
                                                start=(jj == 0), stop=(jj == 10))
                            S.inc(last, "ym", 1)
                            y_u += 1

        @blk.scalar
        def _(sc):
            wait_finals(sc, prev)
            jj_u = 0
            for e in range(NE):
                for gi in range(ng):
                    for jh in range(2):
                        for jj in range(11):
                            S.wait(sc, "hm", 2 * jj_u + 1)
                            if jj_u >= 2:
                                S.wait(sc, "ml", jj_u - 1)
                            bA = self.pb[2 * (jj_u % 2)].ap()
                            S.inc(sc.activation(s_sl[:, (jj_u % 2) * 512:(jj_u % 2 + 1) * 512],
                                                bA, AF.Silu), "sl", 1)
                            jj_u += 1

        @blk.vector
        def _(v):
            wait_finals(v, prev)
            jj_u = 0
            y_u = 0
            for e in range(NE):
                for gi in range(ng):
                    for jh in range(2):
                        for jj in range(11):
                            if jj == 0:
                                S.wait(v, "ym", y_u)
                            S.wait(v, "sl", jj_u + 1)
                            S.wait(v, "hm", 2 * jj_u + 2)
                            bB = self.pb[2 * (jj_u % 2) + 1].ap()
                            S.inc(v.tensor_tensor(s_h[:, jj * 512:(jj + 1) * 512],
                                                  s_sl[:, (jj_u % 2) * 512:(jj_u % 2 + 1) * 512],
                                                  bB, ALU.mult), "ml", 1)
                            jj_u += 1
                        for ii in range(4):
                            i = gi * 4 + ii
                            S.wait(v, "ym", y_u + 1)
                            RA.wait_done(v, e * nt + i)
                            bY0 = self.pb[4 + 2 * (ii % 2)].ap()
                            bY1 = self.pb[5 + 2 * (ii % 2)].ap()
                            cwc = self.s_cw[:, i * NE + e:i * NE + e + 1]
                            accl = self.s_acc[:, (i % 4) * D:(i % 4) * D + 512]
                            acch = self.s_acc[:, (i % 4) * D + 512:(i % 4 + 1) * D]
                            iv = v.scalar_tensor_tensor(accl, bY0, cwc, accl, ALU.mult, ALU.add)
                            S.inc(iv, "yv", 1)
                            S.wait(v, "yv", S.c["yv"])
                            iv = v.scalar_tensor_tensor(acch, bY1, cwc, acch, ALU.mult, ALU.add)
                            S.inc(iv, "yv", 1)
                            S.wait(v, "yv", S.c["yv"])
                            y_u += 1
    f = S.finals()
    for r in (RX, RA, RST):
        f.update(r.finals())
    self.prev = f


Prog.ffn_experts = _ffn_experts


def _latent_attn(self, tag, s_rqT, s_pool, s_latout, ctx=None):
    nc, nt, T = self.nc, self.nt, self.T
    d = self.d
    LPT = 128 // self.CH if self.CH <= 128 else 0
    S = Sems(self, tag, ["wl", "st", "se", "rk", "pa", "rm", "sx", "sro", "sri",
                         "pbm", "mk", "ex", "pv", "nz", "lx", "e1", "ms", "qa"],
             sw=("wl", "sri"))
    prev = self.prev
    s_rkvw = ctx.enter_context(nc.sbuf_tensor(f"{tag}_rkvw", [128, 8 * 512], BF16))
    s_rkvb = ctx.enter_context(nc.sbuf_tensor(f"{tag}_rkvb", [1, 512], BF16))
    s_poolT = ctx.enter_context(nc.sbuf_tensor(f"{tag}_poolT", [128, 512], BF16))
    s_rkv = ctx.enter_context(nc.sbuf_tensor(f"{tag}_rkv", [64, 512], BF16))
    s_rkA = ctx.enter_context(nc.sbuf_tensor(f"{tag}_rkA", [65, 4 * 64], BF16))
    s_rkP = ctx.enter_context(nc.sbuf_tensor(f"{tag}_rkP", [128, 4 * 64], BF16))
    s_rva = ctx.enter_context(nc.sbuf_tensor(f"{tag}_rva", [64, 4 * 65], BF16))
    s_rqa = [ctx.enter_context(nc.sbuf_tensor(f"{tag}_rqa{p}", [65, T], BF16)) for p in range(2)]
    s_cmpT = ctx.enter_context(nc.sbuf_tensor(f"{tag}_cmpT", [64, 128], F32))
    s_lm = ctx.enter_context(nc.sbuf_tensor(f"{tag}_lm", [128, 64], F32))
    s_cmp = ctx.enter_context(nc.sbuf_tensor(f"{tag}_cmp", [128, 64], F32))
    s_pt = ctx.enter_context(nc.sbuf_tensor(f"{tag}_pt", [64, 2 * 128], BF16))

    with nc.Block(f"b_{tag}") as blk:
        @blk.gpsimd
        def _(g):
            wait_finals(g, prev)
            S.inc(g.dma_start(out=s_rkvw.ap().rearrange("p (c n) -> p c n", c=8),
                              in_=rr8(d["wrkv"])), "wl", 16)
            S.inc(g.dma_start(out=s_rkvb[:, :], in_=d["brkv"]), "wl", 16)
            for lh in range(NLH):
                S.wait(g, "sro", 16 * (lh + 1))
                if lh >= 2:
                    S.wait(g, "pbm", (lh - 1) * nt)
                S.inc(g.dma_start(out=s_rqa[lh % 2].ap()[64:65, :],
                                  in_=d["stats"].rearrange("a b -> (a b)").unsqueeze(0)),
                      "sri", 16)

        @blk.sync
        def _(s):
            wait_finals(s, prev)
            for lh in range(NLH):
                S.wait(s, "sx", 3 * lh + 3)
                S.wait(s, "sri", 16 * lh)
                S.inc(s.dma_start(out=d["stats"], in_=self.s_row[0:nt, :]), "sro", 16)

        @blk.tensor
        def _(t):
            wait_finals(t, prev)
            S.wait(t, "ms", 1)
            b6 = self.pb[6].ap().bitcast(BF16)
            for c in range(8):
                iv = t.transpose(b6[:, c * 64:(c + 1) * 64],
                                 s_pool.ap()[0:64, c * 128:(c + 1) * 128],
                                 self.s_id_bf[0:64, 0:64])
            S.inc(iv, "st", 1)
            S.wait(t, "wl", 32)
            S.wait(t, "se", 1)
            last = None
            for c in range(8):
                last = t.matmul(self.pb[0].ap()[0:64, :], s_poolT[:, c * 64:(c + 1) * 64],
                                s_rkvw[:, c * 512:(c + 1) * 512], start=(c == 0), stop=False)
            last = t.matmul(self.pb[0].ap()[0:64, :], self.s_ones_bf[:, 0:64],
                            s_rkvb[:, :], start=False, stop=True)
            S.inc(last, "rk", 1)
            S.wait(t, "se", 2)
            for lh in range(NLH):
                iv = t.transpose(b6[0:64, 512 + lh * 64:512 + (lh + 1) * 64],
                                 s_rkv.ap()[0:64, lh * 64:(lh + 1) * 64],
                                 self.s_id_bf[0:64, 0:64])
            S.inc(iv, "st", 1)
            S.wait(t, "se", 5)
            S.wait(t, "ms", 1)
            au = 0
            for i in range(nt):
                for lh in range(NLH):
                    if au >= 2:
                        S.wait(t, "rm", au - 1)
                    bank = self.pb[au % 2].ap()[:, 0:64]
                    S.inc(t.matmul(bank,
                                   s_rqT.ap()[(lh % 2) * 64:(lh % 2) * 64 + 64,
                                              (lh // 2) * T + i * 128:(lh // 2) * T + (i + 1) * 128],
                                   s_rkP.ap()[(lh % 2) * 64:(lh % 2) * 64 + 64,
                                              lh * 64:(lh + 1) * 64],
                                   start=True, stop=True), "pa", 1)
                    au += 1
            bu = 0
            for lh in range(NLH):
                S.wait(t, "sx", 3 * lh + 1)
                S.inc(t.transpose(self.pb[6].ap()[0:nt, 0:128], self.s_mneg[:, 0:nt],
                                  self.s_id_f32[:, :]), "sx", 1)
                S.wait(t, "sri", 16 * (lh + 1))
                S.wait(t, "qa", lh + 1)
                for i in range(nt):
                    if bu >= 2:
                        S.wait(t, "ex", bu - 1)
                    bank = self.pb[2 + bu % 2].ap()
                    S.inc(t.matmul(bank[0:64, 0:128],
                                   s_rkA.ap()[0:65, lh * 64:(lh + 1) * 64],
                                   s_rqa[lh % 2].ap()[0:65, i * 128:(i + 1) * 128],
                                   start=True, stop=True), "pbm", 1)
                    S.wait(t, "ex", bu + 1)
                    if bu >= 1:
                        S.wait(t, "nz", 3 * bu)
                    S.inc(t.matmul(self.pb[4].ap()[0:65, 0:128],
                                   s_rva.ap()[0:64, lh * 65:lh * 65 + 65],
                                   s_pt[0:64, (bu % 2) * 128:(bu % 2 + 1) * 128],
                                   start=True, stop=True), "pv", 1)
                    S.wait(t, "nz", 3 * bu + 1)
                    S.inc(t.matmul(self.pb[5].ap()[0:64, 0:128], self.s_ones_f32[0:1, 0:64],
                                   self.s_linv[0:1, 0:128], start=True, stop=True), "lx", 1)
                    bu += 1

        @blk.vector
        def _(v):
            wait_finals(v, prev)
            iv = v.memset(s_rkA.ap()[64:65, :], 1.0)
            iv = v.memset(s_rva.ap().rearrange("p (l c) -> p l c", c=65)[:, :, 64:65], 1.0)
            S.inc(iv, "ms", 1)
            S.wait(v, "st", 1)
            S.inc(v.tensor_copy(s_poolT[:, :], self.pb[6].ap().bitcast(BF16)[:, 0:512]),
                  "se", 1)
            S.wait(v, "rk", 1)
            S.inc(v.tensor_copy(s_rkv[:, :], self.pb[0].ap()[0:64, :]), "se", 1)
            S.wait(v, "st", 2)
            S.inc(v.tensor_copy(s_rkA.ap()[0:64, :],
                                self.pb[6].ap().bitcast(BF16)[0:64, 512:768]), "se", 1)
            for lh in range(NLH):
                iv = v.tensor_copy(s_rkP.ap()[(lh % 2) * 64:(lh % 2) * 64 + 64,
                                              lh * 64:(lh + 1) * 64],
                                   self.pb[6].ap().bitcast(BF16)[0:64, 512 + lh * 64:512 + (lh + 1) * 64])
            S.inc(iv, "se", 1)
            for lh in range(NLH):
                iv = v.tensor_copy(s_rva.ap()[0:64, lh * 65:lh * 65 + 64],
                                   s_rkv[0:64, 256 + lh * 64:256 + (lh + 1) * 64])
            S.inc(iv, "se", 1)
            au = 0
            for i in range(nt):
                S.wait(v, "rm", max(0, au - 1))
                S.edge(v, "e1", v.tensor_single_scalar(s_cmp[:, :], self.s_latC[:, :],
                                                       i * LPT - 0.5, ALU.is_ge))
                for lh in range(NLH):
                    S.wait(v, "pa", au + 1)
                    bank = self.pb[au % 2].ap()[:, 0:64]
                    S.edge(v, "e1", v.scalar_tensor_tensor(s_lm[:, :], s_cmp[:, :], NEG,
                                                           bank, ALU.mult, ALU.add))
                    iv = v.tensor_reduce(self.s_stat[:, lh * nt + i:lh * nt + i + 1],
                                         s_lm[:, :], AX.X, ALU.max)
                    S.inc(iv, "rm", 1)
                    S.wait(v, "rm", S.c["rm"])
                    au += 1
            bu = 0
            for lh in range(NLH):
                if lh >= 1:
                    S.wait(v, "sx", 3 * (lh - 1) + 2)
                S.edge(v, "e1", v.tensor_scalar(self.s_mneg[:, 0:nt],
                                                self.s_stat[:, lh * nt:(lh + 1) * nt],
                                                -1e8, -1.0, ALU.max, ALU.mult))
                S.inc(v.nop(), "sx", 1)
                S.wait(v, "sx", 3 * lh + 2)
                S.wait(v, "sro", 16 * lh)
                S.inc(v.tensor_copy(self.s_row[0:nt, :], self.pb[6].ap()[0:nt, 0:128]),
                      "sx", 1)
                if lh >= 2:
                    S.wait(v, "pbm", (lh - 1) * nt)
                S.inc(v.tensor_copy(s_rqa[lh % 2].ap()[0:64, :],
                                    s_rqT.ap()[(lh % 2) * 64:(lh % 2) * 64 + 64,
                                               (lh // 2) * T:(lh // 2) * T + T]), "qa", 1)
                for i in range(nt):
                    S.wait(v, "pbm", bu + 1)
                    S.edge(v, "e1", v.tensor_single_scalar(s_cmpT[:, :], self.s_latCT[:, :],
                                                           i * LPT - 0.5, ALU.is_ge))
                    bank = self.pb[2 + bu % 2].ap()
                    S.inc(v.scalar_tensor_tensor(bank[0:64, 0:128], s_cmpT[:, :], NEG,
                                                 bank[0:64, 0:128], ALU.mult, ALU.add),
                          "mk", 1)
                    S.wait(v, "mk", S.c["mk"])
                    S.wait(v, "pv", bu + 1)
                    S.edge(v, "e1", v.tensor_scalar(self.s_linv[0:1, 0:128],
                                                    self.pb[4].ap()[64:65, 0:128],
                                                    1e-30, None, ALU.max))
                    S.inc(v.reciprocal(self.s_linv[0:1, 0:128], self.s_linv[0:1, 0:128]),
                          "nz", 1)
                    S.wait(v, "lx", bu + 1)
                    iv = v.tensor_copy(self.s_lexp[0:64, 0:128], self.pb[5].ap()[0:64, 0:128])
                    S.inc(iv, "nz", 1)
                    S.wait(v, "nz", S.c["nz"])
                    dst = s_latout.ap()[(lh % 2) * 64:(lh % 2) * 64 + 64,
                                        (lh // 2) * T + i * 128:(lh // 2) * T + (i + 1) * 128]
                    S.inc(v.tensor_tensor(dst, self.pb[4].ap()[0:64, 0:128],
                                          self.s_lexp[0:64, 0:128], ALU.mult), "nz", 1)
                    bu += 1
            S.wait(v, "nz", S.c["nz"])
            S.inc(v.memset(s_latout.ap().rearrange("p (c t) -> p c t", c=2)[:, :, 0:self.CH],
                           0.0), "nz", 1)

        @blk.scalar
        def _(sc):
            wait_finals(sc, prev)
            bu = 0
            for lh in range(NLH):
                for i in range(nt):
                    S.wait(sc, "pbm", bu + 1)
                    S.wait(sc, "mk", bu + 1)
                    if bu >= 2:
                        S.wait(sc, "pv", bu - 1)
                    bank = self.pb[2 + bu % 2].ap()
                    S.inc(sc.activation(s_pt[0:64, (bu % 2) * 128:(bu % 2 + 1) * 128],
                                        bank[0:64, 0:128], AF.Exp), "ex", 1)
                    bu += 1

    f = S.finals()
    self.prev = f


Prog.latent_attn = _latent_attn


def _local_attn(self, tag, s_local, s_g1, ctx=None):
    nc, nt, T = self.nc, self.nt, self.T
    d = self.d
    S = Sems(self, tag, ["pa", "rm", "sx", "sro", "sri", "pbm", "mk", "ex", "pv",
                         "nz", "lx", "ms", "e1", "g1m", "g1v", "lnx"], sw=("sri",))
    RK = Ring(self, f"{tag}kld", 2, base_idx=16)
    prev = self.prev
    s_K = [ctx.enter_context(nc.sbuf_tensor(f"{tag}_K{p}", [65, T], BF16)) for p in range(2)]
    s_Q = [ctx.enter_context(nc.sbuf_tensor(f"{tag}_Q{p}", [65, T], BF16)) for p in range(2)]
    s_V = [ctx.enter_context(nc.sbuf_tensor(f"{tag}_V{p}", [128, nt * 65], BF16)) for p in range(2)]
    s_pt = ctx.enter_context(nc.sbuf_tensor(f"{tag}_pt", [128, 2 * 128], BF16))
    s_mskl = ctx.enter_context(nc.sbuf_tensor(f"{tag}_mskl", [128, 256], F32))

    nbu_h = 2 * nt - 1
    pbm_end = [nbu_h * (h + 1) for h in range(NH)]
    pv_end = [nbu_h * (h + 1) for h in range(NH)]

    def bui(h, t, j):
        return h * nbu_h + (0 if t == 0 else 2 * t - 1) + (j if t > 0 else 0)

    with nc.Block(f"b_{tag}") as blk:
        @blk.sync
        def _(s):
            wait_finals(s, prev)
            for p in range(2):
                S.inc(s.dma_start(out=s_K[p].ap()[64:65, :], in_=d["onesT"]), "ms", 16)
            for h in range(NH):
                par = h % 2
                if h >= 2:
                    S.wait(s, "pbm", pbm_end[h - 2])
                    S.wait(s, "pv", pv_end[h - 2])
                RK.inc(s.dma_start(out=s_K[par].ap()[0:64, :],
                                   in_=d["Kh"][h * 64:(h + 1) * 64, :]), h)
                RK.inc(s.dma_start(out=s_Q[par].ap()[0:64, :],
                                   in_=d["Qh"][h * 64:(h + 1) * 64, :]), h)
                RK.inc(s.dma_start(out=s_V[par].ap().rearrange("p (t c) -> p t c", c=65)[:, :, 0:64],
                                   in_=d["Vh"][:, h * 64:(h + 1) * 64].rearrange("(t p) d -> p t d", p=128)),
                       h)
                S.wait(s, "sx", 3 * h + 3)
                S.wait(s, "sri", 16 * h)
                S.inc(s.dma_start(out=d["stats"], in_=self.s_row[0:nt, :]), "sro", 16)

        @blk.gpsimd
        def _(g):
            wait_finals(g, prev)
            for h in range(NH):
                par = h % 2
                S.wait(g, "sro", 16 * (h + 1))
                if h >= 2:
                    S.wait(g, "pbm", pbm_end[h - 2])
                S.inc(g.dma_start(out=s_Q[par].ap()[64:65, :],
                                  in_=d["stats"].rearrange("a b -> (a b)").unsqueeze(0)),
                      "sri", 16)

        @blk.tensor
        def _(t):
            wait_finals(t, prev)
            S.wait(t, "ms", 33)
            au = 0
            bu = 0

            def pe_passA(h):
                nonlocal au
                par = h % 2
                RK.wait_done(t, h)
                chunk = max(1, (nt + 7) // 8)
                for i in range(nt):
                    if au >= 3:
                        S.wait(t, "rm", au - 2)
                    bank = [self.pb[0], self.pb[1], self.pb[7]][au % 3].ap()
                    k0 = max(0, (i - 1) * 128)
                    N = 256 if i > 0 else 128
                    S.inc(t.matmul(bank[:, 0:N], s_Q[par].ap()[0:64, i * 128:(i + 1) * 128],
                                   s_K[par].ap()[0:64, k0:k0 + N],
                                   start=True, stop=True), "pa", 1)
                    au += 1
                    if (i + 1) % chunk == 0:
                        yield
                S.wait(t, "sx", 3 * h + 1)
                S.inc(t.transpose(self.pb[6].ap()[0:nt, 0:128], self.s_mneg[:, 0:nt],
                                  self.s_id_f32[:, :]), "sx", 1)
                while True:
                    yield

            def pe_passB(h):
                nonlocal bu
                par = h % 2
                S.wait(t, "sri", 16 * (h + 1))
                chunk = max(1, (nt + 7) // 8)
                for i in range(nt):
                    kts = (i,) if i == 0 else (i - 1, i)
                    for j, kt in enumerate(kts):
                        if bu >= 2:
                            S.wait(t, "ex", bu - 1)
                        bank = self.pb[2 + bu % 2].ap()
                        S.inc(t.matmul(bank[:, 0:128],
                                       s_K[par].ap()[0:65, kt * 128:(kt + 1) * 128],
                                       s_Q[par].ap()[0:65, i * 128:(i + 1) * 128],
                                       start=True, stop=True), "pbm", 1)
                        S.wait(t, "ex", bu + 1)
                        if j == 0 or i == 0:
                            S.wait(t, "nz", 3 * (h * nt + i))
                        S.inc(t.matmul(self.pb[4].ap()[0:65, 0:128],
                                       s_V[par].ap()[:, kt * 65:kt * 65 + 65],
                                       s_pt[:, (bu % 2) * 128:(bu % 2 + 1) * 128],
                                       start=(j == 0), stop=(j == len(kts) - 1)), "pv", 1)
                        bu += 1
                    S.wait(t, "lnx", 2 * (h * nt + i + 1))
                    S.inc(t.matmul(self.pb[5].ap()[0:64, 0:128], self.s_ones_bf[0:1, 0:64],
                                   self.s_linvb[0:1, 0:128], start=True, stop=True), "lx", 1)
                    if (i + 1) % chunk == 0:
                        yield

            for h in range(NH):
                ga = pe_passA(h)
                gb = pe_passB(h - 1) if h >= 1 else None
                for _ in range(8):
                    next(ga)
                    if gb is not None:
                        try:
                            next(gb)
                        except StopIteration:
                            pass
                next(ga)
                if gb is not None:
                    for _ in range(8):
                        try:
                            next(gb)
                        except StopIteration:
                            break
            gb = pe_passB(NH - 1)
            for _ in range(9):
                try:
                    next(gb)
                except StopIteration:
                    break
            S.wait(t, "nz", 3 * NH * nt + 1)
            for ts in range(T // 512):
                if ts >= 2:
                    S.wait(t, "g1v", ts - 1)
                bank = self.pb[0 + ts % 2].ap()[0:NH, :]
                for c in range(8):
                    iv = t.matmul(bank, self.s_gl[:, c * NH:(c + 1) * NH],
                                  s_local.ap()[:, c * T + ts * 512:c * T + ts * 512 + 512],
                                  start=(c == 0), stop=(c == 7))
                S.inc(iv, "g1m", 1)

        @blk.vector
        def _(v):
            wait_finals(v, prev)
            for p in range(2):
                iv = v.memset(s_V[p].ap().rearrange("p (t c) -> p t c", c=65)[:, :, 64:65], 1.0)
            S.inc(iv, "ms", 1)
            au = 0

            def dve_passA(h):
                nonlocal au
                chunk = max(1, (nt + 7) // 8)
                for i in range(nt):
                    S.wait(v, "pa", au + 1)
                    bank = [self.pb[0], self.pb[1], self.pb[7]][au % 3].ap()
                    N = 256 if i > 0 else 128
                    iv = v.tensor_reduce(self.s_m[:, i:i + 1], bank[:, 0:N], AX.X, ALU.max)
                    S.inc(iv, "rm", 1)
                    S.wait(v, "rm", S.c["rm"])
                    au += 1
                    if (i + 1) % chunk == 0:
                        yield
                if h >= 1:
                    S.wait(v, "sx", 3 * (h - 1) + 2)
                S.inc(v.tensor_scalar(self.s_mneg[:, 0:nt], self.s_m[:, 0:nt],
                                      -1e8, -1.0, ALU.max, ALU.mult), "sx", 1)
                S.wait(v, "sx", 3 * h + 2)
                S.wait(v, "sro", 16 * h)
                S.inc(v.tensor_copy(self.s_row[0:nt, :], self.pb[6].ap()[0:nt, 0:128]),
                      "sx", 1)
                while True:
                    yield

            def dve_passB(h):
                chunk = max(1, (nt + 7) // 8)
                for i in range(nt):
                    kts = (i,) if i == 0 else (i - 1, i)
                    for j, kt in enumerate(kts):
                        bu = bui(h, i, j if i > 0 else 0)
                        S.wait(v, "pbm", bu + 1)
                        bank = self.pb[2 + bu % 2].ap()
                        mt = self.s_maskLT1 if (i > 0 and j == 0) else self.s_maskLT2
                        S.inc(v.tensor_tensor(bank[0:128, 0:128], bank[0:128, 0:128],
                                              mt[:, :], ALU.add), "mk", 1)
                        S.wait(v, "mk", S.c["mk"])
                    qn = h * nt + i
                    S.wait(v, "pv", bui(h, i, 1 if i > 0 else 0) + 1)
                    if qn >= 1:
                        S.wait(v, "lnx", 2 * qn - 1)
                    S.inc(v.tensor_scalar(self.s_linv[0:1, 0:128],
                                          self.pb[4].ap()[64:65, 0:128],
                                          1e-30, None, ALU.max), "nz", 1)
                    S.wait(v, "lx", qn + 1)
                    iv = v.tensor_copy(self.s_lexp[0:64, 0:128], self.pb[5].ap()[0:64, 0:128])
                    S.inc(iv, "nz", 1)
                    S.wait(v, "nz", S.c["nz"])
                    dst = s_local.ap()[(h % 2) * 64:(h % 2) * 64 + 64,
                                       (h // 2) * T + i * 128:(h // 2) * T + (i + 1) * 128]
                    S.inc(v.tensor_tensor(dst, self.pb[4].ap()[0:64, 0:128],
                                          self.s_lexp[0:64, 0:128], ALU.mult), "nz", 1)
                    if (i + 1) % chunk == 0:
                        yield

            for h in range(NH):
                ga = dve_passA(h)
                gb = dve_passB(h - 1) if h >= 1 else None
                for _ in range(8):
                    next(ga)
                    if gb is not None:
                        try:
                            next(gb)
                        except StopIteration:
                            pass
                next(ga)
                if gb is not None:
                    for _ in range(8):
                        try:
                            next(gb)
                        except StopIteration:
                            break
            gb = dve_passB(NH - 1)
            for _ in range(9):
                try:
                    next(gb)
                except StopIteration:
                    break
            S.wait(v, "nz", S.c["nz"])
            S.inc(v.memset(s_local.ap().rearrange("p (c t) -> p c t", c=8)[:, :, 0:1], 0.0),
                  "nz", 1)
            for ts in range(T // 512):
                S.wait(v, "g1m", ts + 1)
                bank = self.pb[0 + ts % 2].ap()[0:NH, :]
                S.inc(v.tensor_copy(s_g1.ap()[0:NH, ts * 512:(ts + 1) * 512], bank),
                      "g1v", 1)

        @blk.scalar
        def _(sc):
            wait_finals(sc, prev)
            bu = 0
            for h in range(NH):
                for i in range(nt):
                    kts = (i,) if i == 0 else (i - 1, i)
                    for j, kt in enumerate(kts):
                        S.wait(sc, "pbm", bu + 1)
                        S.wait(sc, "mk", bu + 1)
                        if bu >= 2:
                            S.wait(sc, "pv", bu - 1)
                        bank = self.pb[2 + bu % 2].ap()
                        S.inc(sc.activation(s_pt[:, (bu % 2) * 128:(bu % 2 + 1) * 128],
                                            bank[:, 0:128], AF.Exp), "ex", 1)
                        bu += 1
                    qn2 = h * nt + i
                    S.wait(sc, "nz", 3 * qn2 + 1)
                    S.inc(sc.activation(self.s_lt[0:1, 0:128], self.s_linv[0:1, 0:128],
                                        AF.Ln), "lnx", 1)
                    S.wait(sc, "lnx", 2 * qn2 + 1)
                    S.inc(sc.activation(self.s_linvb[0:1, 0:128], self.s_lt[0:1, 0:128],
                                        AF.Exp, scale=-1.0), "lnx", 1)
                    S.wait(sc, "lnx", 2 * qn2 + 2)
    f = S.finals()
    f.update(RK.finals())
    self.prev = f


Prog.local_attn = _local_attn


def _mix_proj(self, tag, s_latout, s_local, s_g1, Rcur, Rnext, ctx=None):
    nc, nt, T = self.nc, self.nt, self.T
    d = self.d
    S = Sems(self, tag, ["wl", "gq", "mm", "g2", "gt", "ev", "mx", "tr", "mtv",
                         "pm", "ve", "e1", "sg"], sw=("wl",))
    RR = Ring(self, f"{tag}rr", 4, base_idx=16)
    RS = Ring(self, f"{tag}rs", 2, base_idx=20)
    prev = self.prev
    s_ro = ctx.enter_context(nc.sbuf_tensor(f"{tag}_ro", [128, 2 * D], BF16))
    s_rot = ctx.enter_context(nc.sbuf_tensor(f"{tag}_rot", [128, 8 * LATD], BF16))
    s_rogr = ctx.enter_context(nc.sbuf_tensor(f"{tag}_rogr", [128, 2 * NH], BF16))
    s_brocol = ctx.enter_context(nc.sbuf_tensor(f"{tag}_brocol", [128, 8], BF16))
    s_bro = ctx.enter_context(nc.sbuf_tensor(f"{tag}_bro", [1, D], BF16))
    s_bgr = ctx.enter_context(nc.sbuf_tensor(f"{tag}_bgr", [1, NH], BF16))
    s_wo = ctx.enter_context(nc.sbuf_tensor(f"{tag}_wo", [128, 8 * D], BF16))
    s_bo = ctx.enter_context(nc.sbuf_tensor(f"{tag}_bo", [1, D], BF16))
    s_rem = ctx.enter_context(nc.sbuf_tensor(f"{tag}_rem", [128, 2 * D], F32))
    s_g2 = ctx.enter_context(nc.sbuf_tensor(f"{tag}_g2", [128, 2 * NH], F32))
    s_gate = ctx.enter_context(nc.sbuf_tensor(f"{tag}_gate", [128, 2 * NH], F32))
    s_loc = ctx.enter_context(nc.sbuf_tensor(f"{tag}_loc", [128, 2 * D], BF16))
    s_d = ctx.enter_context(nc.sbuf_tensor(f"{tag}_d", [128, D], F32))
    s_gd = ctx.enter_context(nc.sbuf_tensor(f"{tag}_gd", [128, D], F32))
    s_mix = ctx.enter_context(nc.sbuf_tensor(f"{tag}_mix", [128, 2 * D], BF16))
    s_mixT = ctx.enter_context(nc.sbuf_tensor(f"{tag}_mixT", [128, 2 * D], BF16))
    s_res = ctx.enter_context(nc.sbuf_tensor(f"{tag}_res", [128, 2 * D], F32))

    lat_ch = s_latout.ap().rearrange("p (c t) -> p c t", c=2)
    loc_ch = s_local.ap().rearrange("p (c t) -> p c t", c=8)

    with nc.Block(f"b_{tag}") as blk:
        @blk.gpsimd
        def _(g):
            wait_finals(g, prev)
            S.inc(g.dma_start(out=s_ro.ap().rearrange("p (c n) -> p c n", c=2),
                              in_=d["wrout"].rearrange("(c p) n -> p c n", p=128)), "wl", 16)
            S.inc(g.dma_start(out=s_rot.ap().rearrange("p (c n) -> p c n", c=8),
                              in_=rr8(d["wroutT"])), "wl", 16)
            S.inc(g.dma_start(out=s_brocol.ap(),
                              in_=d["brout"].rearrange("o (c p) -> p (o c)", p=128),
                              allow_slow_non_contiguous=True), "wl", 16)
            S.inc(g.dma_start(out=s_bro[:, :], in_=d["brout"]), "wl", 16)
            S.inc(g.dma_start(out=s_wo.ap().rearrange("p (c n) -> p c n", c=8),
                              in_=rr8(d["wout_b"])), "wl", 16)
            S.inc(g.dma_start(out=s_bo[:, :], in_=d["bout_b"]), "wl", 16)

        @blk.sync
        def _(s):
            wait_finals(s, prev)
            def emit_store(i):
                S.wait(s, "ve", 2 * i + 2)
                RS.inc(s.dma_start(out=Rnext[i * 128:(i + 1) * 128, :],
                                   in_=s_res[:, (i % 2) * D:(i % 2 + 1) * D]), i)

            for i in range(nt):
                if i >= 4:
                    emit_store(i - 4)
                    S.wait(s, "ve", 2 * (i - 3))
                RR.inc(s.dma_start(out=self.s_acc[:, (i % 4) * D:(i % 4 + 1) * D],
                                   in_=Rcur[i * 128:(i + 1) * 128, :]), i)
            for i in range(max(0, nt - 4), nt):
                emit_store(i)
            for i in range(nt - 2, nt):
                RS.wait_done(s, i)

        @blk.tensor
        def _(t):
            wait_finals(t, prev)
            S.wait(t, "wl", S.c["wl"])
            for lg in range(2):
                last = None
                for c in range(8):
                    last = t.matmul(self.pb[6].ap()[:, lg * NH:(lg + 1) * NH],
                                    s_rot[:, c * LATD + lg * 128:c * LATD + (lg + 1) * 128],
                                    self.s_gr[:, c * NH:(c + 1) * NH],
                                    start=(c == 0), stop=(c == 7))
                S.inc(last, "gq", 1)
            last = None
            for c in range(8):
                last = t.matmul(self.pb[7].ap()[0:1, 0:NH], s_brocol[:, c:c + 1],
                                self.s_gr[:, c * NH:(c + 1) * NH],
                                start=(c == 0), stop=(c == 7))
            S.inc(last, "gq", 1)
            S.wait(t, "gq", 5)
            for i in range(nt):
                S.wait(t, "gt", max(0, 3 * i))
                for half in range(2):
                    bank = self.pb[half].ap()
                    for c in range(2):
                        t.matmul(bank, lat_ch[:, c, i * 128:(i + 1) * 128],
                                 s_ro[:, c * D + half * 512:c * D + half * 512 + 512],
                                 start=(c == 0), stop=False)
                    last = t.matmul(bank, self.s_ones_bf[:, 0:128],
                                    s_bro[:, half * 512:(half + 1) * 512],
                                    start=False, stop=True)
                    S.inc(last, "mm", 1)
                bank = self.pb[2].ap()[:, 0:NH]
                for c in range(2):
                    t.matmul(bank, lat_ch[:, c, i * 128:(i + 1) * 128],
                             s_rogr[:, c * NH:(c + 1) * NH], start=(c == 0), stop=False)
                last = t.matmul(bank, self.s_ones_bf[:, 0:128], s_bgr[:, :],
                                start=False, stop=True)
                S.inc(last, "mm", 1)
                last = t.transpose(self.pb[3].ap()[0:128, 0:NH],
                                   s_g1.ap()[0:NH, i * 128:(i + 1) * 128],
                                   self.s_id_f32[0:NH, 0:NH])
                S.inc(last, "mm", 1)
                S.wait(t, "mx", max(0, 4 * (i - 1) + 1))
                b6 = self.pb[6].ap().bitcast(BF16)
                for c in range(8):
                    last = t.transpose(b6[:, c * 128:(c + 1) * 128],
                                       loc_ch[:, c, i * 128:(i + 1) * 128], self.s_id_bf[:, :])
                S.inc(last, "tr", 1)
                S.wait(t, "mx", 4 * i + 4)
                S.wait(t, "mtv", max(0, 2 * i))
                b7 = self.pb[7].ap().bitcast(BF16)
                for c in range(8):
                    last = t.transpose(b7[:, c * 128:(c + 1) * 128],
                                       s_mix[:, (i % 2) * D + c * 128:(i % 2) * D + (c + 1) * 128],
                                       self.s_id_bf[:, :])
                S.inc(last, "tr", 1)
                S.wait(t, "mtv", 2 * i + 2)
                for half in range(2):
                    if i >= 1:
                        S.wait(t, "ve", 2 * (i - 1) + half + 1)
                    bank = self.pb[4 + half].ap()
                    mixT = s_mixT.ap().rearrange("p (b c t) -> p b c t", b=2, c=8)[:, i % 2]
                    for c in range(8):
                        t.matmul(bank, mixT[:, c, :],
                                 s_wo[:, c * D + half * 512:c * D + half * 512 + 512],
                                 start=(c == 0), stop=False)
                    last = t.matmul(bank, self.s_ones_bf[:, 0:128],
                                    s_bo[:, half * 512:(half + 1) * 512],
                                    start=False, stop=True)
                    S.inc(last, "pm", 1)

        @blk.vector
        def _(v):
            wait_finals(v, prev)
            S.wait(v, "gq", 2)
            v.tensor_copy(s_rogr[:, 0:NH], self.pb[6].ap()[:, 0:NH])
            S.edge(v, "e1", v.tensor_copy(s_rogr[:, NH:2 * NH], self.pb[6].ap()[:, NH:2 * NH]))
            S.wait(v, "gq", 3)
            S.inc(v.tensor_copy(s_bgr[:, :], self.pb[7].ap()[0:1, 0:NH]), "gq", 1)
            S.inc(v.nop(), "gq", 1)
            for i in range(nt):
                S.wait(v, "mm", 4 * i + 2)
                v.tensor_copy(s_rem[:, (i % 2) * D:(i % 2) * D + 512], self.pb[0].ap())
                S.edge(v, "e1", v.tensor_copy(s_rem[:, (i % 2) * D + 512:(i % 2 + 1) * D],
                                              self.pb[1].ap()))
                S.inc(v.nop(), "gt", 1)
                S.wait(v, "mm", 4 * i + 4)
                if i >= 2:
                    S.wait(v, "sg", i - 1)
                S.edge(v, "e1", v.tensor_copy(s_g2[:, (i % 2) * NH:(i % 2 + 1) * NH],
                                              self.pb[2].ap()[:, 0:NH]))
                S.inc(v.tensor_tensor(s_g2[:, (i % 2) * NH:(i % 2 + 1) * NH],
                                      s_g2[:, (i % 2) * NH:(i % 2 + 1) * NH],
                                      self.pb[3].ap()[:, 0:NH], ALU.add), "gt", 1)
                S.inc(v.nop(), "gt", 1)
                S.wait(v, "tr", 2 * i + 1)
                S.edge(v, "e1", v.tensor_copy(s_loc[:, (i % 2) * D:(i % 2 + 1) * D],
                                              self.pb[6].ap().bitcast(BF16)[:, 0:D]))
                S.inc(v.nop(), "mx", 1)
                S.wait(v, "sg", i + 1)
                S.edge(v, "e1", v.tensor_sub(s_d[:, :], s_loc[:, (i % 2) * D:(i % 2 + 1) * D],
                                             s_rem[:, (i % 2) * D:(i % 2 + 1) * D]))
                S.inc(v.nop(), "mx", 1)
                gb = s_gate.ap()[:, (i % 2) * NH:(i % 2 + 1) * NH].unsqueeze(2).broadcast_to([128, NH, HDIM])
                S.edge(v, "e1", v.tensor_tensor(s_gd.ap().rearrange("p (h e) -> p h e", h=NH),
                                                s_d.ap().rearrange("p (h e) -> p h e", h=NH),
                                                gb, ALU.mult))
                S.inc(v.nop(), "mx", 1)
                S.inc(v.tensor_add(s_mix[:, (i % 2) * D:(i % 2 + 1) * D], s_gd[:, :],
                                   s_rem[:, (i % 2) * D:(i % 2 + 1) * D]), "mx", 1)
                S.wait(v, "mx", S.c["mx"])
                S.wait(v, "tr", 2 * i + 2)
                S.inc(v.tensor_copy(
                    s_mixT.ap().rearrange("p (b c t) -> p b c t", b=2, c=8)[:, i % 2],
                    self.pb[7].ap().bitcast(BF16).rearrange("p (c t) -> p c t", c=8)), "mtv", 2)
                RR.wait_done(v, i)
                if i >= 2:
                    RS.wait_done(v, i - 2)
                for half in range(2):
                    S.wait(v, "pm", 2 * i + half + 1)
                    S.inc(v.tensor_add(s_res[:, (i % 2) * D + half * 512:(i % 2) * D + half * 512 + 512],
                                       self.pb[4 + half].ap(),
                                       self.s_acc[:, (i % 4) * D + half * 512:(i % 4) * D + half * 512 + 512]),
                          "ve", 1)

        @blk.scalar
        def _(sc):
            wait_finals(sc, prev)
            for i in range(nt):
                S.wait(sc, "gt", 3 * i + 3)
                if i >= 2:
                    S.wait(sc, "mx", 4 * (i - 1) + 3)
                S.inc(sc.activation(s_gate[:, (i % 2) * NH:(i % 2 + 1) * NH],
                                    s_g2[:, (i % 2) * NH:(i % 2 + 1) * NH], AF.Sigmoid),
                      "sg", 1)
    f = S.finals()
    f.update(RR.finals())
    f.update(RS.finals())
    self.prev = f


Prog.mix_proj = _mix_proj
